# version 12
# baseline (speedup 1.0000x reference)
import numpy as np
import ml_dtypes
from contextlib import ExitStack

import concourse.bass as bass
import concourse.mybir as mybir
from concourse.bass_utils import run_bass_kernel_spmd

F32 = mybir.dt.float32
BF16 = mybir.dt.bfloat16
U8 = mybir.dt.uint8
AF = mybir.ActivationFunctionType
ALU = mybir.AluOpType
AX = mybir.AxisListType
BF = ml_dtypes.bfloat16

D = 2048
T = 4096
NKC = 16
NH = 16
HID0 = 8192
INTER = 5504
NIC = 43
NCORES = 8
RS128 = 1.0 / float(np.sqrt(128.0))

SPECS = {
    "x":     ([T, D], F32, "in"),
    "wqkT":  ([D, 2 * D], F32, "in"),
    "bqk":   ([128, 32], F32, "in"),
    "wvT":   ([D, D], F32, "in"),
    "bv":    ([1, D], F32, "in"),
    "woT":   ([D, D], F32, "in"),
    "bo":    ([128, 16], F32, "in"),
    "w13T":  ([D, 2 * HID0], F32, "in"),
    "w2T":   ([HID0, D], F32, "in"),
    "l1qT":  ([D, D], F32, "in"),
    "l1oT":  ([D, D], F32, "in"),
    "mem2":  ([NH * 4 * 128, 128], F32, "in"),
    "mnl":   ([NH * 128, 4], F32, "in"),
    "mnrows": ([4, D], F32, "in"),
    "e4":    ([4, 512], F32, "in"),
    "e4b":   ([4, 512], BF16, "in"),
    "bwt":   ([16, 4], F32, "in"),
    "lutrow": ([1, 16], F32, "in"),
    "lvrow": ([1, 16], F32, "in"),
    "gidx":  ([D, INTER], U8, "in"),
    "uidx":  ([D, INTER], U8, "in"),
    "didx":  ([INTER, D], U8, "in"),
    "identb": ([128, 128], BF16, "in"),
    "ident32": ([128, 128], F32, "in"),
    "h128b": ([128, 128], BF16, "in"),
    "h128f": ([128, 128], F32, "in"),
    "h64b":  ([64, 64], BF16, "in"),
    "xn":    ([D, T], BF16, "scr"),
    "qk":    ([2 * D, T], BF16, "scr"),
    "vv":    ([T, D], BF16, "scr"),
    "oo":    ([D, T], BF16, "scr"),
    "s0":    ([D, T], BF16, "scr"),
    "w2x1":  ([HID0, D], BF16, "scr"),
    "w2x":   ([HID0, D], BF16, "scr"),
    "hs":    ([HID0, T], BF16, "scr"),
    "g0":    ([D, T], BF16, "scr"),
    "r0tm":  ([T, D], BF16, "scr"),
    "x1":    ([T, D], F32, "scr"),
    "xn1":   ([D, T], BF16, "scr"),
    "sq":    ([D, T], BF16, "scr"),
    "cc":    ([D, T], BF16, "scr"),
    "s1":    ([D, T], BF16, "scr"),
    "h1":    ([INTER, T], BF16, "scr"),
    "wdx":   ([INTER, D], BF16, "scr"),
    "gsh":   ([18 * 128, D], U8, "in"),
    "cgc":   ([18 * 128, D], BF16, "scr"),
    "gwall": ([8 * 18 * 128, D], BF16, "scr"),
    "g1":    ([D, T], BF16, "scr"),
    "r1tm":  ([T, D], BF16, "scr"),
    "out":   ([T, D], F32, "out"),
}

SEM_NAMES = ("SH", "SW", "SS", "PE", "VE", "AC", "PO", "GV", "GA", "GP")


class Sem:
    __slots__ = ("h", "v")

    def __init__(self, h):
        self.h = h
        self.v = 0


class Ctx:
    def __init__(self, nc, st):
        self.nc = nc
        self.st = st
        self.sems = {n: Sem(st.enter_context(nc.semaphore(n))) for n in SEM_NAMES}
        self.consts = {}
        self.dram = {}

    def snap(self):
        return {k: s.v for k, s in self.sems.items()}

    def waits(self, eng, snap):
        for k, v in snap.items():
            if v > 0:
                eng.wait_ge(self.sems[k].h, v)

    def inc(self, inst, name, n=1):
        s = self.sems[name]
        inst.then_inc(s.h, n)
        s.v += n
        return s.v

    def guard(self, eng, *insts, sem="GV"):
        for inst in insts:
            v = self.inc(inst, sem)
        eng.wait_ge(self.sems[sem].h, v)


def _sb(ctx, name, shape, dtype):
    return ctx.st.enter_context(ctx.nc.sbuf_tensor(name, shape, dtype))


def emit_init(ctx, entry):
    nc, d = ctx.nc, ctx.dram
    C = ctx.consts
    C["identb"] = _sb(ctx, "c_identb", [128, 128], BF16)
    C["ident32"] = _sb(ctx, "c_ident32", [128, 128], F32)
    C["h128b"] = _sb(ctx, "c_h128b", [128, 128], BF16)
    C["h128f"] = _sb(ctx, "c_h128f", [128, 128], F32)
    C["h64b"] = _sb(ctx, "c_h64b", [64, 64], BF16)
    C["bqk"] = _sb(ctx, "c_bqk", [128, 32], F32)
    C["bo"] = _sb(ctx, "c_bo", [128, 16], F32)
    C["bv_bf"] = _sb(ctx, "c_bv", [1, D], BF16)
    C["lut_rep"] = _sb(ctx, "c_lutrep", [128, 16], F32)
    C["lvals"] = _sb(ctx, "c_lvals", [128, 16], F32)
    C["ones1f"] = _sb(ctx, "c_ones1f", [1, 128], F32)
    C["ones1b"] = _sb(ctx, "c_ones1b", [1, 128], BF16)
    C["onescb"] = _sb(ctx, "c_onescb", [128, 1], BF16)
    C["onescf"] = _sb(ctx, "c_onescf", [128, 1], F32)
    C["wpT"] = _sb(ctx, "c_wpT", [4, 16], F32)
    C["e4"] = _sb(ctx, "c_e4", [4, 512], F32)
    C["e4b"] = _sb(ctx, "c_e4b", [4, 512], BF16)
    for nm, val in (("eps5", 1e-5), ("eps6", 1e-6), ("invD", 1.0 / D),
                    ("ninvD", -1.0 / D), ("negone", -1.0), ("nRS", -1.02 * RS128)):
        C[nm] = _sb(ctx, "c_" + nm, [128, 1], F32)
    ist = ExitStack()
    lr = ist.enter_context(nc.sbuf_tensor("i_lutrow", [1, 16], F32))
    lv = ist.enter_context(nc.sbuf_tensor("i_lvrow", [1, 16], F32))
    bwt = ist.enter_context(nc.sbuf_tensor("i_bw", [16, 4], F32))
    mnr = ist.enter_context(nc.sbuf_tensor("i_mnr", [4, D], F32))
    sm = ist.enter_context(nc.sbuf_tensor("i_sm", [16, 4], F32))
    t16 = ist.enter_context(nc.sbuf_tensor("i_t16", [16, 1], F32))
    t4 = ist.enter_context(nc.sbuf_tensor("i_t4", [4, 1], F32))
    mrow = ist.enter_context(nc.sbuf_tensor("i_mrow", [1, 4], F32))
    mrep = ist.enter_context(nc.sbuf_tensor("i_mrep", [16, 4], F32))
    wp = ist.enter_context(nc.sbuf_tensor("i_wp", [16, 4], F32))
    ps_a = ist.enter_context(nc.psum_tensor("i_psa", [128, 512], F32))
    ps_b = ist.enter_context(nc.psum_tensor("i_psb", [128, 512], F32))

    blk_guard = nc.Block("init")
    block = blk_guard.__enter__()
    shb, veb, acb, peb = ctx.sems["SH"].v, ctx.sems["VE"].v, ctx.sems["AC"].v, ctx.sems["PE"].v

    @block.sync
    def _(s):
        ctx.waits(s, entry)
        for nm, src in (("identb", "identb"), ("ident32", "ident32"), ("h128b", "h128b"),
                        ("h64b", "h64b"), ("bqk", "bqk"), ("bo", "bo"), ("e4", "e4"), ("e4b", "e4b"),
                        ("h128f", "h128f")):
            ctx.inc(s.dma_start(C[nm][:, :], d[src][:, :]), "SH", 16)
        ctx.inc(s.dma_start(lr[:, :], d["lutrow"][:, :]), "SH", 16)
        ctx.inc(s.dma_start(lv[:, :], d["lvrow"][:, :]), "SH", 16)
        ctx.inc(s.dma_start(bwt[:, :], d["bwt"][:, :]), "SH", 16)
        ctx.inc(s.dma_start(mnr[:, :], d["mnrows"][:, :]), "SH", 16)

    @block.gpsimd
    def _(g):
        ctx.waits(g, entry)
        ctx.inc(g.dma_start(C["bv_bf"][:, :], d["bv"][:, :]), "SW", 16)
    sh_all = shb + 208

    @block.vector
    def _(v):
        ctx.waits(v, entry)
        v.memset(C["ones1f"][:, :], 1.0)
        v.memset(C["ones1b"][:, :], 1.0)
        v.memset(C["onescb"][:, :], 1.0)
        v.memset(C["onescf"][:, :], 1.0)
        for nm, val in (("eps5", 1e-5), ("eps6", 1e-6), ("invD", 1.0 / D),
                        ("ninvD", -1.0 / D), ("negone", -1.0), ("nRS", -1.02 * RS128)):
            v.memset(C[nm][:, :], val)
        ctx.inc(v.nop(), "VE")
        v.wait_ge(ctx.sems["SH"].h, sh_all)
        ctx.inc(v.reduce_max(t16[:, 0:1], bwt[:, :], axis=AX.X), "VE")
        v.wait_ge(ctx.sems["VE"].h, veb + 2)
        ctx.inc(v.tensor_scalar_mul(t16[:, 0:1], t16[:, 0:1], C["negone"][0:16, 0:1]), "VE")
        v.wait_ge(ctx.sems["AC"].h, acb + 1)
        ctx.inc(v.reduce_sum(t16[:, 0:1], sm[:, :], axis=AX.X), "VE")
        v.wait_ge(ctx.sems["VE"].h, veb + 4)
        ctx.guard(v, v.reciprocal(t16[:, 0:1], t16[:, 0:1]))
        ctx.inc(v.tensor_scalar_mul(sm[:, :], sm[:, :], t16[:, 0:1]), "VE")
        ctx.inc(v.reduce_sum(t4[:, 0:1], mnr[:, :], axis=AX.X), "VE")
        v.wait_ge(ctx.sems["VE"].h, veb + 6)
        ctx.inc(v.tensor_scalar(t4[:, 0:1], t4[:, 0:1], C["eps6"][0:4, 0:1], None, ALU.is_ge), "VE")
        v.wait_ge(ctx.sems["PE"].h, peb + 1)
        v.tensor_copy(C["lut_rep"][:, :], ps_a[:, 0:16])
        ctx.inc(v.tensor_copy(C["lvals"][:, :], ps_b[:, 0:16]), "VE")
        v.wait_ge(ctx.sems["PE"].h, peb + 2)
        ctx.inc(v.tensor_copy(mrow[:, :], ps_a[0:1, 16:20]), "VE")
        v.wait_ge(ctx.sems["PE"].h, peb + 3)
        ctx.guard(v, v.tensor_copy(mrep[:, :], ps_b[0:16, 16:20]))
        ctx.inc(v.tensor_mul(wp[:, :], sm[:, :], mrep[:, :]), "VE")
        v.wait_ge(ctx.sems["PE"].h, peb + 4)
        ctx.inc(v.tensor_copy(C["wpT"][:, :], ps_a[0:4, 32:48]), "VE")

    @block.scalar
    def _(sc):
        ctx.waits(sc, entry)
        sc.wait_ge(ctx.sems["VE"].h, veb + 3)
        ctx.inc(sc.activation(sm[:, :], bwt[:, :], AF.Exp, bias=t16[0:16, 0:1]), "AC")

    @block.tensor
    def _(t):
        ctx.waits(t, entry)
        t.wait_ge(ctx.sems["VE"].h, veb + 1)
        t.wait_ge(ctx.sems["SH"].h, sh_all)
        t.matmul(ps_a[:, 0:16], C["ones1f"][:, :], lr[:, :], start=True, stop=True)
        ctx.inc(t.matmul(ps_b[:, 0:16], C["ones1f"][:, :], lv[:, :], start=True, stop=True), "PE")
        t.wait_ge(ctx.sems["VE"].h, veb + 7)
        ctx.inc(t.transpose(ps_a[0:1, 16:20], t4[:, 0:1], C["ident32"][0:4, 0:4]), "PE")
        t.wait_ge(ctx.sems["VE"].h, veb + 9)
        ctx.inc(t.matmul(ps_b[0:16, 16:20], C["ones1f"][0:1, 0:16], mrow[:, :], start=True, stop=True), "PE")
        t.wait_ge(ctx.sems["VE"].h, veb + 10)
        ctx.inc(t.transpose(ps_a[0:4, 32:48], wp[:, :], C["ident32"][0:16, 0:16]), "PE")

    blk_guard.__exit__(None, None, None)
    ist.close()
    return ctx.snap()


def emit_proj(ctx, entry, wname, aname, oname, K, M, evac, bias=None, tag=""):
    nc, d, C = ctx.nc, ctx.dram, ctx.consts
    NT = T // 512
    NQ = NT // 4
    with ExitStack() as st:
        act = st.enter_context(nc.sbuf_tensor(f"p{tag}_act", [128, K * T], BF16))
        wbuf = [st.enter_context(nc.sbuf_tensor(f"p{tag}_w{i}", [128, K * 128], BF16)) for i in range(2)]
        stg = [st.enter_context(nc.sbuf_tensor(f"p{tag}_s{i}", [128, T], BF16)) for i in range(2)]
        ps = [st.enter_context(nc.psum_tensor(f"p{tag}_ps{i}", [128, 512], F32)) for i in range(8)]
        if evac == "elu1":
            t1 = [st.enter_context(nc.sbuf_tensor(f"p{tag}_t1{i}", [128, 512], F32)) for i in range(2)]
            t2 = [st.enter_context(nc.sbuf_tensor(f"p{tag}_t2{i}", [128, 512], F32)) for i in range(2)]
        block = st.enter_context(nc.Block(f"proj{tag}"))
        shb, swb, peb, acb, veb, ssb = (ctx.sems[k].v for k in ("SH", "SW", "PE", "AC", "VE", "SS"))
        NGRP = max(1, K // 4)
        sh_act = shb + 16 * NGRP
        sw = lambda m: swb + 16 * (m + 1)
        pe = lambda m, tq: peb + m * NQ + tq + 1
        ac = lambda m, tq, j: acb + (m * NQ + tq) * 4 + j + 1
        ve = lambda m, tq, j: veb + (m * NQ + tq) * 4 + j + 1
        ss = lambda m: ssb + 16 * (m + 1)
        ev = ve if evac == "elu1" else ac
        evs = "VE" if evac == "elu1" else "AC"

        @block.sync
        def _(s):
            ctx.waits(s, entry)
            a3 = d[aname][:, :].rearrange("(kc p) t -> p kc t", p=128)
            o3 = act[:, :].rearrange("p (kc t) -> p kc t", t=T)
            for gi in range(NGRP):
                k0, k1 = gi * 4, min(K, gi * 4 + 4)
                ctx.inc(s.dma_start(o3[:, k0:k1, :], a3[:, k0:k1, :]), "SH", 16)
            for m in range(M):
                s.wait_ge(ctx.sems[evs].h, ev(m, NQ - 1, 3))
                ctx.inc(s.dma_start(d[oname][m * 128:(m + 1) * 128, :], stg[m % 2][:, :]), "SS", 16)

        @block.gpsimd
        def _(g):
            ctx.waits(g, entry)
            for m in range(M):
                if m >= 2:
                    g.wait_ge(ctx.sems["PE"].h, pe(m - 2, NQ - 1))
                win = d[wname][:, m * 128:(m + 1) * 128].rearrange("(kc p) m -> p kc m", p=128)
                wout = wbuf[m % 2][:, :].rearrange("p (kc m) -> p kc m", m=128)
                ctx.inc(g.dma_start(wout, win), "SW", 16)

        @block.tensor
        def _(t):
            ctx.waits(t, entry)
            t.wait_ge(ctx.sems["SH"].h, sh_act)
            for m in range(M):
                t.wait_ge(ctx.sems["SW"].h, sw(m))
                for tq in range(NQ):
                    idx = m * NQ + tq
                    if idx >= 2:
                        pm, pq = divmod(idx - 2, NQ)
                        t.wait_ge(ctx.sems[evs].h, ev(pm, pq, 3))
                    bs = (idx % 2) * 4
                    for kc in range(K):
                        for j in range(4):
                            inst = t.matmul(ps[bs + j][:, :], wbuf[m % 2][:, kc * 128:(kc + 1) * 128],
                                            act[:, kc * T + (tq * 4 + j) * 512: kc * T + (tq * 4 + j) * 512 + 512],
                                            start=(kc == 0), stop=(kc == K - 1))
                    ctx.inc(inst, "PE")

        @block.scalar
        def _(sc):
            ctx.waits(sc, entry)
            for m in range(M):
                for tq in range(NQ):
                    idx = m * NQ + tq
                    sc.wait_ge(ctx.sems["PE"].h, pe(m, tq))
                    if tq == 0 and m >= 2:
                        sc.wait_ge(ctx.sems["SS"].h, ss(m - 2))
                    bs = (idx % 2) * 4
                    for j in range(4):
                        tt = tq * 4 + j
                        dst = stg[m % 2][:, tt * 512: tt * 512 + 512]
                        if evac == "copy":
                            ctx.inc(sc.activation(dst, ps[bs + j][:, :], AF.Copy), "AC")
                        elif evac == "bias":
                            ctx.inc(sc.activation(dst, ps[bs + j][:, :], AF.Identity, bias=C[bias][:, m:m + 1]), "AC")
                        elif evac == "silu":
                            ctx.inc(sc.activation(dst, ps[bs + j][:, :], AF.Silu), "AC")
                        elif evac == "elu1":
                            if idx >= 1 or j >= 2:
                                pj = (idx * 4 + j - 2)
                                sc.wait_ge(ctx.sems["VE"].h, veb + pj + 1)
                            sc.activation(t1[j % 2][:, :], ps[bs + j][:, :], AF.Relu)
                            sc.activation(t2[j % 2][:, :], ps[bs + j][:, :], AF.Relu, scale=-1.0)
                            ctx.inc(sc.activation(t2[j % 2][:, :], t2[j % 2][:, :], AF.Exp, scale=-1.0), "AC")

        if evac == "elu1":
            @block.vector
            def _(v):
                ctx.waits(v, entry)
                for m in range(M):
                    for tq in range(NQ):
                        idx = m * NQ + tq
                        for j in range(4):
                            tt = tq * 4 + j
                            v.wait_ge(ctx.sems["AC"].h, ac(m, tq, j))
                            if tq == 0 and j == 0 and m >= 2:
                                v.wait_ge(ctx.sems["SS"].h, ss(m - 2))
                            dst = stg[m % 2][:, tt * 512: tt * 512 + 512]
                            ctx.inc(v.tensor_add(dst, t1[j % 2][:, :], t2[j % 2][:, :]), "VE")

    return ctx.snap()


def emit_vproj(ctx, entry, tag="v"):
    nc, d, C = ctx.nc, ctx.dram, ctx.consts
    with ExitStack() as st:
        act = st.enter_context(nc.sbuf_tensor(f"{tag}_act", [128, NKC * T], BF16))
        wres = [st.enter_context(nc.sbuf_tensor(f"{tag}_w{i}", [128, NKC * 512], BF16)) for i in range(2)]
        stg = [st.enter_context(nc.sbuf_tensor(f"{tag}_s{i}", [128, 512], BF16)) for i in range(2)]
        ps = [st.enter_context(nc.psum_tensor(f"{tag}_ps{i}", [128, 512], F32)) for i in range(2)]
        block = st.enter_context(nc.Block(f"vproj{tag}"))
        shb, swb, peb, acb, ssb = (ctx.sems[k].v for k in ("SH", "SW", "PE", "AC", "SS"))
        sh_act = shb + 16 * 4
        sw = lambda oc: swb + 16 * (oc + 1)
        pe = lambda idx: peb + idx + 1
        ac = lambda idx: acb + idx + 1
        ss = lambda idx: ssb + 16 * (idx + 1)

        @block.sync
        def _(s):
            ctx.waits(s, entry)
            a3 = d["xn"][:, :].rearrange("(kc p) t -> p kc t", p=128)
            o3 = act[:, :].rearrange("p (kc t) -> p kc t", t=T)
            for gi in range(4):
                ctx.inc(s.dma_start(o3[:, gi * 4:gi * 4 + 4, :], a3[:, gi * 4:gi * 4 + 4, :]), "SH", 16)
            for oc in range(4):
                for m in range(32):
                    idx = oc * 32 + m
                    s.wait_ge(ctx.sems["AC"].h, ac(idx))
                    ctx.inc(s.dma_start(d["vv"][m * 128:(m + 1) * 128, oc * 512:(oc + 1) * 512],
                                        stg[idx % 2][:, :]), "SS", 16)

        @block.gpsimd
        def _(g):
            ctx.waits(g, entry)
            for oc in range(4):
                if oc >= 2:
                    g.wait_ge(ctx.sems["PE"].h, pe((oc - 1) * 32 - 1))
                win = d["wvT"][:, oc * 512:(oc + 1) * 512].rearrange("(kc p) o -> p kc o", p=128)
                wout = wres[oc % 2][:, :].rearrange("p (kc o) -> p kc o", o=512)
                ctx.inc(g.dma_start(wout, win), "SW", 16)

        @block.tensor
        def _(t):
            ctx.waits(t, entry)
            t.wait_ge(ctx.sems["SH"].h, sh_act)
            for oc in range(4):
                t.wait_ge(ctx.sems["SW"].h, sw(oc))
                for m in range(32):
                    idx = oc * 32 + m
                    if idx >= 2:
                        t.wait_ge(ctx.sems["AC"].h, ac(idx - 2))
                    p = ps[idx % 2]
                    for kc in range(NKC):
                        t.matmul(p[:, :], act[:, kc * T + m * 128: kc * T + m * 128 + 128],
                                 wres[oc % 2][:, kc * 512:(kc + 1) * 512],
                                 start=(kc == 0), stop=False)
                    inst = t.matmul(p[:, :], C["ones1b"][:, :],
                                    C["bv_bf"][:, oc * 512:oc * 512 + 512], start=False, stop=True)
                    ctx.inc(inst, "PE")

        @block.scalar
        def _(sc):
            ctx.waits(sc, entry)
            for oc in range(4):
                for m in range(32):
                    idx = oc * 32 + m
                    sc.wait_ge(ctx.sems["PE"].h, pe(idx))
                    if idx >= 2:
                        sc.wait_ge(ctx.sems["SS"].h, ss(idx - 2))
                    ctx.inc(sc.activation(stg[idx % 2][:, :], ps[idx % 2][:, :], AF.Copy), "AC")

    return ctx.snap()


def emit_combine_ln(ctx, entry, base_name, r_name, xout_name, ln_out, tag=""):
    nc, d, C = ctx.nc, ctx.dram, ctx.consts
    with ExitStack() as st:
        xa = [st.enter_context(nc.sbuf_tensor(f"c{tag}_x{i}", [128, D], F32)) for i in range(2)]
        ra = [st.enter_context(nc.sbuf_tensor(f"c{tag}_r{i}", [128, D], F32)) for i in range(2)] if r_name else None
        xsq = st.enter_context(nc.sbuf_tensor(f"c{tag}_sq", [128, D], F32))
        sv = st.enter_context(nc.sbuf_tensor(f"c{tag}_sv", [128, 8], F32))
        xnb = [st.enter_context(nc.sbuf_tensor(f"c{tag}_n{i}", [128, D], BF16)) for i in range(2)]
        stg = [st.enter_context(nc.sbuf_tensor(f"c{tag}_t{i}", [128, D], BF16)) for i in range(2)]
        pst = [st.enter_context(nc.psum_tensor(f"c{tag}_p{i}", [128, D], BF16)) for i in range(2)]
        block = st.enter_context(nc.Block(f"comb{tag}"))
        shb, swb, peb, acb, veb, ssb = (ctx.sems[k].v for k in ("SH", "SW", "PE", "AC", "VE", "SS"))
        sh = lambda tt: shb + 16 * (tt + 1)
        sw = lambda tt: swb + 16 * (tt + 1)
        ac = lambda tt, j: acb + tt * 2 + j + 1
        pe = lambda tt: peb + tt + 1
        nve = 6 if ln_out else (2 if r_name else 1)
        ve = lambda tt, j: veb + tt * nve + j + 1
        ss_per = (1 if xout_name else 0) + (1 if ln_out else 0)
        ss_x = lambda tt: ssb + 16 * (tt * ss_per + 1)
        ss_xn = lambda tt: ssb + 16 * (tt * ss_per + ss_per)

        def emit_stores(s, tt):
            if xout_name:
                s.wait_ge(ctx.sems["VE"].h, ve(tt, 0))
                ctx.inc(s.dma_start(d[xout_name][tt * 128:(tt + 1) * 128, :], xa[tt % 2][:, :]), "SS", 16)
            if ln_out:
                s.wait_ge(ctx.sems["VE"].h, ve(tt, 5))
                o3 = d[ln_out][:, tt * 128:(tt + 1) * 128].rearrange("(c p) t -> p c t", p=128)
                i3 = stg[tt % 2][:, :].rearrange("p (c t) -> p c t", t=128)
                ctx.inc(s.dma_start(o3, i3), "SS", 16)

        @block.sync
        def _(s):
            ctx.waits(s, entry)
            for tt in range(32):
                if tt >= 2:
                    s.wait_ge(ctx.sems["VE"].h, ve(tt - 2, nve - 1))
                    if xout_name:
                        s.wait_ge(ctx.sems["SS"].h, ss_x(tt - 2))
                ctx.inc(s.dma_start(xa[tt % 2][:, :], d[base_name][tt * 128:(tt + 1) * 128, :]), "SH", 16)
                if tt >= 1:
                    emit_stores(s, tt - 1)
            emit_stores(s, 31)

        if r_name:
            @block.gpsimd
            def _(g):
                ctx.waits(g, entry)
                for tt in range(32):
                    if tt >= 2:
                        g.wait_ge(ctx.sems["VE"].h, ve(tt - 2, 0))
                    ctx.inc(g.dma_start(ra[tt % 2][:, :], d[r_name][tt * 128:(tt + 1) * 128, :]), "SW", 16)

        @block.vector
        def _(v):
            ctx.waits(v, entry)
            for tt in range(32):
                v.wait_ge(ctx.sems["SH"].h, sh(tt))
                if r_name:
                    v.wait_ge(ctx.sems["SW"].h, sw(tt))
                    ctx.inc(v.tensor_add(xa[tt % 2][:, :], xa[tt % 2][:, :], ra[tt % 2][:, :]), "VE")
                else:
                    ctx.inc(v.nop(), "VE")
                if not ln_out:
                    if nve == 2:
                        ctx.inc(v.nop(), "VE")
                    continue
                ctx.inc(v.reduce_sum(sv[:, 0:1], xa[tt % 2][:, :], axis=AX.X), "VE")
                v.wait_ge(ctx.sems["AC"].h, ac(tt, 0))
                ctx.inc(v.reduce_sum(sv[:, 1:2], xsq[:, :], axis=AX.X), "VE")
                v.wait_ge(ctx.sems["VE"].h, ve(tt, 2))
                i1 = v.tensor_scalar_mul(sv[:, 2:3], sv[:, 0:1], C["ninvD"][:, 0:1])
                i2 = v.tensor_scalar_mul(sv[:, 4:5], sv[:, 1:2], C["invD"][:, 0:1])
                ctx.guard(v, i1, i2)
                ctx.guard(v, v.tensor_mul(sv[:, 3:4], sv[:, 2:3], sv[:, 2:3]))
                ctx.inc(v.tensor_sub(sv[:, 4:5], sv[:, 4:5], sv[:, 3:4]), "VE")
                v.wait_ge(ctx.sems["AC"].h, ac(tt, 1))
                ctx.inc(v.tensor_scalar(xnb[tt % 2][:, :], xa[tt % 2][:, :], sv[:, 2:3], sv[:, 6:7],
                                        ALU.add, ALU.mult), "VE")
                v.wait_ge(ctx.sems["PE"].h, pe(tt))
                if tt >= 2:
                    v.wait_ge(ctx.sems["SS"].h, ss_xn(tt - 2))
                ctx.inc(v.tensor_copy(stg[tt % 2][:, :], pst[tt % 2][:, :]), "VE")

        @block.scalar
        def _(sc):
            ctx.waits(sc, entry)
            if not ln_out:
                return
            for tt in range(32):
                v0 = ve(tt, 0)
                sc.wait_ge(ctx.sems["VE"].h, v0) if r_name else sc.wait_ge(ctx.sems["SH"].h, sh(tt))
                ctx.inc(sc.activation(xsq[:, :], xa[tt % 2][:, :], AF.Square), "AC")
                sc.wait_ge(ctx.sems["VE"].h, ve(tt, 3))
                sc.activation(sv[:, 5:6], sv[:, 4:5], AF.Ln, bias=C["eps5"][:, 0:1])
                ctx.inc(sc.activation(sv[:, 6:7], sv[:, 5:6], AF.Exp, scale=-0.5), "AC")

        if ln_out:
            @block.tensor
            def _(t):
                ctx.waits(t, entry)
                for tt in range(32):
                    t.wait_ge(ctx.sems["VE"].h, ve(tt, 4))
                    if tt >= 2:
                        t.wait_ge(ctx.sems["VE"].h, ve(tt - 2, 5))
                    for c in range(16):
                        inst = t.transpose(pst[tt % 2][:, c * 128:(c + 1) * 128],
                                           xnb[tt % 2][:, c * 128:(c + 1) * 128], C["identb"][:, :])
                    ctx.inc(inst, "PE")

    return ctx.snap()


def emit_attn(ctx, entry):
    nc, d, C = ctx.nc, ctx.dram, ctx.consts
    with ExitStack() as st:
        qh = [st.enter_context(nc.sbuf_tensor(f"at_q{i}", [128, T], BF16)) for i in range(2)]
        kh = [st.enter_context(nc.sbuf_tensor(f"at_k{i}", [128, T], BF16)) for i in range(2)]
        vt = [st.enter_context(nc.sbuf_tensor(f"at_v{i}", [128, T], BF16)) for i in range(2)]
        pbuf = [st.enter_context(nc.sbuf_tensor(f"at_p{i}", [128, 32 * 512], BF16)) for i in range(2)]
        ostage = [st.enter_context(nc.sbuf_tensor(f"at_o{i}", [128, T], BF16)) for i in range(2)]
        oraw = [st.enter_context(nc.sbuf_tensor(f"at_or{i}", [128, 512], F32)) for i in range(2)]
        sqb = [st.enter_context(nc.sbuf_tensor(f"at_sq{i}", [128, 512], BF16)) for i in range(2)]
        qn = st.enter_context(nc.sbuf_tensor("at_qn", [1, T], F32))
        kn = st.enter_context(nc.sbuf_tensor("at_kn", [1, T], F32))
        cm = st.enter_context(nc.sbuf_tensor("at_cm", [1, 8], F32))
        rs = st.enter_context(nc.sbuf_tensor("at_rs", [1, 512], F32))
        rep_sb = st.enter_context(nc.sbuf_tensor("at_rep", [128, 512], F32))
        negc = st.enter_context(nc.sbuf_tensor("at_negc", [128, 2], F32))
        ps_s = [st.enter_context(nc.psum_tensor(f"at_pss{i}", [128, 1024], F32)) for i in range(2)]
        ps_o = st.enter_context(nc.psum_tensor("at_pso", [128, 512], F32))
        ps_sum = [st.enter_context(nc.psum_tensor(f"at_psum{i}", [1, 512], F32)) for i in range(2)]
        ps_rep = st.enter_context(nc.psum_tensor("at_prep", [128, 512], F32))
        block = st.enter_context(nc.Block("attn"))
        shb, peb, acb, veb, ssb = (ctx.sems[k].v for k in ("SH", "PE", "AC", "VE", "SS"))
        NAC, NVE = 146, 44
        ac_base = lambda h: acb + h * NAC
        ac_sq = lambda h, i: ac_base(h) + i + 1
        ac_ln = lambda h: ac_base(h) + 17
        ac_expc = lambda h: ac_base(h) + 18
        ac_exp = lambda h, qg, j: ac_base(h) + 18 + qg * 16 + j + 1
        ve_base = lambda h: veb + h * NVE
        ve_evc = lambda h, i: ve_base(h) + i + 1
        ve_max = lambda h: ve_base(h) + 17
        ve_sum = lambda h: ve_base(h) + 18
        ve_ncp = lambda h: ve_base(h) + 19
        ve_ncc = lambda h: ve_base(h) + 20
        ve_or = lambda h, qg: ve_base(h) + 20 + qg * 3 + 1
        ve_rcp = lambda h, qg: ve_base(h) + 20 + qg * 3 + 2
        ve_ev = lambda h, qg: ve_base(h) + 20 + qg * 3 + 3
        sh = lambda h: shb + 48 * (h + 1)
        ss = lambda h: ssb + 16 * (h + 1)
        pe_c, pe_st, pe_pv, pe_rep = {}, {}, {}, {}

        @block.tensor
        def _(t):
            ctx.waits(t, entry)
            for h in range(NH):
                for i in range(16):
                    t.wait_ge(ctx.sems["AC"].h, ac_sq(h, i))
                    if i >= 4:
                        t.wait_ge(ctx.sems["VE"].h, ve_evc(h, i - 4))
                    elif h >= 1:
                        t.wait_ge(ctx.sems["AC"].h, ac_base(h - 1) + NAC)
                    pe_c[(h, i)] = ctx.inc(t.matmul(
                        ps_s[(i // 2) % 2][0:1, (i % 2) * 512:(i % 2) * 512 + 512],
                        C["onescb"][:, :], sqb[i % 2][:, :], start=True, stop=True), "PE")
                t.wait_ge(ctx.sems["VE"].h, ve_ncp(h))
                if h >= 1:
                    t.wait_ge(ctx.sems["VE"].h, ve_base(h - 1) + NVE)
                pe_c[(h, 16)] = ctx.inc(t.matmul(ps_o[:, 0:1], C["ones1f"][:, :], cm[0:1, 6:7],
                                                 start=True, stop=True), "PE")

                def emit_st1(qg, kc):
                    idx = qg * 32 + kc
                    if qg == 0 and kc == 0:
                        t.wait_ge(ctx.sems["VE"].h, ve_base(h) + 16)
                    if idx >= 4 and idx % 2 == 0:
                        pqg, pj = divmod((idx - 4) // 2, 16)
                        t.wait_ge(ctx.sems["AC"].h, ac_exp(h, pqg, pj))
                    pe_st[(h, qg, kc)] = ctx.inc(
                        t.matmul(ps_s[(idx // 2) % 2][:, (idx % 2) * 512:(idx % 2) * 512 + 512],
                                 kh[h % 2][:, kc * 128:(kc + 1) * 128],
                                 qh[h % 2][:, qg * 512:(qg + 1) * 512],
                                 start=True, stop=True), "PE")

                def emit_rep(qg):
                    t.wait_ge(ctx.sems["VE"].h, ve_rcp(h, qg))
                    pe_rep[(h, qg)] = ctx.inc(t.matmul(ps_rep[:, :], C["ones1f"][:, :], rs[0:1, :],
                                                       start=True, stop=True), "PE")

                def emit_block(qg):
                    pv = qg - 1
                    if qg >= 2:
                        emit_rep(qg - 2)
                    t.wait_ge(ctx.sems["AC"].h, ac_exp(h, pv, 15))
                    if pv >= 2:
                        t.wait_ge(ctx.sems["VE"].h, ve_or(h, pv - 2))
                    elif h >= 1 and pv < 2:
                        t.wait_ge(ctx.sems["VE"].h, ve_or(h - 1, 6 + pv))
                    for kc in range(32):
                        t.matmul(ps_o[:, :], vt[h % 2][:, kc * 128:(kc + 1) * 128],
                                 pbuf[pv % 2][:, kc * 512:(kc + 1) * 512],
                                 start=(kc == 0), stop=(kc == 31))
                        inst = t.matmul(ps_sum[pv % 2][0:1, :], C["onescb"][:, :],
                                        pbuf[pv % 2][:, kc * 512:(kc + 1) * 512],
                                        start=(kc == 0), stop=(kc == 31))
                        if qg < 8:
                            emit_st1(qg, kc)
                    pe_pv[(h, pv)] = ctx.inc(inst, "PE")

                emit_st1(0, 0)
                for kc in range(1, 32):
                    emit_st1(0, kc)
                for qg in range(1, 9):
                    emit_block(qg)
                emit_rep(7)

        @block.sync
        def _(s):
            ctx.waits(s, entry)
            for h in range(NH):
                if h >= 2:
                    s.wait_ge(ctx.sems["PE"].h, pe_c[(h - 1, 0)] - 1)
                ctx.inc(s.dma_start(qh[h % 2][:, :], d["qk"][h * 128:(h + 1) * 128, :]), "SH", 16)
                ctx.inc(s.dma_start(kh[h % 2][:, :], d["qk"][D + h * 128:D + (h + 1) * 128, :]), "SH", 16)
                v3 = d["vv"][:, h * 128:(h + 1) * 128].rearrange("(kc p) e -> p kc e", p=128)
                o3 = vt[h % 2][:, :].rearrange("p (kc e) -> p kc e", e=128)
                ctx.inc(s.dma_start(o3, v3), "SH", 16)
                if h >= 1:
                    s.wait_ge(ctx.sems["VE"].h, ve_ev(h - 1, 7))
                    ctx.inc(s.dma_start(d["oo"][(h - 1) * 128:h * 128, :], ostage[(h - 1) % 2][:, :]), "SS", 16)
            s.wait_ge(ctx.sems["VE"].h, ve_ev(NH - 1, 7))
            ctx.inc(s.dma_start(d["oo"][(NH - 1) * 128:NH * 128, :], ostage[(NH - 1) % 2][:, :]), "SS", 16)

        @block.scalar
        def _(sc):
            ctx.waits(sc, entry)
            for h in range(NH):
                sc.wait_ge(ctx.sems["SH"].h, sh(h))
                for i in range(16):
                    src2 = qh[h % 2] if i < 8 else kh[h % 2]
                    g = i % 8
                    if i >= 2:
                        sc.wait_ge(ctx.sems["PE"].h, pe_c[(h, i - 2)])
                    elif h >= 1:
                        sc.wait_ge(ctx.sems["PE"].h, pe_c[(h - 1, 14 + i)])
                    ctx.inc(sc.activation(sqb[i % 2][:, :], src2[:, g * 512:(g + 1) * 512], AF.Square), "AC")
                sc.wait_ge(ctx.sems["VE"].h, ve_max(h))
                sc.activation(cm[0:1, 2:3], cm[0:1, 0:1], AF.Ln)
                ctx.inc(sc.activation(cm[0:1, 3:4], cm[0:1, 1:2], AF.Ln), "AC")
                sc.wait_ge(ctx.sems["VE"].h, ve_sum(h))
                ctx.inc(sc.activation(cm[0:1, 5:6], cm[0:1, 4:5], AF.Exp, scale=0.5), "AC")
                sc.wait_ge(ctx.sems["VE"].h, ve_ncc(h))
                for qg in range(8):
                    if qg >= 2:
                        sc.wait_ge(ctx.sems["PE"].h, pe_pv[(h, qg - 2)])
                    elif h >= 1:
                        sc.wait_ge(ctx.sems["PE"].h, pe_pv[(h - 1, 6 + qg)])
                    for j in range(16):
                        idx = qg * 32 + 2 * j
                        sc.wait_ge(ctx.sems["PE"].h, pe_st[(h, qg, 2 * j + 1)])
                        ctx.inc(sc.activation(pbuf[qg % 2][:, j * 1024:(j + 1) * 1024],
                                              ps_s[(idx // 2) % 2][:, :], AF.Exp,
                                              bias=negc[:, h % 2:h % 2 + 1], scale=RS128), "AC")

        @block.vector
        def _(v):
            ctx.waits(v, entry)
            for h in range(NH):
                for i in range(16):
                    dst = qn if i < 8 else kn
                    g = i % 8
                    v.wait_ge(ctx.sems["PE"].h, pe_c[(h, i)])
                    ctx.inc(v.tensor_copy(dst[0:1, g * 512:(g + 1) * 512],
                                          ps_s[(i // 2) % 2][0:1, (i % 2) * 512:(i % 2) * 512 + 512]), "VE")
                ctx.guard(v, v.reduce_max(cm[0:1, 0:1], qn[0:1, :], axis=AX.X))
                ctx.guard(v, v.reduce_max(cm[0:1, 1:2], kn[0:1, :], axis=AX.X))
                ctx.inc(v.nop(), "VE")
                v.wait_ge(ctx.sems["AC"].h, ac_ln(h))
                ctx.guard(v, v.tensor_add(cm[0:1, 4:5], cm[0:1, 2:3], cm[0:1, 3:4]))
                ctx.inc(v.nop(), "VE")
                v.wait_ge(ctx.sems["AC"].h, ac_expc(h))
                ctx.guard(v, v.tensor_scalar_mul(cm[0:1, 6:7], cm[0:1, 5:6], C["nRS"][0:1, 0:1]))
                ctx.inc(v.nop(), "VE")
                v.wait_ge(ctx.sems["PE"].h, pe_c[(h, 16)])
                ctx.guard(v, v.tensor_copy(negc[:, h % 2:h % 2 + 1], ps_o[:, 0:1]))
                ctx.inc(v.nop(), "VE")
                for qg in range(8):
                    v.wait_ge(ctx.sems["PE"].h, pe_pv[(h, qg)])
                    ctx.inc(v.tensor_copy(oraw[qg % 2][:, :], ps_o[:, :]), "VE")
                    ctx.guard(v, v.reciprocal(rs[0:1, :], ps_sum[qg % 2][0:1, :]))
                    ctx.inc(v.nop(), "VE")
                    v.wait_ge(ctx.sems["PE"].h, pe_rep[(h, qg)])
                    ctx.guard(v, v.tensor_copy(rep_sb[:, :], ps_rep[:, :]))
                    if qg == 0 and h >= 2:
                        v.wait_ge(ctx.sems["SS"].h, ss(h - 2))
                    ctx.inc(v.tensor_mul(ostage[h % 2][:, qg * 512:(qg + 1) * 512],
                                         oraw[qg % 2][:, :], rep_sb[:, :]), "VE")

    assert ctx.sems["AC"].v == acb + NH * NAC
    assert ctx.sems["VE"].v == veb + NH * NVE
    return ctx.snap()


def emit_w2x(ctx, entry):
    nc, d, C = ctx.nc, ctx.dram, ctx.consts
    with ExitStack() as st:
        chunk = [st.enter_context(nc.sbuf_tensor(f"wx_c{i}", [128, D], F32)) for i in range(2)]
        stg = [st.enter_context(nc.sbuf_tensor(f"wx_s{i}", [128, D], BF16)) for i in range(2)]
        ps = [st.enter_context(nc.psum_tensor(f"wx_p{i}", [128, 512], F32)) for i in range(2)]
        t2in = [st.enter_context(nc.sbuf_tensor(f"wx_i{i}", [64, 16 * 1024], BF16)) for i in range(2)]
        t2out = [st.enter_context(nc.sbuf_tensor(f"wx_o{i}", [64, 16 * 1024], BF16)) for i in range(2)]
        ps2 = [st.enter_context(nc.psum_tensor(f"wx_q{i}", [64, 512], F32)) for i in range(2)]
        block = st.enter_context(nc.Block("w2x"))
        shb, swb, peb, acb, veb, ssb = (ctx.sems[k].v for k in ("SH", "SW", "PE", "AC", "VE", "SS"))
        sh1 = lambda a: shb + 16 * (a + 1)
        pe1 = lambda a, j: peb + a * 4 + j + 1
        ac1 = lambda a, j: acb + a * 4 + j + 1
        ss1 = lambda a: ssb + 16 * (a + 1)
        NM2 = 32
        NIT2 = 16
        pe2 = lambda bb, j: peb + 256 + bb * NM2 + j + 1
        ac2 = lambda bb, j: acb + 256 + bb * NM2 + j + 1
        sh2 = lambda bb: shb + 16 * (64 + bb + 1)
        ss2 = lambda bb: ssb + 16 * (64 + bb + 1)
        ss1_all = ssb + 16 * 64

        @block.tensor
        def _(t):
            ctx.waits(t, entry)
            for a in range(64):
                t.wait_ge(ctx.sems["SH"].h, sh1(a))
                for j in range(4):
                    idx = a * 4 + j
                    if idx >= 2:
                        t.wait_ge(ctx.sems["AC"].h, ac1(*divmod(idx - 2, 4)))
                    ctx.inc(t.matmul(ps[idx % 2][:, :], C["h128f"][:, :],
                                     chunk[a % 2][:, j * 512:(j + 1) * 512], start=True, stop=True), "PE")
            for bb in range(NIT2):
                t.wait_ge(ctx.sems["SH"].h, sh2(bb))
                for j in range(NM2):
                    idx = bb * NM2 + j
                    if idx >= 2:
                        t.wait_ge(ctx.sems["AC"].h, ac2(*divmod(idx - 2, NM2)))
                    ctx.inc(t.matmul(ps2[idx % 2][:, :], C["h64b"][:, :],
                                     t2in[bb % 2][:, j * 512:(j + 1) * 512], start=True, stop=True), "PE")

        @block.scalar
        def _(sc):
            ctx.waits(sc, entry)
            SQ2 = float(np.sqrt(2.0))
            for a in range(64):
                for j in range(4):
                    idx = a * 4 + j
                    sc.wait_ge(ctx.sems["PE"].h, pe1(a, j))
                    if j == 0 and a >= 2:
                        sc.wait_ge(ctx.sems["SS"].h, ss1(a - 2))
                    ctx.inc(sc.activation(stg[a % 2][:, j * 512:(j + 1) * 512], ps[idx % 2][:, :],
                                          AF.Identity, scale=SQ2), "AC")
            for bb in range(NIT2):
                for j in range(NM2):
                    idx = bb * NM2 + j
                    sc.wait_ge(ctx.sems["PE"].h, pe2(bb, j))
                    if j == 0 and bb >= 2:
                        sc.wait_ge(ctx.sems["SS"].h, ss2(bb - 2))
                    ctx.inc(sc.activation(t2out[bb % 2][:, j * 512:(j + 1) * 512], ps2[idx % 2][:, :],
                                          AF.Identity), "AC")

        @block.sync
        def _(s):
            ctx.waits(s, entry)
            for a in range(64):
                if a >= 2:
                    s.wait_ge(ctx.sems["PE"].h, pe1(a - 2, 3))
                ctx.inc(s.dma_start(chunk[a % 2][:, :], d["w2T"][a * 128:(a + 1) * 128, :]), "SH", 16)
                if a >= 1:
                    s.wait_ge(ctx.sems["AC"].h, ac1(a - 1, 3))
                    ctx.inc(s.dma_start(d["w2x1"][(a - 1) * 128:a * 128, :], stg[(a - 1) % 2][:, :]), "SS", 16)
            s.wait_ge(ctx.sems["AC"].h, ac1(63, 3))
            ctx.inc(s.dma_start(d["w2x1"][63 * 128:64 * 128, :], stg[63 % 2][:, :]), "SS", 16)
            w1v = d["w2x1"][:, :].rearrange("(a b) dd -> a b dd", a=64)
            wov = d["w2x"][:, :].rearrange("(a b) dd -> a b dd", a=64)

            def reg2(bb):
                blk, dh = divmod(bb, 2)
                return slice(blk * 16, (blk + 1) * 16), slice(dh * 1024, (dh + 1) * 1024)

            def store2(bb):
                s.wait_ge(ctx.sems["AC"].h, ac2(bb, NM2 - 1))
                bs, ds = reg2(bb)
                o3 = t2out[bb % 2][:, :].rearrange("p (b dd) -> p b dd", dd=1024)
                ctx.inc(s.dma_start(wov[:, bs, ds], o3), "SS", 16)

            for bb in range(NIT2):
                if bb == 0:
                    s.wait_ge(ctx.sems["SS"].h, ss1_all)
                if bb >= 2:
                    s.wait_ge(ctx.sems["PE"].h, pe2(bb - 2, NM2 - 1))
                bs, ds = reg2(bb)
                i3 = t2in[bb % 2][:, :].rearrange("p (b dd) -> p b dd", dd=1024)
                ctx.inc(s.dma_start(i3, w1v[:, bs, ds]), "SH", 16)
                if bb >= 1:
                    store2(bb - 1)
            store2(NIT2 - 1)

    return ctx.snap()


def emit_ffn13(ctx, entry):
    nc, d, C = ctx.nc, ctx.dram, ctx.consts
    NT = T // 512
    with ExitStack() as st:
        act = st.enter_context(nc.sbuf_tensor("f13_act", [128, NKC * T], BF16))
        wg = [st.enter_context(nc.sbuf_tensor(f"f13_wg{i}", [128, NKC * 128], BF16)) for i in range(2)]
        wl = [st.enter_context(nc.sbuf_tensor(f"f13_wl{i}", [128, NKC * 128], BF16)) for i in range(2)]
        gs = [st.enter_context(nc.sbuf_tensor(f"f13_g{i}", [128, 512], F32)) for i in range(2)]
        stg = [st.enter_context(nc.sbuf_tensor(f"f13_s{i}", [128, T], BF16)) for i in range(2)]
        pg = [st.enter_context(nc.psum_tensor(f"f13_pg{i}", [128, 512], F32)) for i in range(4)]
        pl = [st.enter_context(nc.psum_tensor(f"f13_pl{i}", [128, 512], F32)) for i in range(4)]
        block = st.enter_context(nc.Block("ffn13"))
        shb, swb, peb, acb, veb, ssb = (ctx.sems[k].v for k in ("SH", "SW", "PE", "AC", "VE", "SS"))
        sh_act = shb + 64
        NP = NT // 2
        sw = lambda hc: swb + 32 * (hc + 1)
        pe = lambda pidx, k: peb + pidx * 2 + k + 1
        ac = lambda pidx, j: acb + pidx * 2 + j + 1
        ve = lambda pidx, j: veb + pidx * 2 + j + 1
        ss = lambda hc: ssb + 16 * (hc + 1)

        @block.sync
        def _(s):
            ctx.waits(s, entry)
            a3 = d["xn"][:, :].rearrange("(kc p) t -> p kc t", p=128)
            o3 = act[:, :].rearrange("p (kc t) -> p kc t", t=T)
            for gi in range(4):
                ctx.inc(s.dma_start(o3[:, gi * 4:gi * 4 + 4, :], a3[:, gi * 4:gi * 4 + 4, :]), "SH", 16)
            for hc in range(64):
                s.wait_ge(ctx.sems["VE"].h, ve(hc * NP + NP - 1, 1))
                ctx.inc(s.dma_start(d["hs"][hc * 128:(hc + 1) * 128, :], stg[hc % 2][:, :]), "SS", 16)

        @block.gpsimd
        def _(g):
            ctx.waits(g, entry)
            for hc in range(64):
                if hc >= 2:
                    g.wait_ge(ctx.sems["PE"].h, pe((hc - 2) * NP + NP - 1, 1))
                for wbuf, coff in ((wg, hc * 128), (wl, HID0 + hc * 128)):
                    win = d["w13T"][:, coff:coff + 128].rearrange("(kc p) m -> p kc m", p=128)
                    wout = wbuf[hc % 2][:, :].rearrange("p (kc m) -> p kc m", m=128)
                    ctx.inc(g.dma_start(wout, win), "SW", 16)

        @block.tensor
        def _(t):
            ctx.waits(t, entry)
            t.wait_ge(ctx.sems["SH"].h, sh_act)
            for hc in range(64):
                t.wait_ge(ctx.sems["SW"].h, sw(hc))
                for tp in range(NP):
                    pidx = hc * NP + tp
                    if pidx >= 2:
                        t.wait_ge(ctx.sems["AC"].h, ac(pidx - 2, 1))
                        t.wait_ge(ctx.sems["VE"].h, ve(pidx - 2, 1))
                    bs = (pidx % 2) * 2
                    o0 = (tp * 2) * 512
                    for kc in range(NKC):
                        co = kc * T + o0
                        inst = t.matmul(pg[bs][:, :], wg[hc % 2][:, kc * 128:(kc + 1) * 128],
                                        act[:, co: co + 512], start=(kc == 0), stop=(kc == NKC - 1))
                        inst = t.matmul(pg[bs + 1][:, :], wg[hc % 2][:, kc * 128:(kc + 1) * 128],
                                        act[:, co + 512: co + 1024], start=(kc == 0), stop=(kc == NKC - 1))
                    ctx.inc(inst, "PE")
                    for kc in range(NKC):
                        co = kc * T + o0
                        inst = t.matmul(pl[bs][:, :], wl[hc % 2][:, kc * 128:(kc + 1) * 128],
                                        act[:, co: co + 512], start=(kc == 0), stop=(kc == NKC - 1))
                        inst = t.matmul(pl[bs + 1][:, :], wl[hc % 2][:, kc * 128:(kc + 1) * 128],
                                        act[:, co + 512: co + 1024], start=(kc == 0), stop=(kc == NKC - 1))
                    ctx.inc(inst, "PE")

        @block.scalar
        def _(sc):
            ctx.waits(sc, entry)
            for hc in range(64):
                for tp in range(NP):
                    pidx = hc * NP + tp
                    bs = (pidx % 2) * 2
                    sc.wait_ge(ctx.sems["PE"].h, pe(pidx, 0))
                    for j in range(2):
                        lin = pidx * 2 + j
                        if lin >= 2:
                            sc.wait_ge(ctx.sems["VE"].h, veb + lin - 1)
                        ctx.inc(sc.activation(gs[lin % 2][:, :], pg[bs + j][:, :], AF.Silu), "AC")

        @block.vector
        def _(v):
            ctx.waits(v, entry)
            for hc in range(64):
                for tp in range(NP):
                    pidx = hc * NP + tp
                    bs = (pidx % 2) * 2
                    v.wait_ge(ctx.sems["PE"].h, pe(pidx, 1))
                    for j in range(2):
                        tt = tp * 2 + j
                        lin = pidx * 2 + j
                        v.wait_ge(ctx.sems["AC"].h, ac(pidx, j))
                        if tt == 0 and j == 0 and hc >= 2:
                            v.wait_ge(ctx.sems["SS"].h, ss(hc - 2))
                        ctx.inc(v.tensor_mul(stg[hc % 2][:, tt * 512:(tt + 1) * 512],
                                             pl[bs + j][:, :], gs[lin % 2][:, :]), "VE")

    return ctx.snap()


def emit_ffn2(ctx, entry, wname="w2x", aname="hs", oname="g0", K=64, tag="f2"):
    nc, d, C = ctx.nc, ctx.dram, ctx.consts
    NKH = K
    NLG = (K + 7) // 8
    with ExitStack() as st:
        hbuf = [st.enter_context(nc.sbuf_tensor(f"{tag}_h{i}", [128, NKH * 512], BF16)) for i in range(2)]
        wbuf = [st.enter_context(nc.sbuf_tensor(f"{tag}_w{i}", [128, NKH * 128], BF16)) for i in range(2)]
        stg = [st.enter_context(nc.sbuf_tensor(f"{tag}_s{i}", [128, 512], BF16)) for i in range(2)]
        ps = [st.enter_context(nc.psum_tensor(f"{tag}_p{i}", [128, 512], F32)) for i in range(2)]
        block = st.enter_context(nc.Block(f"ffn2{tag}"))
        shb, swb, peb, acb, ssb = (ctx.sems[k].v for k in ("SH", "SW", "PE", "AC", "SS"))
        sh = lambda tg: shb + 16 * NLG * (tg + 1)
        sw = lambda idx: swb + 16 * (idx + 1)
        pe = lambda idx: peb + idx + 1
        ac = lambda idx: acb + idx + 1
        ss = lambda idx: ssb + 16 * (idx + 1)

        @block.sync
        def _(s):
            ctx.waits(s, entry)
            h3 = d[aname][:, :].rearrange("(kc p) t -> p kc t", p=128)

            def loads(tg):
                o3 = hbuf[tg % 2][:, :].rearrange("p (kc t) -> p kc t", t=512)
                if tg >= 2:
                    s.wait_ge(ctx.sems["PE"].h, pe((tg - 1) * 16 - 1))
                for gi in range(NLG):
                    k0, k1 = gi * 8, min(K, gi * 8 + 8)
                    ctx.inc(s.dma_start(o3[:, k0:k1, :],
                                        h3[:, k0:k1, tg * 512:(tg + 1) * 512]), "SH", 16)

            loads(0)
            for tg in range(8):
                if tg + 1 < 8:
                    loads(tg + 1)
                for m in range(16):
                    idx = tg * 16 + m
                    s.wait_ge(ctx.sems["AC"].h, ac(idx))
                    ctx.inc(s.dma_start(d[oname][m * 128:(m + 1) * 128, tg * 512:(tg + 1) * 512],
                                        stg[idx % 2][:, :]), "SS", 16)

        @block.gpsimd
        def _(g):
            ctx.waits(g, entry)
            for tg in range(8):
                for m in range(16):
                    idx = tg * 16 + m
                    if idx >= 2:
                        g.wait_ge(ctx.sems["PE"].h, pe(idx - 2))
                    win = d[wname][:, m * 128:(m + 1) * 128].rearrange("(kc p) m -> p kc m", p=128)
                    wout = wbuf[idx % 2][:, :].rearrange("p (kc m) -> p kc m", m=128)
                    ctx.inc(g.dma_start(wout, win), "SW", 16)

        @block.tensor
        def _(t):
            ctx.waits(t, entry)
            for tg in range(8):
                t.wait_ge(ctx.sems["SH"].h, sh(tg))
                for m in range(16):
                    idx = tg * 16 + m
                    t.wait_ge(ctx.sems["SW"].h, sw(idx))
                    if idx >= 2:
                        t.wait_ge(ctx.sems["AC"].h, ac(idx - 2))
                    for kc in range(NKH):
                        inst = t.matmul(ps[idx % 2][:, :], wbuf[idx % 2][:, kc * 128:(kc + 1) * 128],
                                        hbuf[tg % 2][:, kc * 512:(kc + 1) * 512],
                                        start=(kc == 0), stop=(kc == NKH - 1))
                    ctx.inc(inst, "PE")

        @block.scalar
        def _(sc):
            ctx.waits(sc, entry)
            for tg in range(8):
                for m in range(16):
                    idx = tg * 16 + m
                    sc.wait_ge(ctx.sems["PE"].h, pe(idx))
                    if idx >= 2:
                        sc.wait_ge(ctx.sems["SS"].h, ss(idx - 2))
                    ctx.inc(sc.activation(stg[idx % 2][:, :], ps[idx % 2][:, :], AF.Silu), "AC")

    return ctx.snap()


def emit_mult_t(ctx, entry, a_name, b_name, out_name, tag=""):
    nc, d, C = ctx.nc, ctx.dram, ctx.consts
    with ExitStack() as st:
        ab = [st.enter_context(nc.sbuf_tensor(f"mt{tag}_a{i}", [128, T], BF16)) for i in range(2)]
        bb = [st.enter_context(nc.sbuf_tensor(f"mt{tag}_b{i}", [128, T], BF16)) for i in range(2)]
        rr = [st.enter_context(nc.sbuf_tensor(f"mt{tag}_r{i}", [128, T], BF16)) for i in range(2)]
        stg = [st.enter_context(nc.sbuf_tensor(f"mt{tag}_s{i}", [128, T], BF16)) for i in range(2)]
        pst = [st.enter_context(nc.psum_tensor(f"mt{tag}_p{i}", [128, D], BF16)) for i in range(2)]
        block = st.enter_context(nc.Block(f"multT{tag}"))
        shb, swb, peb, veb, ssb = (ctx.sems[k].v for k in ("SH", "SW", "PE", "VE", "SS"))
        sh = lambda m: shb + 16 * (m + 1)
        sw = lambda m: swb + 16 * (m + 1)
        pe = lambda m, k: peb + m * 2 + k + 1
        ve = lambda m, j: veb + m * 3 + j + 1
        ss = lambda m: ssb + 16 * (m + 1)

        @block.sync
        def _(s):
            ctx.waits(s, entry)
            for m in range(16):
                if m >= 2:
                    s.wait_ge(ctx.sems["VE"].h, ve(m - 2, 0))
                ctx.inc(s.dma_start(ab[m % 2][:, :], d[a_name][m * 128:(m + 1) * 128, :]), "SH", 16)
                if m >= 1:
                    s.wait_ge(ctx.sems["VE"].h, ve(m - 1, 2))
                    o3 = d[out_name][:, (m - 1) * 128:m * 128].rearrange("(j p) dd -> p j dd", p=128)
                    i3 = stg[(m - 1) % 2][:, :].rearrange("p (j dd) -> p j dd", dd=128)
                    ctx.inc(s.dma_start(o3, i3), "SS", 16)
            s.wait_ge(ctx.sems["VE"].h, ve(15, 2))
            o3 = d[out_name][:, 15 * 128:16 * 128].rearrange("(j p) dd -> p j dd", p=128)
            i3 = stg[15 % 2][:, :].rearrange("p (j dd) -> p j dd", dd=128)
            ctx.inc(s.dma_start(o3, i3), "SS", 16)

        @block.gpsimd
        def _(g):
            ctx.waits(g, entry)
            for m in range(16):
                if m >= 2:
                    g.wait_ge(ctx.sems["VE"].h, ve(m - 2, 0))
                ctx.inc(g.dma_start(bb[m % 2][:, :], d[b_name][m * 128:(m + 1) * 128, :]), "SW", 16)

        @block.vector
        def _(v):
            ctx.waits(v, entry)
            for m in range(16):
                v.wait_ge(ctx.sems["SH"].h, sh(m))
                v.wait_ge(ctx.sems["SW"].h, sw(m))
                if m >= 2:
                    v.wait_ge(ctx.sems["PE"].h, pe(m - 1, 1))
                ctx.inc(v.tensor_mul(rr[m % 2][:, :], ab[m % 2][:, :], bb[m % 2][:, :]), "VE")
                for k in range(2):
                    v.wait_ge(ctx.sems["PE"].h, pe(m, k))
                    if m >= 2 and k == 0:
                        v.wait_ge(ctx.sems["SS"].h, ss(m - 2))
                    ctx.inc(v.tensor_copy(stg[m % 2][:, k * 2048:(k + 1) * 2048], pst[k][:, :]), "VE")

        @block.tensor
        def _(t):
            ctx.waits(t, entry)
            for m in range(16):
                t.wait_ge(ctx.sems["VE"].h, ve(m, 0))
                for k in range(2):
                    if m >= 1:
                        t.wait_ge(ctx.sems["VE"].h, ve(m - 1, 1 + k))
                    for j in range(16):
                        inst = t.transpose(pst[k][:, j * 128:(j + 1) * 128],
                                           rr[m % 2][:, (k * 16 + j) * 128:(k * 16 + j + 1) * 128],
                                           C["identb"][:, :])
                    ctx.inc(inst, "PE")

    return ctx.snap()


def emit_banks(ctx, entry):
    nc, d, C = ctx.nc, ctx.dram, ctx.consts
    with ExitStack() as st:
        sqh = [st.enter_context(nc.sbuf_tensor(f"bk_s{i}", [128, T], BF16)) for i in range(2)]
        membuf = [st.enter_context(nc.sbuf_tensor(f"bk_m{i}", [128, 512], BF16)) for i in range(2)]
        mnb = [st.enter_context(nc.sbuf_tensor(f"bk_n{i}", [128, 4], BF16)) for i in range(2)]
        norms = st.enter_context(nc.sbuf_tensor("bk_no", [4, T], F32))
        rn = st.enter_context(nc.sbuf_tensor("bk_rn", [4, T], F32))
        rnb = st.enter_context(nc.sbuf_tensor("bk_rnb", [4, T], BF16))
        asb = [st.enter_context(nc.sbuf_tensor(f"bk_a{i}", [128, 512], F32)) for i in range(4)]
        accb = st.enter_context(nc.sbuf_tensor("bk_ac", [128, 512], F32))
        t1 = st.enter_context(nc.sbuf_tensor("bk_t1", [128, 512], F32))
        t2 = st.enter_context(nc.sbuf_tensor("bk_t2", [128, 512], F32))
        cstage = [st.enter_context(nc.sbuf_tensor(f"bk_c{i}", [128, T], BF16)) for i in range(2)]
        ps_n = [st.enter_context(nc.psum_tensor(f"bk_pn{i}", [4, 512], F32)) for i in range(2)]
        ps_a = [st.enter_context(nc.psum_tensor(f"bk_pa{i}", [128, 512], F32)) for i in range(4)]
        ps_rep = [st.enter_context(nc.psum_tensor(f"bk_pr{i}", [128, 512], F32)) for i in range(2)]
        block = st.enter_context(nc.Block("banks"))
        shb, swb, peb, acb, veb, ssb = (ctx.sems[k].v for k in ("SH", "SW", "PE", "AC", "VE", "SS"))
        NPE, NAC, NVE = 48, 40, 17
        pe_b = lambda h: peb + h * NPE
        pe_n = lambda h, g: pe_b(h) + g + 1
        pe_ag = lambda h, g: pe_b(h) + 8 + g * 5 + 1
        pe_rp = lambda h, g, n: pe_b(h) + 8 + g * 5 + 2 + n
        ac_b = lambda h: acb + h * NAC
        ac_n = lambda h, g: ac_b(h) + g + 1
        ac_ae = lambda h, g, n: ac_b(h) + 8 + g * 4 + n + 1
        ve_b = lambda h: veb + h * NVE
        ve_rn = lambda h: ve_b(h) + 1
        ve_pair = lambda h, g: ve_b(h) + 1 + g * 2 + 1
        ve_comb = lambda h, g: ve_b(h) + 1 + g * 2 + 2
        sh = lambda h: shb + 16 * (h + 1)
        sw = lambda h: swb + 32 * (h + 1)
        ss = lambda h: ssb + 16 * (h + 1)

        @block.sync
        def _(s):
            ctx.waits(s, entry)
            for h in range(NH):
                if h >= 2:
                    s.wait_ge(ctx.sems["PE"].h, pe_b(h - 1))
                ctx.inc(s.dma_start(sqh[h % 2][:, :], d["sq"][h * 128:(h + 1) * 128, :]), "SH", 16)
                if h >= 1:
                    s.wait_ge(ctx.sems["VE"].h, ve_comb(h - 1, 7))
                    ctx.inc(s.dma_start(d["cc"][(h - 1) * 128:h * 128, :], cstage[(h - 1) % 2][:, :]), "SS", 16)
            s.wait_ge(ctx.sems["VE"].h, ve_comb(NH - 1, 7))
            ctx.inc(s.dma_start(d["cc"][(NH - 1) * 128:NH * 128, :], cstage[(NH - 1) % 2][:, :]), "SS", 16)

        @block.gpsimd
        def _(g):
            ctx.waits(g, entry)
            for h in range(NH):
                if h >= 2:
                    g.wait_ge(ctx.sems["PE"].h, pe_b(h - 1))
                m3 = d["mem2"][h * 512:(h + 1) * 512, :].rearrange("(n p) e -> p n e", p=128)
                o3 = membuf[h % 2][:, :].rearrange("p (n e) -> p n e", e=128)
                ctx.inc(g.dma_start(o3, m3), "SW", 16)
                ctx.inc(g.dma_start(mnb[h % 2][:, :], d["mnl"][h * 128:(h + 1) * 128, :]), "SW", 16)

        @block.tensor
        def _(t):
            ctx.waits(t, entry)
            for h in range(NH):
                t.wait_ge(ctx.sems["SW"].h, sw(h))
                t.wait_ge(ctx.sems["SH"].h, sh(h))
                for g in range(8):
                    if g >= 2:
                        t.wait_ge(ctx.sems["AC"].h, ac_n(h, g - 2))
                    elif h >= 1:
                        t.wait_ge(ctx.sems["AC"].h, ac_n(h - 1, 6 + g))
                    ctx.inc(t.matmul(ps_n[g % 2][:, :], mnb[h % 2][:, :],
                                     sqh[h % 2][:, g * 512:(g + 1) * 512], start=True, stop=True), "PE")
                for g in range(8):
                    if g >= 1:
                        t.wait_ge(ctx.sems["AC"].h, ac_ae(h, g - 1, 3))
                    elif h >= 1:
                        t.wait_ge(ctx.sems["AC"].h, ac_ae(h - 1, 7, 3))
                    for n in range(4):
                        inst = t.matmul(ps_a[n][:, :], membuf[h % 2][:, n * 128:(n + 1) * 128],
                                        sqh[h % 2][:, g * 512:(g + 1) * 512], start=True, stop=True)
                    ctx.inc(inst, "PE")
                    if g == 0:
                        t.wait_ge(ctx.sems["VE"].h, ve_rn(h))
                    for n in range(4):
                        if n < 2:
                            w = ve_comb(h, g - 1) if g >= 1 else (ve_comb(h - 1, 7) if h >= 1 else 0)
                        else:
                            w = ve_pair(h, g)
                        if w:
                            t.wait_ge(ctx.sems["VE"].h, w)
                        ctx.inc(t.matmul(ps_rep[n % 2][:, :], C["e4b"][:, n * 128:(n + 1) * 128],
                                         rnb[:, g * 512:(g + 1) * 512], start=True, stop=True), "PE")

        @block.scalar
        def _(sc):
            ctx.waits(sc, entry)
            for h in range(NH):
                for g in range(8):
                    sc.wait_ge(ctx.sems["PE"].h, pe_n(h, g))
                    if g == 0 and h >= 1:
                        sc.wait_ge(ctx.sems["VE"].h, ve_rn(h - 1))
                    ctx.inc(sc.activation(norms[:, g * 512:(g + 1) * 512], ps_n[g % 2][:, :],
                                          AF.Identity), "AC")
                for g in range(8):
                    sc.wait_ge(ctx.sems["PE"].h, pe_ag(h, g))
                    if g >= 1:
                        sc.wait_ge(ctx.sems["VE"].h, ve_comb(h, g - 1))
                    elif h >= 1:
                        sc.wait_ge(ctx.sems["VE"].h, ve_comb(h - 1, 7))
                    for n in range(4):
                        ctx.inc(sc.activation(asb[n][:, :], ps_a[n][:, :], AF.Identity), "AC")

        @block.vector
        def _(v):
            ctx.waits(v, entry)
            for h in range(NH):
                v.wait_ge(ctx.sems["AC"].h, ac_n(h, 7))
                ctx.guard(v, v.tensor_scalar_max(norms[:, :], norms[:, :], C["eps6"][0:4, 0:1]))
                ctx.guard(v, v.reciprocal(rn[:, :], norms[:, :]))
                ctx.guard(v, v.tensor_scalar_mul(rn[:, :], rn[:, :], C["wpT"][:, h:h + 1]))
                ctx.inc(v.tensor_copy(rnb[:, :], rn[:, :]), "VE")
                for g in range(8):
                    if g == 0 and h >= 2:
                        v.wait_ge(ctx.sems["SS"].h, ss(h - 2))
                    v.wait_ge(ctx.sems["AC"].h, ac_ae(h, g, 3))
                    for n in range(4):
                        v.wait_ge(ctx.sems["PE"].h, pe_rp(h, g, n))
                        dst = [accb, t1, t2, t1][n]
                        inst = v.tensor_mul(dst[:, :], asb[n][:, :], ps_rep[n % 2][:, :])
                        if n == 1:
                            ctx.inc(inst, "VE")
                        if n == 1:
                            v.tensor_add(accb[:, :], accb[:, :], t1[:, :])
                        elif n == 2:
                            v.tensor_add(accb[:, :], accb[:, :], t2[:, :])
                    ctx.inc(v.tensor_add(cstage[h % 2][:, g * 512:(g + 1) * 512],
                                         accb[:, :], t1[:, :]), "VE")

    assert ctx.sems["PE"].v == peb + NH * NPE
    assert ctx.sems["AC"].v == acb + NH * NAC
    assert ctx.sems["VE"].v == veb + NH * NVE
    return ctx.snap()


def _emit_build(ctx, v, dst, idx_bf, width, ls=range(16), tmp_name="_btmp"):
    C = ctx.consts
    ls = list(ls)
    l0 = ls[0]
    v.tensor_scalar(dst, idx_bf, C["lvals"][:, l0:l0 + 1], C["lut_rep"][:, l0:l0 + 1],
                    ALU.is_equal, ALU.mult)
    inst = None
    tmp = ctx.consts[tmp_name]
    for l in ls[1:]:
        v.tensor_scalar(tmp[:, 0:width], idx_bf, C["lvals"][:, l:l + 1],
                        C["lut_rep"][:, l:l + 1], ALU.is_equal, ALU.mult)
        inst = v.tensor_add(dst, dst, tmp[:, 0:width])
    return inst


def emit_ghost_gu(ctx, entry):
    nc, d, C = ctx.nc, ctx.dram, ctx.consts
    NT = T // 512
    with ExitStack() as st:
        act = st.enter_context(nc.sbuf_tensor("gg_act", [128, NKC * T], BF16))
        wgb = [st.enter_context(nc.sbuf_tensor(f"gg_wg{i}", [128, D], BF16)) for i in range(2)]
        wub = [st.enter_context(nc.sbuf_tensor(f"gg_wu{i}", [128, D], BF16)) for i in range(2)]
        gs = [st.enter_context(nc.sbuf_tensor(f"gg_g{i}", [128, 512], F32)) for i in range(2)]
        stg = [st.enter_context(nc.sbuf_tensor(f"gg_s{i}", [128, T], BF16)) for i in range(2)]
        pg = [st.enter_context(nc.psum_tensor(f"gg_pg{i}", [128, 512], F32)) for i in range(4)]
        pl = [st.enter_context(nc.psum_tensor(f"gg_pl{i}", [128, 512], F32)) for i in range(4)]
        block = st.enter_context(nc.Block("ghostgu"))
        shb, swb, peb, acb, veb, ssb = (ctx.sems[k].v for k in ("SH", "SW", "PE", "AC", "VE", "SS"))
        sh_act = shb + 64
        NP = NT // 2
        sw = lambda ic: swb + 32 * (ic + 1)
        pe = lambda pidx, k: peb + pidx * 2 + k + 1
        ac = lambda pidx, j: acb + pidx * 2 + j + 1
        ve_m = lambda pidx, j: veb + pidx * 2 + j + 1
        ss = lambda ic: ssb + 16 * (ic + 1)

        @block.sync
        def _(s):
            ctx.waits(s, entry)
            a3 = d["xn1"][:, :].rearrange("(kc p) t -> p kc t", p=128)
            o3 = act[:, :].rearrange("p (kc t) -> p kc t", t=T)
            for gi in range(4):
                ctx.inc(s.dma_start(o3[:, gi * 4:gi * 4 + 4, :], a3[:, gi * 4:gi * 4 + 4, :]), "SH", 16)
            for ic in range(NIC):
                s.wait_ge(ctx.sems["VE"].h, ve_m(ic * NP + NP - 1, 1))
                ctx.inc(s.dma_start(d["h1"][ic * 128:(ic + 1) * 128, :], stg[ic % 2][:, :]), "SS", 16)

        @block.gpsimd
        def _(g):
            ctx.waits(g, entry)
            g.wait_ge(ctx.consts["ccdone"], 1)
            for ic in range(NIC):
                if ic >= 2:
                    g.wait_ge(ctx.sems["PE"].h, pe((ic - 2) * NP + NP - 1, 1))
                rg, ru = _grow(ic, 0), _grow(ic, 1)
                ctx.inc(g.dma_start(wgb[ic % 2][:, :], d["gwall"][rg:rg + 128, :]), "SW", 16)
                ctx.inc(g.dma_start(wub[ic % 2][:, :], d["gwall"][ru:ru + 128, :]), "SW", 16)

        @block.vector
        def _(v):
            ctx.waits(v, entry)
            for ic in range(NIC):
                for tp in range(NP):
                    pidx = ic * NP + tp
                    bs = (pidx % 2) * 2
                    v.wait_ge(ctx.sems["PE"].h, pe(pidx, 1))
                    for j in range(2):
                        tt = tp * 2 + j
                        lin = pidx * 2 + j
                        v.wait_ge(ctx.sems["AC"].h, ac(pidx, j))
                        if tt == 0 and j == 0 and ic >= 2:
                            v.wait_ge(ctx.sems["SS"].h, ss(ic - 2))
                        ctx.inc(v.tensor_mul(stg[ic % 2][:, tt * 512:(tt + 1) * 512],
                                             pl[bs + j][:, :], gs[lin % 2][:, :]), "VE")

        @block.tensor
        def _(t):
            ctx.waits(t, entry)
            t.wait_ge(ctx.sems["SH"].h, sh_act)
            for ic in range(NIC):
                t.wait_ge(ctx.sems["SW"].h, sw(ic))
                for tp in range(NP):
                    pidx = ic * NP + tp
                    if pidx >= 2:
                        t.wait_ge(ctx.sems["AC"].h, ac(pidx - 2, 1))
                        t.wait_ge(ctx.sems["VE"].h, ve_m(pidx - 2, 1))
                    bs = (pidx % 2) * 2
                    o0 = (tp * 2) * 512
                    for kc in range(NKC):
                        co = kc * T + o0
                        inst = t.matmul(pg[bs][:, :], wgb[ic % 2][:, kc * 128:(kc + 1) * 128],
                                        act[:, co: co + 512], start=(kc == 0), stop=(kc == NKC - 1))
                        inst = t.matmul(pg[bs + 1][:, :], wgb[ic % 2][:, kc * 128:(kc + 1) * 128],
                                        act[:, co + 512: co + 1024], start=(kc == 0), stop=(kc == NKC - 1))
                    ctx.inc(inst, "PE")
                    for kc in range(NKC):
                        co = kc * T + o0
                        inst = t.matmul(pl[bs][:, :], wub[ic % 2][:, kc * 128:(kc + 1) * 128],
                                        act[:, co: co + 512], start=(kc == 0), stop=(kc == NKC - 1))
                        inst = t.matmul(pl[bs + 1][:, :], wub[ic % 2][:, kc * 128:(kc + 1) * 128],
                                        act[:, co + 512: co + 1024], start=(kc == 0), stop=(kc == NKC - 1))
                    ctx.inc(inst, "PE")

        @block.scalar
        def _(sc):
            ctx.waits(sc, entry)
            for ic in range(NIC):
                for tp in range(NP):
                    pidx = ic * NP + tp
                    bs = (pidx % 2) * 2
                    sc.wait_ge(ctx.sems["PE"].h, pe(pidx, 0))
                    for j in range(2):
                        lin = pidx * 2 + j
                        if lin >= 2:
                            sc.wait_ge(ctx.sems["VE"].h, veb + lin - 1)
                        ctx.inc(sc.activation(gs[lin % 2][:, :], pg[bs + j][:, :], AF.Silu), "AC")

    return ctx.snap()


def emit_gbuild(ctx, entry):
    nc, d, C = ctx.nc, ctx.dram, ctx.consts
    NTILE = 18
    idxt = [_sb(ctx, f"gb2_i{i}", [128, D], BF16) for i in range(2)]
    wst = [_sb(ctx, f"gb2_w{i}", [128, D], BF16) for i in range(2)]
    btmp = _sb(ctx, "gb2_bt", [128, D], BF16)
    ctx.consts["_btmp"] = btmp
    ccdone = ctx.st.enter_context(nc.semaphore("ccdone"))
    ctx.consts["ccdone"] = ccdone
    blk = nc.Block("gbuild")
    block = blk.__enter__()
    swb, veb, ssb = (ctx.sems[k].v for k in ("SW", "VE", "SS"))
    sw = lambda k: swb + 16 * (k + 1)
    ve = lambda k: veb + k + 1
    ss = lambda k: ssb + 16 * (k + 1)

    @block.gpsimd
    def _(g):
        ctx.waits(g, entry)
        for k in range(NTILE):
            if k >= 2:
                g.wait_ge(ctx.sems["VE"].h, ve(k - 2))
            ctx.inc(g.dma_start(idxt[k % 2][:, :], d["gsh"][k * 128:(k + 1) * 128, :]), "SW", 16)
        ctx.consts["gbuild_ss"] = ss(NTILE - 1)

    @block.vector
    def _(v):
        ctx.waits(v, entry)
        for k in range(NTILE):
            v.wait_ge(ctx.sems["SW"].h, sw(k))
            if k >= 2:
                v.wait_ge(ctx.sems["SS"].h, ss(k - 2))
            ctx.inc(_emit_build(ctx, v, wst[k % 2][:, :], idxt[k % 2][:, :], D), "VE")

    @block.sync
    def _(s):
        ctx.waits(s, entry)
        for k in range(NTILE):
            s.wait_ge(ctx.sems["VE"].h, ve(k))
            ctx.inc(s.dma_start(d["cgc"][k * 128:(k + 1) * 128, :], wst[k % 2][:, :]), "SS", 16)

    blk.__exit__(None, None, None)
    del ctx.consts["_btmp"]
    return ctx.snap()


def emit_gcoll(ctx, entry):
    nc, d = ctx.nc, ctx.dram
    blk = nc.Block("gcoll")
    block = blk.__enter__()

    @block.gpsimd
    def _(g):
        g.wait_ge(ctx.sems["SS"].h, ctx.consts["gbuild_ss"])
        g.collective_compute(
            "AllGather", mybir.AluOpType.bypass,
            replica_groups=[list(range(NCORES))],
            ins=[d["cgc"].ap().opt()], outs=[d["gwall"].ap().opt()],
        ).then_inc(ctx.consts["ccdone"], 1)

    blk.__exit__(None, None, None)
    return ctx.snap()


def emit_repack(ctx, entry):
    nc, d = ctx.nc, ctx.dram
    ccdone = ctx.consts["ccdone"]
    blk = nc.Block("repack")
    block = blk.__enter__()

    @block.gpsimd
    def _(g):
        ctx.waits(g, entry)
        g.wait_ge(ccdone, 1)
        for ic in range(NIC):
            r = _grow(ic, 2)
            ctx.inc(g.dma_start(d["wdx"][ic * 128:(ic + 1) * 128, :],
                                d["gwall"][r:r + 128, :]), "SW", 16)

    blk.__exit__(None, None, None)
    return ctx.snap()


def emit_gdbuild(ctx, entry):
    nc, d, C = ctx.nc, ctx.dram, ctx.consts
    with ExitStack() as st:
        idx_t = [st.enter_context(nc.sbuf_tensor(f"gb_i{i}", [128, D], BF16)) for i in range(2)]
        wds = [st.enter_context(nc.sbuf_tensor(f"gb_w{i}", [128, D], BF16)) for i in range(2)]
        btmp = st.enter_context(nc.sbuf_tensor("gb_bt", [128, D], BF16))
        ctx.consts["_btmp"] = btmp
        block = st.enter_context(nc.Block("gdbuild"))
        swb, veb, ssb = (ctx.sems[k].v for k in ("SW", "VE", "SS"))
        sw = lambda kc: swb + 16 * (kc + 1)
        ve_b = lambda kc: veb + kc + 1
        ss = lambda kc: ssb + 16 * (kc + 1)

        @block.gpsimd
        def _(g):
            ctx.waits(g, entry)
            for kc in range(NIC):
                if kc >= 2:
                    g.wait_ge(ctx.sems["VE"].h, ve_b(kc - 2))
                ctx.inc(g.dma_start(idx_t[kc % 2][:, :], d["didx"][kc * 128:(kc + 1) * 128, :]), "SW", 16)

        @block.vector
        def _(v):
            ctx.waits(v, entry)
            for kc in range(NIC):
                v.wait_ge(ctx.sems["SW"].h, sw(kc))
                if kc >= 2:
                    v.wait_ge(ctx.sems["SS"].h, ss(kc - 2))
                ctx.inc(_emit_build(ctx, v, wds[kc % 2][:, :], idx_t[kc % 2][:, :], D), "VE")

        @block.sync
        def _(s):
            ctx.waits(s, entry)
            for kc in range(NIC):
                s.wait_ge(ctx.sems["VE"].h, ve_b(kc))
                ctx.inc(s.dma_start(d["wdx"][kc * 128:(kc + 1) * 128, :], wds[kc % 2][:, :]), "SS", 16)

    del ctx.consts["_btmp"]
    return ctx.snap()


def _u8view(a):
    a = np.ascontiguousarray(a)
    return np.ascontiguousarray(a.view(np.uint8).reshape(*a.shape, 4)[..., 0])


def _hadamard(n):
    h = np.array([[1.0]], dtype=np.float64)
    while h.shape[0] < n:
        h = np.block([[h, h], [h, -h]])
    return h


def _grow(ic, mat):
    return ((ic % 8) * 18 + mat * 6 + ic // 8) * 128


def prep_gsh(W, core):
    gi, ui, di = W["gidx"], W["uidx"], W["didx"]
    gsh = np.zeros((18, 128, D), np.uint8)
    for s in range(6):
        ic = core + 8 * s
        if ic >= NIC:
            continue
        for mat, srcm in ((0, gi), (1, ui)):
            col = srcm[:, ic * 128:(ic + 1) * 128]
            gsh[mat * 6 + s] = col.reshape(16, 128, 128).transpose(1, 0, 2).reshape(128, D)
        gsh[12 + s] = di[ic * 128:(ic + 1) * 128, :]
    return np.ascontiguousarray(gsh.reshape(18 * 128, D))


def prep_inputs(inputs):
    W = {}
    f32 = lambda a: np.ascontiguousarray(np.asarray(a), dtype=np.float32)
    W["wqkT"] = np.ascontiguousarray(np.concatenate(
        [f32(inputs["L0_attn_Wq"]).T, f32(inputs["L0_attn_Wk"]).T], axis=1))
    bqk = np.concatenate([f32(inputs["L0_attn_bq"]), f32(inputs["L0_attn_bk"])])
    W["bqk"] = np.ascontiguousarray(bqk.reshape(32, 128).T)
    W["wvT"] = np.ascontiguousarray(f32(inputs["L0_attn_Wv"]).T)
    W["bv"] = f32(inputs["L0_attn_bv"]).reshape(1, D)
    W["woT"] = np.ascontiguousarray(f32(inputs["L0_attn_Wo"]).T)
    W["bo"] = np.ascontiguousarray(f32(inputs["L0_attn_bo"]).reshape(16, 128).T)
    W["w13T"] = np.ascontiguousarray(np.concatenate(
        [f32(inputs["L0_ffn_W1"]).T, f32(inputs["L0_ffn_W3"]).T], axis=1))
    W["w2T"] = np.ascontiguousarray(f32(inputs["L0_ffn_W2"]).T)
    W["l1qT"] = np.ascontiguousarray(f32(inputs["L1_attn_Wq"]).T)
    W["l1oT"] = np.ascontiguousarray(f32(inputs["L1_attn_Wo"]).T)
    W["mem2"] = np.ascontiguousarray(
        np.transpose(f32(inputs["L1_attn_memories"]), (1, 0, 2, 3)).reshape(NH * 4 * 128, 128))
    W["mnl"] = np.ascontiguousarray(
        np.transpose(f32(inputs["L1_attn_memory_norms"]), (1, 2, 0)).reshape(NH * 128, 4))
    W["mnrows"] = np.ascontiguousarray(f32(inputs["L1_attn_memory_norms"]).reshape(4, D))
    W["bwt"] = f32(inputs["L1_attn_bank_weights"])
    e4 = np.zeros((4, 512), np.float32)
    for n in range(4):
        e4[n, n * 128:(n + 1) * 128] = 1.0
    W["e4"] = e4
    W["e4b"] = e4.astype(BF)
    W["lutrow"] = f32(inputs["L1_ffn_lut"]).reshape(1, 16)
    W["lvrow"] = np.arange(16, dtype=np.float32).reshape(1, 16)
    W["gidx"] = _u8view(np.asarray(inputs["L1_ffn_gate_idx"]).T)
    W["uidx"] = _u8view(np.asarray(inputs["L1_ffn_up_idx"]).T)
    W["didx"] = _u8view(np.asarray(inputs["L1_ffn_down_idx"]).T)
    W["identb"] = np.eye(128, dtype=BF)
    W["ident32"] = np.eye(128, dtype=np.float32)
    W["h128b"] = (_hadamard(128) * (2.0 ** -7)).astype(BF)
    W["h128f"] = (_hadamard(128) * (2.0 ** -7)).astype(np.float32)
    W["h64b"] = _hadamard(64).astype(BF)
    return W


def build_phase_program(phases, io_names):
    nc = bass.Bass("TRN2", target_bir_lowering=False)
    st = nc.ctx
    ctx = Ctx(nc, st)
    ins, outs = io_names
    for n in ins:
        shape, dt, _ = SPECS[n]
        ctx.dram[n] = nc.dram_tensor(n, shape, dt, kind="ExternalInput")
    for n in outs:
        shape, dt, _ = SPECS[n]
        ctx.dram[n] = nc.dram_tensor(n, shape, dt, kind="ExternalOutput")
    entry = ctx.snap()
    entry = emit_init(ctx, entry)
    for ph in phases:
        entry = ph(ctx, entry)
    with nc.Block("fin") as block:
        @block.sync
        def _(s):
            ctx.waits(s, entry)
    return nc


CONST_INS = ["identb", "ident32", "h128b", "h64b", "h128f", "lutrow", "lvrow",
             "bqk", "bo", "bv", "bwt", "mnrows", "e4", "e4b"]


PHASE_LIST = [
    ("ln0", [lambda c, e: emit_combine_ln(c, e, "x", None, None, "xn", tag="l0")], ["x"], ["xn"]),
    ("qkv", [lambda c, e: emit_proj(c, e, "wqkT", "xn", "qk", NKC, 32, "bias", bias="bqk", tag="qk")],
     ["wqkT", "xn"], ["qk"]),
    ("vproj", [lambda c, e: emit_vproj(c, e)], ["wvT", "xn"], ["vv"]),
    ("attn", [lambda c, e: emit_attn(c, e)], ["qk", "vv"], ["oo"]),
    ("oproj", [lambda c, e: emit_proj(c, e, "woT", "oo", "s0", NKC, 16, "bias", bias="bo", tag="op")],
     ["woT", "oo"], ["s0"]),
    ("w2x", [lambda c, e: emit_w2x(c, e)], ["w2T"], ["w2x1", "w2x"]),
    ("ffn13", [lambda c, e: emit_ffn13(c, e)], ["w13T", "xn"], ["hs"]),
    ("ffn2", [lambda c, e: emit_ffn2(c, e)], ["w2x", "hs"], ["g0"]),
    ("mt0", [lambda c, e: emit_mult_t(c, e, "s0", "g0", "r0tm", tag="0")], ["s0", "g0"], ["r0tm"]),
    ("cmb0", [lambda c, e: emit_combine_ln(c, e, "x", "r0tm", "x1", "xn1", tag="c0")],
     ["x", "r0tm"], ["x1", "xn1"]),
    ("q1", [lambda c, e: emit_proj(c, e, "l1qT", "xn1", "sq", NKC, 16, "elu1", tag="q1")],
     ["l1qT", "xn1"], ["sq"]),
    ("banks", [lambda c, e: emit_banks(c, e)], ["sq", "mem2", "mnl"], ["cc"]),
    ("o1", [lambda c, e: emit_proj(c, e, "l1oT", "cc", "s1", NKC, 16, "copy", tag="o1")],
     ["l1oT", "cc"], ["s1"]),
    ("ggu", [lambda c, e: emit_ghost_gu(c, e)], ["gsh", "xn1"], ["h1"]),
    ("rpk", [lambda c, e: emit_repack(c, e)], [], ["wdx"]),
    ("gdn", [lambda c, e: emit_ffn2(c, e, "wdx", "h1", "g1", NIC, tag="gd")], ["wdx", "h1"], ["g1"]),
    ("mt1", [lambda c, e: emit_mult_t(c, e, "s1", "g1", "r1tm", tag="1")], ["s1", "g1"], ["r1tm"]),
    ("cmb1", [lambda c, e: emit_combine_ln(c, e, "x1", "r1tm", "out", None, tag="c1")],
     ["x1", "r1tm"], ["out"]),
]


def build_fused_program():
    nc = bass.Bass("TRN2", target_bir_lowering=False)
    ctx = Ctx(nc, nc.ctx)
    for n, (shape, dt, kind) in SPECS.items():
        if kind == "in":
            ctx.dram[n] = nc.dram_tensor(n, shape, dt, kind="ExternalInput")
        elif kind == "out":
            ctx.dram[n] = nc.dram_tensor(n, shape, dt, kind="ExternalOutput")
        else:
            aspace = "Shared" if n == "gwall" else "Local"
            ctx.dram[n] = nc.dram_tensor(n, shape, dt, kind="Internal", addr_space=aspace)
    entry = ctx.snap()
    entry = emit_init(ctx, entry)
    for name, emitters, ins, outs in PHASE_LIST:
        for ph in emitters:
            entry2 = ph(ctx, entry)
            entry = entry2
        if name == "oproj":
            emit_gbuild(ctx, entry)
        elif name == "w2x":
            emit_gcoll(ctx, entry)
    with nc.Block("fin") as block:
        @block.sync
        def _(s):
            ctx.waits(s, entry)
    return nc


FUSED = True


def kernel(**inputs):
    W = prep_inputs(inputs)
    x = np.asarray(inputs["L0_attn_x"], dtype=np.float32)
    percore = {"x": [np.ascontiguousarray(x[i]) for i in range(NCORES)],
               "gsh": [prep_gsh(W, i) for i in range(NCORES)]}
    if FUSED:
        nc = build_fused_program()
        in_names = [n for n, (s, dt, k) in SPECS.items() if k == "in"]
        in_maps = []
        for core in range(NCORES):
            m = {n: (percore[n][core] if n in percore else W[n]) for n in in_names}
            in_maps.append(m)
        res = run_bass_kernel_spmd(nc, in_maps, core_ids=list(range(NCORES)))
        return np.stack([np.asarray(res.results[c]["out"], dtype=np.float32) for c in range(NCORES)])
    for name, emitters, ins, outs in PHASE_LIST:
        all_ins = sorted(set(ins) | set(CONST_INS))
        nc = build_phase_program(emitters, (all_ins, outs))
        in_maps = []
        for core in range(NCORES):
            m = {}
            for n in all_ins:
                m[n] = percore[n][core] if n in percore else W[n]
            in_maps.append(m)
        res = run_bass_kernel_spmd(nc, in_maps, core_ids=list(range(NCORES)))
        for n in outs:
            percore[n] = [res.results[core][n] for core in range(NCORES)]
    return np.stack([np.asarray(o, dtype=np.float32) for o in percore["out"]])


# version 13
# speedup vs baseline: 1.0036x; 1.0036x over previous
import numpy as np
import ml_dtypes
from contextlib import ExitStack

import concourse.bass as bass
import concourse.mybir as mybir
from concourse.bass_utils import run_bass_kernel_spmd

F32 = mybir.dt.float32
BF16 = mybir.dt.bfloat16
U8 = mybir.dt.uint8
AF = mybir.ActivationFunctionType
ALU = mybir.AluOpType
AX = mybir.AxisListType
BF = ml_dtypes.bfloat16

D = 2048
T = 4096
NKC = 16
NH = 16
HID0 = 8192
INTER = 5504
NIC = 43
NCORES = 8
RS128 = 1.0 / float(np.sqrt(128.0))

SPECS = {
    "x":     ([T, D], F32, "in"),
    "wqkT":  ([D, 2 * D], F32, "in"),
    "bqk":   ([128, 32], F32, "in"),
    "wvT":   ([D, D], F32, "in"),
    "bv":    ([1, D], F32, "in"),
    "woT":   ([D, D], F32, "in"),
    "bo":    ([128, 16], F32, "in"),
    "w13T":  ([D, 2 * HID0], F32, "in"),
    "w2T":   ([HID0, D], F32, "in"),
    "l1qT":  ([D, D], F32, "in"),
    "l1oT":  ([D, D], F32, "in"),
    "mem2":  ([NH * 4 * 128, 128], F32, "in"),
    "mnl":   ([NH * 128, 4], F32, "in"),
    "mnrows": ([4, D], F32, "in"),
    "e4":    ([4, 512], F32, "in"),
    "e4b":   ([4, 512], BF16, "in"),
    "bwt":   ([16, 4], F32, "in"),
    "lutrow": ([1, 16], F32, "in"),
    "lvrow": ([1, 16], F32, "in"),
    "gidx":  ([D, INTER], U8, "in"),
    "uidx":  ([D, INTER], U8, "in"),
    "didx":  ([INTER, D], U8, "in"),
    "identb": ([128, 128], BF16, "in"),
    "ident32": ([128, 128], F32, "in"),
    "h128b": ([128, 128], BF16, "in"),
    "h128f": ([128, 128], F32, "in"),
    "h64b":  ([64, 64], BF16, "in"),
    "xn":    ([D, T], BF16, "scr"),
    "qk":    ([2 * D, T], BF16, "scr"),
    "vv":    ([T, D], BF16, "scr"),
    "oo":    ([D, T], BF16, "scr"),
    "s0":    ([D, T], BF16, "scr"),
    "w2x1":  ([HID0, D], BF16, "scr"),
    "w2x":   ([HID0, D], BF16, "scr"),
    "hs":    ([HID0, T], BF16, "scr"),
    "g0":    ([D, T], BF16, "scr"),
    "r0tm":  ([T, D], BF16, "scr"),
    "x1":    ([T, D], F32, "scr"),
    "xn1":   ([D, T], BF16, "scr"),
    "sq":    ([D, T], BF16, "scr"),
    "cc":    ([D, T], BF16, "scr"),
    "s1":    ([D, T], BF16, "scr"),
    "h1":    ([INTER, T], BF16, "scr"),
    "wdx":   ([INTER, D], BF16, "scr"),
    "gsh":   ([18 * 128, D], U8, "in"),
    "cgc":   ([18 * 128, D], BF16, "scr"),
    "gwall": ([8 * 18 * 128, D], BF16, "scr"),
    "g1":    ([D, T], BF16, "scr"),
    "r1tm":  ([T, D], BF16, "scr"),
    "out":   ([T, D], F32, "out"),
}

SEM_NAMES = ("SH", "SW", "SS", "PE", "VE", "AC", "PO", "GV", "GA", "GP")


class Sem:
    __slots__ = ("h", "v")

    def __init__(self, h):
        self.h = h
        self.v = 0


class Ctx:
    def __init__(self, nc, st):
        self.nc = nc
        self.st = st
        self.sems = {n: Sem(st.enter_context(nc.semaphore(n))) for n in SEM_NAMES}
        self.consts = {}
        self.dram = {}

    def snap(self):
        return {k: s.v for k, s in self.sems.items()}

    def waits(self, eng, snap):
        for k, v in snap.items():
            if v > 0:
                eng.wait_ge(self.sems[k].h, v)

    def inc(self, inst, name, n=1):
        s = self.sems[name]
        inst.then_inc(s.h, n)
        s.v += n
        return s.v

    def guard(self, eng, *insts, sem="GV"):
        for inst in insts:
            v = self.inc(inst, sem)
        eng.wait_ge(self.sems[sem].h, v)


def _sb(ctx, name, shape, dtype):
    return ctx.st.enter_context(ctx.nc.sbuf_tensor(name, shape, dtype))


def emit_init(ctx, entry):
    nc, d = ctx.nc, ctx.dram
    C = ctx.consts
    C["identb"] = _sb(ctx, "c_identb", [128, 128], BF16)
    C["ident32"] = _sb(ctx, "c_ident32", [128, 128], F32)
    C["h128b"] = _sb(ctx, "c_h128b", [128, 128], BF16)
    C["h128f"] = _sb(ctx, "c_h128f", [128, 128], F32)
    C["h64b"] = _sb(ctx, "c_h64b", [64, 64], BF16)
    C["bqk"] = _sb(ctx, "c_bqk", [128, 32], F32)
    C["bo"] = _sb(ctx, "c_bo", [128, 16], F32)
    C["bv_bf"] = _sb(ctx, "c_bv", [1, D], BF16)
    C["lut_rep"] = _sb(ctx, "c_lutrep", [128, 16], F32)
    C["lvals"] = _sb(ctx, "c_lvals", [128, 16], F32)
    C["ones1f"] = _sb(ctx, "c_ones1f", [1, 128], F32)
    C["ones1b"] = _sb(ctx, "c_ones1b", [1, 128], BF16)
    C["onescb"] = _sb(ctx, "c_onescb", [128, 1], BF16)
    C["onescf"] = _sb(ctx, "c_onescf", [128, 1], F32)
    C["wpT"] = _sb(ctx, "c_wpT", [4, 16], F32)
    C["e4"] = _sb(ctx, "c_e4", [4, 512], F32)
    C["e4b"] = _sb(ctx, "c_e4b", [4, 512], BF16)
    for nm, val in (("eps5", 1e-5), ("eps6", 1e-6), ("invD", 1.0 / D),
                    ("ninvD", -1.0 / D), ("negone", -1.0), ("nRS", -1.02 * RS128)):
        C[nm] = _sb(ctx, "c_" + nm, [128, 1], F32)
    ist = ExitStack()
    lr = ist.enter_context(nc.sbuf_tensor("i_lutrow", [1, 16], F32))
    lv = ist.enter_context(nc.sbuf_tensor("i_lvrow", [1, 16], F32))
    bwt = ist.enter_context(nc.sbuf_tensor("i_bw", [16, 4], F32))
    mnr = ist.enter_context(nc.sbuf_tensor("i_mnr", [4, D], F32))
    sm = ist.enter_context(nc.sbuf_tensor("i_sm", [16, 4], F32))
    t16 = ist.enter_context(nc.sbuf_tensor("i_t16", [16, 1], F32))
    t4 = ist.enter_context(nc.sbuf_tensor("i_t4", [4, 1], F32))
    mrow = ist.enter_context(nc.sbuf_tensor("i_mrow", [1, 4], F32))
    mrep = ist.enter_context(nc.sbuf_tensor("i_mrep", [16, 4], F32))
    wp = ist.enter_context(nc.sbuf_tensor("i_wp", [16, 4], F32))
    ps_a = ist.enter_context(nc.psum_tensor("i_psa", [128, 512], F32))
    ps_b = ist.enter_context(nc.psum_tensor("i_psb", [128, 512], F32))

    blk_guard = nc.Block("init")
    block = blk_guard.__enter__()
    shb, veb, acb, peb = ctx.sems["SH"].v, ctx.sems["VE"].v, ctx.sems["AC"].v, ctx.sems["PE"].v

    @block.sync
    def _(s):
        ctx.waits(s, entry)
        for nm, src in (("identb", "identb"), ("ident32", "ident32"), ("h128b", "h128b"),
                        ("h64b", "h64b"), ("bqk", "bqk"), ("bo", "bo"), ("e4", "e4"), ("e4b", "e4b"),
                        ("h128f", "h128f")):
            ctx.inc(s.dma_start(C[nm][:, :], d[src][:, :]), "SH", 16)
        ctx.inc(s.dma_start(lr[:, :], d["lutrow"][:, :]), "SH", 16)
        ctx.inc(s.dma_start(lv[:, :], d["lvrow"][:, :]), "SH", 16)
        ctx.inc(s.dma_start(bwt[:, :], d["bwt"][:, :]), "SH", 16)
        ctx.inc(s.dma_start(mnr[:, :], d["mnrows"][:, :]), "SH", 16)

    @block.gpsimd
    def _(g):
        ctx.waits(g, entry)
        ctx.inc(g.dma_start(C["bv_bf"][:, :], d["bv"][:, :]), "SW", 16)
    sh_all = shb + 208

    @block.vector
    def _(v):
        ctx.waits(v, entry)
        v.memset(C["ones1f"][:, :], 1.0)
        v.memset(C["ones1b"][:, :], 1.0)
        v.memset(C["onescb"][:, :], 1.0)
        v.memset(C["onescf"][:, :], 1.0)
        for nm, val in (("eps5", 1e-5), ("eps6", 1e-6), ("invD", 1.0 / D),
                        ("ninvD", -1.0 / D), ("negone", -1.0), ("nRS", -1.02 * RS128)):
            v.memset(C[nm][:, :], val)
        ctx.inc(v.nop(), "VE")
        v.wait_ge(ctx.sems["SH"].h, sh_all)
        ctx.inc(v.reduce_max(t16[:, 0:1], bwt[:, :], axis=AX.X), "VE")
        v.wait_ge(ctx.sems["VE"].h, veb + 2)
        ctx.inc(v.tensor_scalar_mul(t16[:, 0:1], t16[:, 0:1], C["negone"][0:16, 0:1]), "VE")
        v.wait_ge(ctx.sems["AC"].h, acb + 1)
        ctx.inc(v.reduce_sum(t16[:, 0:1], sm[:, :], axis=AX.X), "VE")
        v.wait_ge(ctx.sems["VE"].h, veb + 4)
        ctx.guard(v, v.reciprocal(t16[:, 0:1], t16[:, 0:1]))
        ctx.inc(v.tensor_scalar_mul(sm[:, :], sm[:, :], t16[:, 0:1]), "VE")
        ctx.inc(v.reduce_sum(t4[:, 0:1], mnr[:, :], axis=AX.X), "VE")
        v.wait_ge(ctx.sems["VE"].h, veb + 6)
        ctx.inc(v.tensor_scalar(t4[:, 0:1], t4[:, 0:1], C["eps6"][0:4, 0:1], None, ALU.is_ge), "VE")
        v.wait_ge(ctx.sems["PE"].h, peb + 1)
        v.tensor_copy(C["lut_rep"][:, :], ps_a[:, 0:16])
        ctx.inc(v.tensor_copy(C["lvals"][:, :], ps_b[:, 0:16]), "VE")
        v.wait_ge(ctx.sems["PE"].h, peb + 2)
        ctx.inc(v.tensor_copy(mrow[:, :], ps_a[0:1, 16:20]), "VE")
        v.wait_ge(ctx.sems["PE"].h, peb + 3)
        ctx.guard(v, v.tensor_copy(mrep[:, :], ps_b[0:16, 16:20]))
        ctx.inc(v.tensor_mul(wp[:, :], sm[:, :], mrep[:, :]), "VE")
        v.wait_ge(ctx.sems["PE"].h, peb + 4)
        ctx.inc(v.tensor_copy(C["wpT"][:, :], ps_a[0:4, 32:48]), "VE")

    @block.scalar
    def _(sc):
        ctx.waits(sc, entry)
        sc.wait_ge(ctx.sems["VE"].h, veb + 3)
        ctx.inc(sc.activation(sm[:, :], bwt[:, :], AF.Exp, bias=t16[0:16, 0:1]), "AC")

    @block.tensor
    def _(t):
        ctx.waits(t, entry)
        t.wait_ge(ctx.sems["VE"].h, veb + 1)
        t.wait_ge(ctx.sems["SH"].h, sh_all)
        t.matmul(ps_a[:, 0:16], C["ones1f"][:, :], lr[:, :], start=True, stop=True)
        ctx.inc(t.matmul(ps_b[:, 0:16], C["ones1f"][:, :], lv[:, :], start=True, stop=True), "PE")
        t.wait_ge(ctx.sems["VE"].h, veb + 7)
        ctx.inc(t.transpose(ps_a[0:1, 16:20], t4[:, 0:1], C["ident32"][0:4, 0:4]), "PE")
        t.wait_ge(ctx.sems["VE"].h, veb + 9)
        ctx.inc(t.matmul(ps_b[0:16, 16:20], C["ones1f"][0:1, 0:16], mrow[:, :], start=True, stop=True), "PE")
        t.wait_ge(ctx.sems["VE"].h, veb + 10)
        ctx.inc(t.transpose(ps_a[0:4, 32:48], wp[:, :], C["ident32"][0:16, 0:16]), "PE")

    blk_guard.__exit__(None, None, None)
    ist.close()
    return ctx.snap()


def emit_proj(ctx, entry, wname, aname, oname, K, M, evac, bias=None, tag=""):
    nc, d, C = ctx.nc, ctx.dram, ctx.consts
    NT = T // 512
    NQ = NT // 4
    with ExitStack() as st:
        act = st.enter_context(nc.sbuf_tensor(f"p{tag}_act", [128, K * T], BF16))
        wbuf = [st.enter_context(nc.sbuf_tensor(f"p{tag}_w{i}", [128, K * 128], BF16)) for i in range(2)]
        stg = [st.enter_context(nc.sbuf_tensor(f"p{tag}_s{i}", [128, T], BF16)) for i in range(2)]
        ps = [st.enter_context(nc.psum_tensor(f"p{tag}_ps{i}", [128, 512], F32)) for i in range(8)]
        if evac == "elu1":
            t1 = [st.enter_context(nc.sbuf_tensor(f"p{tag}_t1{i}", [128, 512], F32)) for i in range(2)]
            t2 = [st.enter_context(nc.sbuf_tensor(f"p{tag}_t2{i}", [128, 512], F32)) for i in range(2)]
        block = st.enter_context(nc.Block(f"proj{tag}"))
        shb, swb, peb, acb, veb, ssb = (ctx.sems[k].v for k in ("SH", "SW", "PE", "AC", "VE", "SS"))
        NGRP = max(1, K // 4)
        sh_act = shb + 16 * NGRP
        sw = lambda m: swb + 16 * (m + 1)
        pe = lambda m, tq: peb + m * NQ + tq + 1
        ac = lambda m, tq, j: acb + (m * NQ + tq) * 4 + j + 1
        ve = lambda m, tq, j: veb + (m * NQ + tq) * 4 + j + 1
        ss = lambda m: ssb + 16 * (m + 1)
        ev = ve if evac == "elu1" else ac
        evs = "VE" if evac == "elu1" else "AC"

        @block.sync
        def _(s):
            ctx.waits(s, entry)
            a3 = d[aname][:, :].rearrange("(kc p) t -> p kc t", p=128)
            o3 = act[:, :].rearrange("p (kc t) -> p kc t", t=T)
            for gi in range(NGRP):
                k0, k1 = gi * 4, min(K, gi * 4 + 4)
                ctx.inc(s.dma_start(o3[:, k0:k1, :], a3[:, k0:k1, :]), "SH", 16)
            for m in range(M):
                s.wait_ge(ctx.sems[evs].h, ev(m, NQ - 1, 3))
                ctx.inc(s.dma_start(d[oname][m * 128:(m + 1) * 128, :], stg[m % 2][:, :]), "SS", 16)

        @block.gpsimd
        def _(g):
            ctx.waits(g, entry)
            for m in range(M):
                if m >= 2:
                    g.wait_ge(ctx.sems["PE"].h, pe(m - 2, NQ - 1))
                win = d[wname][:, m * 128:(m + 1) * 128].rearrange("(kc p) m -> p kc m", p=128)
                wout = wbuf[m % 2][:, :].rearrange("p (kc m) -> p kc m", m=128)
                ctx.inc(g.dma_start(wout, win), "SW", 16)

        @block.tensor
        def _(t):
            ctx.waits(t, entry)
            t.wait_ge(ctx.sems["SH"].h, sh_act)
            for m in range(M):
                t.wait_ge(ctx.sems["SW"].h, sw(m))
                for tq in range(NQ):
                    idx = m * NQ + tq
                    if idx >= 2:
                        pm, pq = divmod(idx - 2, NQ)
                        t.wait_ge(ctx.sems[evs].h, ev(pm, pq, 3))
                    bs = (idx % 2) * 4
                    for kc in range(K):
                        for j in range(4):
                            inst = t.matmul(ps[bs + j][:, :], wbuf[m % 2][:, kc * 128:(kc + 1) * 128],
                                            act[:, kc * T + (tq * 4 + j) * 512: kc * T + (tq * 4 + j) * 512 + 512],
                                            start=(kc == 0), stop=(kc == K - 1))
                    ctx.inc(inst, "PE")

        @block.scalar
        def _(sc):
            ctx.waits(sc, entry)
            for m in range(M):
                for tq in range(NQ):
                    idx = m * NQ + tq
                    sc.wait_ge(ctx.sems["PE"].h, pe(m, tq))
                    if tq == 0 and m >= 2:
                        sc.wait_ge(ctx.sems["SS"].h, ss(m - 2))
                    bs = (idx % 2) * 4
                    for j in range(4):
                        tt = tq * 4 + j
                        dst = stg[m % 2][:, tt * 512: tt * 512 + 512]
                        if evac == "copy":
                            ctx.inc(sc.activation(dst, ps[bs + j][:, :], AF.Copy), "AC")
                        elif evac == "bias":
                            ctx.inc(sc.activation(dst, ps[bs + j][:, :], AF.Identity, bias=C[bias][:, m:m + 1]), "AC")
                        elif evac == "silu":
                            ctx.inc(sc.activation(dst, ps[bs + j][:, :], AF.Silu), "AC")
                        elif evac == "elu1":
                            if idx >= 1 or j >= 2:
                                pj = (idx * 4 + j - 2)
                                sc.wait_ge(ctx.sems["VE"].h, veb + pj + 1)
                            sc.activation(t1[j % 2][:, :], ps[bs + j][:, :], AF.Relu)
                            sc.activation(t2[j % 2][:, :], ps[bs + j][:, :], AF.Relu, scale=-1.0)
                            ctx.inc(sc.activation(t2[j % 2][:, :], t2[j % 2][:, :], AF.Exp, scale=-1.0), "AC")

        if evac == "elu1":
            @block.vector
            def _(v):
                ctx.waits(v, entry)
                for m in range(M):
                    for tq in range(NQ):
                        idx = m * NQ + tq
                        for j in range(4):
                            tt = tq * 4 + j
                            v.wait_ge(ctx.sems["AC"].h, ac(m, tq, j))
                            if tq == 0 and j == 0 and m >= 2:
                                v.wait_ge(ctx.sems["SS"].h, ss(m - 2))
                            dst = stg[m % 2][:, tt * 512: tt * 512 + 512]
                            ctx.inc(v.tensor_add(dst, t1[j % 2][:, :], t2[j % 2][:, :]), "VE")

    return ctx.snap()


def emit_vproj(ctx, entry, tag="v"):
    nc, d, C = ctx.nc, ctx.dram, ctx.consts
    with ExitStack() as st:
        act = st.enter_context(nc.sbuf_tensor(f"{tag}_act", [128, NKC * T], BF16))
        wres = [st.enter_context(nc.sbuf_tensor(f"{tag}_w{i}", [128, NKC * 512], BF16)) for i in range(2)]
        stg = [st.enter_context(nc.sbuf_tensor(f"{tag}_s{i}", [128, 512], BF16)) for i in range(2)]
        ps = [st.enter_context(nc.psum_tensor(f"{tag}_ps{i}", [128, 512], F32)) for i in range(2)]
        block = st.enter_context(nc.Block(f"vproj{tag}"))
        shb, swb, peb, acb, ssb = (ctx.sems[k].v for k in ("SH", "SW", "PE", "AC", "SS"))
        sh_act = shb + 16 * 4
        sw = lambda oc: swb + 16 * (oc + 1)
        pe = lambda idx: peb + idx + 1
        ac = lambda idx: acb + idx + 1
        ss = lambda idx: ssb + 16 * (idx + 1)

        @block.sync
        def _(s):
            ctx.waits(s, entry)
            a3 = d["xn"][:, :].rearrange("(kc p) t -> p kc t", p=128)
            o3 = act[:, :].rearrange("p (kc t) -> p kc t", t=T)
            for gi in range(4):
                ctx.inc(s.dma_start(o3[:, gi * 4:gi * 4 + 4, :], a3[:, gi * 4:gi * 4 + 4, :]), "SH", 16)
            for oc in range(4):
                for m in range(32):
                    idx = oc * 32 + m
                    s.wait_ge(ctx.sems["AC"].h, ac(idx))
                    ctx.inc(s.dma_start(d["vv"][m * 128:(m + 1) * 128, oc * 512:(oc + 1) * 512],
                                        stg[idx % 2][:, :]), "SS", 16)

        @block.gpsimd
        def _(g):
            ctx.waits(g, entry)
            for oc in range(4):
                if oc >= 2:
                    g.wait_ge(ctx.sems["PE"].h, pe((oc - 1) * 32 - 1))
                win = d["wvT"][:, oc * 512:(oc + 1) * 512].rearrange("(kc p) o -> p kc o", p=128)
                wout = wres[oc % 2][:, :].rearrange("p (kc o) -> p kc o", o=512)
                ctx.inc(g.dma_start(wout, win), "SW", 16)

        @block.tensor
        def _(t):
            ctx.waits(t, entry)
            t.wait_ge(ctx.sems["SH"].h, sh_act)
            for oc in range(4):
                t.wait_ge(ctx.sems["SW"].h, sw(oc))
                for m in range(32):
                    idx = oc * 32 + m
                    if idx >= 2:
                        t.wait_ge(ctx.sems["AC"].h, ac(idx - 2))
                    p = ps[idx % 2]
                    for kc in range(NKC):
                        t.matmul(p[:, :], act[:, kc * T + m * 128: kc * T + m * 128 + 128],
                                 wres[oc % 2][:, kc * 512:(kc + 1) * 512],
                                 start=(kc == 0), stop=False)
                    inst = t.matmul(p[:, :], C["ones1b"][:, :],
                                    C["bv_bf"][:, oc * 512:oc * 512 + 512], start=False, stop=True)
                    ctx.inc(inst, "PE")

        @block.scalar
        def _(sc):
            ctx.waits(sc, entry)
            for oc in range(4):
                for m in range(32):
                    idx = oc * 32 + m
                    sc.wait_ge(ctx.sems["PE"].h, pe(idx))
                    if idx >= 2:
                        sc.wait_ge(ctx.sems["SS"].h, ss(idx - 2))
                    ctx.inc(sc.activation(stg[idx % 2][:, :], ps[idx % 2][:, :], AF.Copy), "AC")

    return ctx.snap()


def emit_combine_ln(ctx, entry, base_name, r_name, xout_name, ln_out, tag=""):
    nc, d, C = ctx.nc, ctx.dram, ctx.consts
    with ExitStack() as st:
        xa = [st.enter_context(nc.sbuf_tensor(f"c{tag}_x{i}", [128, D], F32)) for i in range(2)]
        ra = [st.enter_context(nc.sbuf_tensor(f"c{tag}_r{i}", [128, D], F32)) for i in range(2)] if r_name else None
        xsq = st.enter_context(nc.sbuf_tensor(f"c{tag}_sq", [128, D], F32))
        sv = st.enter_context(nc.sbuf_tensor(f"c{tag}_sv", [128, 8], F32))
        xnb = [st.enter_context(nc.sbuf_tensor(f"c{tag}_n{i}", [128, D], BF16)) for i in range(2)]
        stg = [st.enter_context(nc.sbuf_tensor(f"c{tag}_t{i}", [128, D], BF16)) for i in range(2)]
        pst = [st.enter_context(nc.psum_tensor(f"c{tag}_p{i}", [128, D], BF16)) for i in range(2)]
        block = st.enter_context(nc.Block(f"comb{tag}"))
        shb, swb, peb, acb, veb, ssb = (ctx.sems[k].v for k in ("SH", "SW", "PE", "AC", "VE", "SS"))
        sh = lambda tt: shb + 16 * (tt + 1)
        sw = lambda tt: swb + 16 * (tt + 1)
        ac = lambda tt, j: acb + tt * 2 + j + 1
        pe = lambda tt: peb + tt + 1
        nve = 6 if ln_out else (2 if r_name else 1)
        ve = lambda tt, j: veb + tt * nve + j + 1
        ss_per = (1 if xout_name else 0) + (1 if ln_out else 0)
        ss_x = lambda tt: ssb + 16 * (tt * ss_per + 1)
        ss_xn = lambda tt: ssb + 16 * (tt * ss_per + ss_per)

        def emit_stores(s, tt):
            if xout_name:
                s.wait_ge(ctx.sems["VE"].h, ve(tt, 0))
                ctx.inc(s.dma_start(d[xout_name][tt * 128:(tt + 1) * 128, :], xa[tt % 2][:, :]), "SS", 16)
            if ln_out:
                s.wait_ge(ctx.sems["VE"].h, ve(tt, 5))
                o3 = d[ln_out][:, tt * 128:(tt + 1) * 128].rearrange("(c p) t -> p c t", p=128)
                i3 = stg[tt % 2][:, :].rearrange("p (c t) -> p c t", t=128)
                ctx.inc(s.dma_start(o3, i3), "SS", 16)

        @block.sync
        def _(s):
            ctx.waits(s, entry)
            for tt in range(32):
                if tt >= 2:
                    s.wait_ge(ctx.sems["VE"].h, ve(tt - 2, nve - 1))
                    if xout_name:
                        s.wait_ge(ctx.sems["SS"].h, ss_x(tt - 2))
                ctx.inc(s.dma_start(xa[tt % 2][:, :], d[base_name][tt * 128:(tt + 1) * 128, :]), "SH", 16)
                if tt >= 1:
                    emit_stores(s, tt - 1)
            emit_stores(s, 31)

        if r_name:
            @block.gpsimd
            def _(g):
                ctx.waits(g, entry)
                for tt in range(32):
                    if tt >= 2:
                        g.wait_ge(ctx.sems["VE"].h, ve(tt - 2, 0))
                    ctx.inc(g.dma_start(ra[tt % 2][:, :], d[r_name][tt * 128:(tt + 1) * 128, :]), "SW", 16)

        @block.vector
        def _(v):
            ctx.waits(v, entry)
            for tt in range(32):
                v.wait_ge(ctx.sems["SH"].h, sh(tt))
                if r_name:
                    v.wait_ge(ctx.sems["SW"].h, sw(tt))
                    ctx.inc(v.tensor_add(xa[tt % 2][:, :], xa[tt % 2][:, :], ra[tt % 2][:, :]), "VE")
                else:
                    ctx.inc(v.nop(), "VE")
                if not ln_out:
                    if nve == 2:
                        ctx.inc(v.nop(), "VE")
                    continue
                ctx.inc(v.reduce_sum(sv[:, 0:1], xa[tt % 2][:, :], axis=AX.X), "VE")
                v.wait_ge(ctx.sems["AC"].h, ac(tt, 0))
                ctx.inc(v.reduce_sum(sv[:, 1:2], xsq[:, :], axis=AX.X), "VE")
                v.wait_ge(ctx.sems["VE"].h, ve(tt, 2))
                i1 = v.tensor_scalar_mul(sv[:, 2:3], sv[:, 0:1], C["ninvD"][:, 0:1])
                i2 = v.tensor_scalar_mul(sv[:, 4:5], sv[:, 1:2], C["invD"][:, 0:1])
                ctx.guard(v, i1, i2)
                ctx.guard(v, v.tensor_mul(sv[:, 3:4], sv[:, 2:3], sv[:, 2:3]))
                ctx.inc(v.tensor_sub(sv[:, 4:5], sv[:, 4:5], sv[:, 3:4]), "VE")
                v.wait_ge(ctx.sems["AC"].h, ac(tt, 1))
                ctx.inc(v.tensor_scalar(xnb[tt % 2][:, :], xa[tt % 2][:, :], sv[:, 2:3], sv[:, 6:7],
                                        ALU.add, ALU.mult), "VE")
                v.wait_ge(ctx.sems["PE"].h, pe(tt))
                if tt >= 2:
                    v.wait_ge(ctx.sems["SS"].h, ss_xn(tt - 2))
                ctx.inc(v.tensor_copy(stg[tt % 2][:, :], pst[tt % 2][:, :]), "VE")

        @block.scalar
        def _(sc):
            ctx.waits(sc, entry)
            if not ln_out:
                return
            for tt in range(32):
                v0 = ve(tt, 0)
                sc.wait_ge(ctx.sems["VE"].h, v0) if r_name else sc.wait_ge(ctx.sems["SH"].h, sh(tt))
                ctx.inc(sc.activation(xsq[:, :], xa[tt % 2][:, :], AF.Square), "AC")
                sc.wait_ge(ctx.sems["VE"].h, ve(tt, 3))
                sc.activation(sv[:, 5:6], sv[:, 4:5], AF.Ln, bias=C["eps5"][:, 0:1])
                ctx.inc(sc.activation(sv[:, 6:7], sv[:, 5:6], AF.Exp, scale=-0.5), "AC")

        if ln_out:
            @block.tensor
            def _(t):
                ctx.waits(t, entry)
                for tt in range(32):
                    t.wait_ge(ctx.sems["VE"].h, ve(tt, 4))
                    if tt >= 2:
                        t.wait_ge(ctx.sems["VE"].h, ve(tt - 2, 5))
                    for c in range(16):
                        inst = t.transpose(pst[tt % 2][:, c * 128:(c + 1) * 128],
                                           xnb[tt % 2][:, c * 128:(c + 1) * 128], C["identb"][:, :])
                    ctx.inc(inst, "PE")

    return ctx.snap()


def emit_attn(ctx, entry):
    nc, d, C = ctx.nc, ctx.dram, ctx.consts
    with ExitStack() as st:
        qh = [st.enter_context(nc.sbuf_tensor(f"at_q{i}", [128, T], BF16)) for i in range(2)]
        kh = [st.enter_context(nc.sbuf_tensor(f"at_k{i}", [128, T], BF16)) for i in range(2)]
        vt = [st.enter_context(nc.sbuf_tensor(f"at_v{i}", [128, T], BF16)) for i in range(2)]
        pbuf = [st.enter_context(nc.sbuf_tensor(f"at_p{i}", [128, 32 * 512], BF16)) for i in range(2)]
        ostage = [st.enter_context(nc.sbuf_tensor(f"at_o{i}", [128, T], BF16)) for i in range(2)]
        oraw = [st.enter_context(nc.sbuf_tensor(f"at_or{i}", [128, 512], F32)) for i in range(2)]
        sqb = [st.enter_context(nc.sbuf_tensor(f"at_sq{i}", [128, 512], BF16)) for i in range(2)]
        qn = st.enter_context(nc.sbuf_tensor("at_qn", [1, T], F32))
        kn = st.enter_context(nc.sbuf_tensor("at_kn", [1, T], F32))
        cm = st.enter_context(nc.sbuf_tensor("at_cm", [1, 8], F32))
        rs = st.enter_context(nc.sbuf_tensor("at_rs", [1, 512], F32))
        rep_sb = st.enter_context(nc.sbuf_tensor("at_rep", [128, 512], F32))
        negc = st.enter_context(nc.sbuf_tensor("at_negc", [128, 2], F32))
        ps_s = [st.enter_context(nc.psum_tensor(f"at_pss{i}", [128, 1024], F32)) for i in range(2)]
        ps_o = st.enter_context(nc.psum_tensor("at_pso", [128, 512], F32))
        ps_sum = [st.enter_context(nc.psum_tensor(f"at_psum{i}", [1, 512], F32)) for i in range(2)]
        ps_rep = st.enter_context(nc.psum_tensor("at_prep", [128, 512], F32))
        block = st.enter_context(nc.Block("attn"))
        shb, peb, acb, veb, ssb = (ctx.sems[k].v for k in ("SH", "PE", "AC", "VE", "SS"))
        NAC, NVE = 146, 44
        ac_base = lambda h: acb + h * NAC
        ac_sq = lambda h, i: ac_base(h) + i + 1
        ac_ln = lambda h: ac_base(h) + 17
        ac_expc = lambda h: ac_base(h) + 18
        ac_exp = lambda h, qg, j: ac_base(h) + 18 + qg * 16 + j + 1
        ve_base = lambda h: veb + h * NVE
        ve_evc = lambda h, i: ve_base(h) + i + 1
        ve_max = lambda h: ve_base(h) + 17
        ve_sum = lambda h: ve_base(h) + 18
        ve_ncp = lambda h: ve_base(h) + 19
        ve_ncc = lambda h: ve_base(h) + 20
        ve_or = lambda h, qg: ve_base(h) + 20 + qg * 3 + 1
        ve_rcp = lambda h, qg: ve_base(h) + 20 + qg * 3 + 2
        ve_ev = lambda h, qg: ve_base(h) + 20 + qg * 3 + 3
        sh = lambda h: shb + 48 * (h + 1)
        ss = lambda h: ssb + 16 * (h + 1)
        pe_c, pe_st, pe_pv, pe_rep = {}, {}, {}, {}

        @block.tensor
        def _(t):
            ctx.waits(t, entry)
            for h in range(NH):
                for i in range(16):
                    t.wait_ge(ctx.sems["AC"].h, ac_sq(h, i))
                    if i >= 4:
                        t.wait_ge(ctx.sems["VE"].h, ve_evc(h, i - 4))
                    elif h >= 1:
                        t.wait_ge(ctx.sems["AC"].h, ac_base(h - 1) + NAC)
                    pe_c[(h, i)] = ctx.inc(t.matmul(
                        ps_s[(i // 2) % 2][0:1, (i % 2) * 512:(i % 2) * 512 + 512],
                        C["onescb"][:, :], sqb[i % 2][:, :], start=True, stop=True), "PE")
                t.wait_ge(ctx.sems["VE"].h, ve_ncp(h))
                if h >= 1:
                    t.wait_ge(ctx.sems["VE"].h, ve_base(h - 1) + NVE)
                pe_c[(h, 16)] = ctx.inc(t.matmul(ps_o[:, 0:1], C["ones1f"][:, :], cm[0:1, 6:7],
                                                 start=True, stop=True), "PE")

                def emit_st1(qg, kc):
                    idx = qg * 32 + kc
                    if qg == 0 and kc == 0:
                        t.wait_ge(ctx.sems["VE"].h, ve_base(h) + 16)
                    if idx >= 4 and idx % 2 == 0:
                        pqg, pj = divmod((idx - 4) // 2, 16)
                        t.wait_ge(ctx.sems["AC"].h, ac_exp(h, pqg, pj))
                    pe_st[(h, qg, kc)] = ctx.inc(
                        t.matmul(ps_s[(idx // 2) % 2][:, (idx % 2) * 512:(idx % 2) * 512 + 512],
                                 kh[h % 2][:, kc * 128:(kc + 1) * 128],
                                 qh[h % 2][:, qg * 512:(qg + 1) * 512],
                                 start=True, stop=True), "PE")

                def emit_rep(qg):
                    t.wait_ge(ctx.sems["VE"].h, ve_rcp(h, qg))
                    pe_rep[(h, qg)] = ctx.inc(t.matmul(ps_rep[:, :], C["ones1f"][:, :], rs[0:1, :],
                                                       start=True, stop=True), "PE")

                def emit_block(qg):
                    pv = qg - 1
                    if qg >= 2:
                        emit_rep(qg - 2)
                    t.wait_ge(ctx.sems["AC"].h, ac_exp(h, pv, 15))
                    if pv >= 2:
                        t.wait_ge(ctx.sems["VE"].h, ve_or(h, pv - 2))
                    elif h >= 1 and pv < 2:
                        t.wait_ge(ctx.sems["VE"].h, ve_or(h - 1, 6 + pv))
                    for kc in range(32):
                        t.matmul(ps_o[:, :], vt[h % 2][:, kc * 128:(kc + 1) * 128],
                                 pbuf[pv % 2][:, kc * 512:(kc + 1) * 512],
                                 start=(kc == 0), stop=(kc == 31))
                        inst = t.matmul(ps_sum[pv % 2][0:1, :], C["onescb"][:, :],
                                        pbuf[pv % 2][:, kc * 512:(kc + 1) * 512],
                                        start=(kc == 0), stop=(kc == 31))
                        if qg < 8:
                            emit_st1(qg, kc)
                    pe_pv[(h, pv)] = ctx.inc(inst, "PE")

                emit_st1(0, 0)
                for kc in range(1, 32):
                    emit_st1(0, kc)
                for qg in range(1, 9):
                    emit_block(qg)
                emit_rep(7)

        @block.sync
        def _(s):
            ctx.waits(s, entry)
            for h in range(NH):
                if h >= 2:
                    s.wait_ge(ctx.sems["PE"].h, pe_c[(h - 1, 0)] - 1)
                ctx.inc(s.dma_start(qh[h % 2][:, :], d["qk"][h * 128:(h + 1) * 128, :]), "SH", 16)
                ctx.inc(s.dma_start(kh[h % 2][:, :], d["qk"][D + h * 128:D + (h + 1) * 128, :]), "SH", 16)
                v3 = d["vv"][:, h * 128:(h + 1) * 128].rearrange("(kc p) e -> p kc e", p=128)
                o3 = vt[h % 2][:, :].rearrange("p (kc e) -> p kc e", e=128)
                ctx.inc(s.dma_start(o3, v3), "SH", 16)
                if h >= 1:
                    s.wait_ge(ctx.sems["VE"].h, ve_ev(h - 1, 7))
                    ctx.inc(s.dma_start(d["oo"][(h - 1) * 128:h * 128, :], ostage[(h - 1) % 2][:, :]), "SS", 16)
            s.wait_ge(ctx.sems["VE"].h, ve_ev(NH - 1, 7))
            ctx.inc(s.dma_start(d["oo"][(NH - 1) * 128:NH * 128, :], ostage[(NH - 1) % 2][:, :]), "SS", 16)

        @block.scalar
        def _(sc):
            ctx.waits(sc, entry)
            for h in range(NH):
                sc.wait_ge(ctx.sems["SH"].h, sh(h))
                for i in range(16):
                    src2 = qh[h % 2] if i < 8 else kh[h % 2]
                    g = i % 8
                    if i >= 2:
                        sc.wait_ge(ctx.sems["PE"].h, pe_c[(h, i - 2)])
                    elif h >= 1:
                        sc.wait_ge(ctx.sems["PE"].h, pe_c[(h - 1, 14 + i)])
                    ctx.inc(sc.activation(sqb[i % 2][:, :], src2[:, g * 512:(g + 1) * 512], AF.Square), "AC")
                sc.wait_ge(ctx.sems["VE"].h, ve_max(h))
                sc.activation(cm[0:1, 2:3], cm[0:1, 0:1], AF.Ln)
                ctx.inc(sc.activation(cm[0:1, 3:4], cm[0:1, 1:2], AF.Ln), "AC")
                sc.wait_ge(ctx.sems["VE"].h, ve_sum(h))
                ctx.inc(sc.activation(cm[0:1, 5:6], cm[0:1, 4:5], AF.Exp, scale=0.5), "AC")
                sc.wait_ge(ctx.sems["VE"].h, ve_ncc(h))
                for qg in range(8):
                    if qg >= 2:
                        sc.wait_ge(ctx.sems["PE"].h, pe_pv[(h, qg - 2)])
                    elif h >= 1:
                        sc.wait_ge(ctx.sems["PE"].h, pe_pv[(h - 1, 6 + qg)])
                    for j in range(16):
                        idx = qg * 32 + 2 * j
                        sc.wait_ge(ctx.sems["PE"].h, pe_st[(h, qg, 2 * j + 1)])
                        ctx.inc(sc.activation(pbuf[qg % 2][:, j * 1024:(j + 1) * 1024],
                                              ps_s[(idx // 2) % 2][:, :], AF.Exp,
                                              bias=negc[:, h % 2:h % 2 + 1], scale=RS128), "AC")

        @block.vector
        def _(v):
            ctx.waits(v, entry)
            for h in range(NH):
                for i in range(16):
                    dst = qn if i < 8 else kn
                    g = i % 8
                    v.wait_ge(ctx.sems["PE"].h, pe_c[(h, i)])
                    ctx.inc(v.tensor_copy(dst[0:1, g * 512:(g + 1) * 512],
                                          ps_s[(i // 2) % 2][0:1, (i % 2) * 512:(i % 2) * 512 + 512]), "VE")
                ctx.guard(v, v.reduce_max(cm[0:1, 0:1], qn[0:1, :], axis=AX.X))
                ctx.guard(v, v.reduce_max(cm[0:1, 1:2], kn[0:1, :], axis=AX.X))
                ctx.inc(v.nop(), "VE")
                v.wait_ge(ctx.sems["AC"].h, ac_ln(h))
                ctx.guard(v, v.tensor_add(cm[0:1, 4:5], cm[0:1, 2:3], cm[0:1, 3:4]))
                ctx.inc(v.nop(), "VE")
                v.wait_ge(ctx.sems["AC"].h, ac_expc(h))
                ctx.guard(v, v.tensor_scalar_mul(cm[0:1, 6:7], cm[0:1, 5:6], C["nRS"][0:1, 0:1]))
                ctx.inc(v.nop(), "VE")
                v.wait_ge(ctx.sems["PE"].h, pe_c[(h, 16)])
                ctx.guard(v, v.tensor_copy(negc[:, h % 2:h % 2 + 1], ps_o[:, 0:1]))
                ctx.inc(v.nop(), "VE")
                for qg in range(8):
                    v.wait_ge(ctx.sems["PE"].h, pe_pv[(h, qg)])
                    ctx.inc(v.tensor_copy(oraw[qg % 2][:, :], ps_o[:, :]), "VE")
                    ctx.guard(v, v.reciprocal(rs[0:1, :], ps_sum[qg % 2][0:1, :]))
                    ctx.inc(v.nop(), "VE")
                    v.wait_ge(ctx.sems["PE"].h, pe_rep[(h, qg)])
                    ctx.guard(v, v.tensor_copy(rep_sb[:, :], ps_rep[:, :]))
                    if qg == 0 and h >= 2:
                        v.wait_ge(ctx.sems["SS"].h, ss(h - 2))
                    ctx.inc(v.tensor_mul(ostage[h % 2][:, qg * 512:(qg + 1) * 512],
                                         oraw[qg % 2][:, :], rep_sb[:, :]), "VE")

    assert ctx.sems["AC"].v == acb + NH * NAC
    assert ctx.sems["VE"].v == veb + NH * NVE
    return ctx.snap()


def emit_w2x(ctx, entry):
    nc, d, C = ctx.nc, ctx.dram, ctx.consts
    with ExitStack() as st:
        chunk = [st.enter_context(nc.sbuf_tensor(f"wx_c{i}", [128, D], F32)) for i in range(2)]
        stg = [st.enter_context(nc.sbuf_tensor(f"wx_s{i}", [128, D], BF16)) for i in range(2)]
        ps = [st.enter_context(nc.psum_tensor(f"wx_p{i}", [128, 512], F32)) for i in range(2)]
        t2in = [st.enter_context(nc.sbuf_tensor(f"wx_i{i}", [64, 16 * 1024], BF16)) for i in range(2)]
        t2out = [st.enter_context(nc.sbuf_tensor(f"wx_o{i}", [64, 16 * 1024], BF16)) for i in range(2)]
        ps2 = [st.enter_context(nc.psum_tensor(f"wx_q{i}", [64, 512], F32)) for i in range(2)]
        block = st.enter_context(nc.Block("w2x"))
        shb, swb, peb, acb, veb, ssb = (ctx.sems[k].v for k in ("SH", "SW", "PE", "AC", "VE", "SS"))
        sh1 = lambda a: shb + 16 * (a + 1)
        pe1 = lambda a, j: peb + a * 4 + j + 1
        ac1 = lambda a, j: acb + a * 4 + j + 1
        ss1 = lambda a: ssb + 16 * (a + 1)
        NM2 = 32
        NIT2 = 16
        pe2 = lambda bb, j: peb + 256 + bb * NM2 + j + 1
        ac2 = lambda bb, j: acb + 256 + bb * NM2 + j + 1
        sh2 = lambda bb: shb + 16 * (64 + bb + 1)
        ss2 = lambda bb: ssb + 16 * (64 + bb + 1)
        ss1_all = ssb + 16 * 64

        @block.tensor
        def _(t):
            ctx.waits(t, entry)
            for a in range(64):
                t.wait_ge(ctx.sems["SH"].h, sh1(a))
                for j in range(4):
                    idx = a * 4 + j
                    if idx >= 2:
                        t.wait_ge(ctx.sems["AC"].h, ac1(*divmod(idx - 2, 4)))
                    ctx.inc(t.matmul(ps[idx % 2][:, :], C["h128f"][:, :],
                                     chunk[a % 2][:, j * 512:(j + 1) * 512], start=True, stop=True), "PE")
            for bb in range(NIT2):
                t.wait_ge(ctx.sems["SH"].h, sh2(bb))
                for j in range(NM2):
                    idx = bb * NM2 + j
                    if idx >= 2:
                        t.wait_ge(ctx.sems["AC"].h, ac2(*divmod(idx - 2, NM2)))
                    ctx.inc(t.matmul(ps2[idx % 2][:, :], C["h64b"][:, :],
                                     t2in[bb % 2][:, j * 512:(j + 1) * 512], start=True, stop=True), "PE")

        @block.scalar
        def _(sc):
            ctx.waits(sc, entry)
            SQ2 = float(np.sqrt(2.0))
            for a in range(64):
                for j in range(4):
                    idx = a * 4 + j
                    sc.wait_ge(ctx.sems["PE"].h, pe1(a, j))
                    if j == 0 and a >= 2:
                        sc.wait_ge(ctx.sems["SS"].h, ss1(a - 2))
                    ctx.inc(sc.activation(stg[a % 2][:, j * 512:(j + 1) * 512], ps[idx % 2][:, :],
                                          AF.Identity, scale=SQ2), "AC")
            for bb in range(NIT2):
                for j in range(NM2):
                    idx = bb * NM2 + j
                    sc.wait_ge(ctx.sems["PE"].h, pe2(bb, j))
                    if j == 0 and bb >= 2:
                        sc.wait_ge(ctx.sems["SS"].h, ss2(bb - 2))
                    ctx.inc(sc.activation(t2out[bb % 2][:, j * 512:(j + 1) * 512], ps2[idx % 2][:, :],
                                          AF.Identity), "AC")

        @block.sync
        def _(s):
            ctx.waits(s, entry)
            for a in range(64):
                if a >= 2:
                    s.wait_ge(ctx.sems["PE"].h, pe1(a - 2, 3))
                ctx.inc(s.dma_start(chunk[a % 2][:, :], d["w2T"][a * 128:(a + 1) * 128, :]), "SH", 16)
                if a >= 1:
                    s.wait_ge(ctx.sems["AC"].h, ac1(a - 1, 3))
                    ctx.inc(s.dma_start(d["w2x1"][(a - 1) * 128:a * 128, :], stg[(a - 1) % 2][:, :]), "SS", 16)
            s.wait_ge(ctx.sems["AC"].h, ac1(63, 3))
            ctx.inc(s.dma_start(d["w2x1"][63 * 128:64 * 128, :], stg[63 % 2][:, :]), "SS", 16)
            w1v = d["w2x1"][:, :].rearrange("(a b) dd -> a b dd", a=64)
            wov = d["w2x"][:, :].rearrange("(a b) dd -> a b dd", a=64)

            def reg2(bb):
                blk, dh = divmod(bb, 2)
                return slice(blk * 16, (blk + 1) * 16), slice(dh * 1024, (dh + 1) * 1024)

            def store2(bb):
                s.wait_ge(ctx.sems["AC"].h, ac2(bb, NM2 - 1))
                bs, ds = reg2(bb)
                o3 = t2out[bb % 2][:, :].rearrange("p (b dd) -> p b dd", dd=1024)
                ctx.inc(s.dma_start(wov[:, bs, ds], o3), "SS", 16)

            for bb in range(NIT2):
                if bb == 0:
                    s.wait_ge(ctx.sems["SS"].h, ss1_all)
                if bb >= 2:
                    s.wait_ge(ctx.sems["PE"].h, pe2(bb - 2, NM2 - 1))
                bs, ds = reg2(bb)
                i3 = t2in[bb % 2][:, :].rearrange("p (b dd) -> p b dd", dd=1024)
                ctx.inc(s.dma_start(i3, w1v[:, bs, ds]), "SH", 16)
                if bb >= 1:
                    store2(bb - 1)
            store2(NIT2 - 1)

    return ctx.snap()


def emit_ffn13(ctx, entry):
    nc, d, C = ctx.nc, ctx.dram, ctx.consts
    NT = T // 512
    with ExitStack() as st:
        act = st.enter_context(nc.sbuf_tensor("f13_act", [128, NKC * T], BF16))
        wg = [st.enter_context(nc.sbuf_tensor(f"f13_wg{i}", [128, NKC * 128], BF16)) for i in range(2)]
        wl = [st.enter_context(nc.sbuf_tensor(f"f13_wl{i}", [128, NKC * 128], BF16)) for i in range(2)]
        gs = [st.enter_context(nc.sbuf_tensor(f"f13_g{i}", [128, 512], F32)) for i in range(2)]
        stg = [st.enter_context(nc.sbuf_tensor(f"f13_s{i}", [128, T], BF16)) for i in range(2)]
        pg = [st.enter_context(nc.psum_tensor(f"f13_pg{i}", [128, 512], F32)) for i in range(4)]
        pl = [st.enter_context(nc.psum_tensor(f"f13_pl{i}", [128, 512], F32)) for i in range(4)]
        block = st.enter_context(nc.Block("ffn13"))
        shb, swb, peb, acb, veb, ssb = (ctx.sems[k].v for k in ("SH", "SW", "PE", "AC", "VE", "SS"))
        sh_act = shb + 64
        NP = NT // 2
        sw = lambda hc: swb + 32 * (hc + 1)
        pe = lambda pidx, k: peb + pidx * 2 + k + 1
        ac = lambda pidx, j: acb + pidx * 2 + j + 1
        ve = lambda pidx, j: veb + pidx * 2 + j + 1
        ss = lambda hc: ssb + 16 * (hc + 1)

        @block.sync
        def _(s):
            ctx.waits(s, entry)
            a3 = d["xn"][:, :].rearrange("(kc p) t -> p kc t", p=128)
            o3 = act[:, :].rearrange("p (kc t) -> p kc t", t=T)
            for gi in range(4):
                ctx.inc(s.dma_start(o3[:, gi * 4:gi * 4 + 4, :], a3[:, gi * 4:gi * 4 + 4, :]), "SH", 16)
            for hc in range(64):
                s.wait_ge(ctx.sems["VE"].h, ve(hc * NP + NP - 1, 1))
                ctx.inc(s.dma_start(d["hs"][hc * 128:(hc + 1) * 128, :], stg[hc % 2][:, :]), "SS", 16)

        @block.gpsimd
        def _(g):
            ctx.waits(g, entry)
            for hc in range(64):
                if hc >= 2:
                    g.wait_ge(ctx.sems["PE"].h, pe((hc - 2) * NP + NP - 1, 1))
                for wbuf, coff in ((wg, hc * 128), (wl, HID0 + hc * 128)):
                    win = d["w13T"][:, coff:coff + 128].rearrange("(kc p) m -> p kc m", p=128)
                    wout = wbuf[hc % 2][:, :].rearrange("p (kc m) -> p kc m", m=128)
                    ctx.inc(g.dma_start(wout, win), "SW", 16)

        @block.tensor
        def _(t):
            ctx.waits(t, entry)
            t.wait_ge(ctx.sems["SH"].h, sh_act)
            for hc in range(64):
                t.wait_ge(ctx.sems["SW"].h, sw(hc))
                for tp in range(NP):
                    pidx = hc * NP + tp
                    if pidx >= 2:
                        t.wait_ge(ctx.sems["AC"].h, ac(pidx - 2, 1))
                        t.wait_ge(ctx.sems["VE"].h, ve(pidx - 2, 1))
                    bs = (pidx % 2) * 2
                    o0 = (tp * 2) * 512
                    for kc in range(NKC):
                        co = kc * T + o0
                        inst = t.matmul(pg[bs][:, :], wg[hc % 2][:, kc * 128:(kc + 1) * 128],
                                        act[:, co: co + 512], start=(kc == 0), stop=(kc == NKC - 1))
                        inst = t.matmul(pg[bs + 1][:, :], wg[hc % 2][:, kc * 128:(kc + 1) * 128],
                                        act[:, co + 512: co + 1024], start=(kc == 0), stop=(kc == NKC - 1))
                    ctx.inc(inst, "PE")
                    for kc in range(NKC):
                        co = kc * T + o0
                        inst = t.matmul(pl[bs][:, :], wl[hc % 2][:, kc * 128:(kc + 1) * 128],
                                        act[:, co: co + 512], start=(kc == 0), stop=(kc == NKC - 1))
                        inst = t.matmul(pl[bs + 1][:, :], wl[hc % 2][:, kc * 128:(kc + 1) * 128],
                                        act[:, co + 512: co + 1024], start=(kc == 0), stop=(kc == NKC - 1))
                    ctx.inc(inst, "PE")

        @block.scalar
        def _(sc):
            ctx.waits(sc, entry)
            for hc in range(64):
                for tp in range(NP):
                    pidx = hc * NP + tp
                    bs = (pidx % 2) * 2
                    sc.wait_ge(ctx.sems["PE"].h, pe(pidx, 0))
                    for j in range(2):
                        lin = pidx * 2 + j
                        if lin >= 2:
                            sc.wait_ge(ctx.sems["VE"].h, veb + lin - 1)
                        ctx.inc(sc.activation(gs[lin % 2][:, :], pg[bs + j][:, :], AF.Silu), "AC")

        @block.vector
        def _(v):
            ctx.waits(v, entry)
            for hc in range(64):
                for tp in range(NP):
                    pidx = hc * NP + tp
                    bs = (pidx % 2) * 2
                    v.wait_ge(ctx.sems["PE"].h, pe(pidx, 1))
                    for j in range(2):
                        tt = tp * 2 + j
                        lin = pidx * 2 + j
                        v.wait_ge(ctx.sems["AC"].h, ac(pidx, j))
                        if tt == 0 and j == 0 and hc >= 2:
                            v.wait_ge(ctx.sems["SS"].h, ss(hc - 2))
                        ctx.inc(v.tensor_mul(stg[hc % 2][:, tt * 512:(tt + 1) * 512],
                                             pl[bs + j][:, :], gs[lin % 2][:, :]), "VE")

    return ctx.snap()


def emit_ffn2(ctx, entry, wname="w2x", aname="hs", oname="g0", K=64, tag="f2"):
    nc, d, C = ctx.nc, ctx.dram, ctx.consts
    NKH = K
    NLG = (K + 7) // 8
    with ExitStack() as st:
        hbuf = [st.enter_context(nc.sbuf_tensor(f"{tag}_h{i}", [128, NKH * 512], BF16)) for i in range(2)]
        wbuf = [st.enter_context(nc.sbuf_tensor(f"{tag}_w{i}", [128, NKH * 128], BF16)) for i in range(2)]
        stg = [st.enter_context(nc.sbuf_tensor(f"{tag}_s{i}", [128, 512], BF16)) for i in range(2)]
        ps = [st.enter_context(nc.psum_tensor(f"{tag}_p{i}", [128, 512], F32)) for i in range(2)]
        block = st.enter_context(nc.Block(f"ffn2{tag}"))
        shb, swb, peb, acb, ssb = (ctx.sems[k].v for k in ("SH", "SW", "PE", "AC", "SS"))
        sh = lambda tg: shb + 16 * NLG * (tg + 1)
        sw = lambda idx: swb + 16 * (idx + 1)
        pe = lambda idx: peb + idx + 1
        ac = lambda idx: acb + idx + 1
        ss = lambda idx: ssb + 16 * (idx + 1)

        @block.sync
        def _(s):
            ctx.waits(s, entry)
            h3 = d[aname][:, :].rearrange("(kc p) t -> p kc t", p=128)

            def loads(tg):
                o3 = hbuf[tg % 2][:, :].rearrange("p (kc t) -> p kc t", t=512)
                if tg >= 2:
                    s.wait_ge(ctx.sems["PE"].h, pe((tg - 1) * 16 - 1))
                for gi in range(NLG):
                    k0, k1 = gi * 8, min(K, gi * 8 + 8)
                    ctx.inc(s.dma_start(o3[:, k0:k1, :],
                                        h3[:, k0:k1, tg * 512:(tg + 1) * 512]), "SH", 16)

            loads(0)
            for tg in range(8):
                if tg + 1 < 8:
                    loads(tg + 1)
                for m in range(16):
                    idx = tg * 16 + m
                    s.wait_ge(ctx.sems["AC"].h, ac(idx))
                    ctx.inc(s.dma_start(d[oname][m * 128:(m + 1) * 128, tg * 512:(tg + 1) * 512],
                                        stg[idx % 2][:, :]), "SS", 16)

        @block.gpsimd
        def _(g):
            ctx.waits(g, entry)
            for tg in range(8):
                for m in range(16):
                    idx = tg * 16 + m
                    if idx >= 2:
                        g.wait_ge(ctx.sems["PE"].h, pe(idx - 2))
                    win = d[wname][:, m * 128:(m + 1) * 128].rearrange("(kc p) m -> p kc m", p=128)
                    wout = wbuf[idx % 2][:, :].rearrange("p (kc m) -> p kc m", m=128)
                    ctx.inc(g.dma_start(wout, win), "SW", 16)

        @block.tensor
        def _(t):
            ctx.waits(t, entry)
            for tg in range(8):
                t.wait_ge(ctx.sems["SH"].h, sh(tg))
                for m in range(16):
                    idx = tg * 16 + m
                    t.wait_ge(ctx.sems["SW"].h, sw(idx))
                    if idx >= 2:
                        t.wait_ge(ctx.sems["AC"].h, ac(idx - 2))
                    for kc in range(NKH):
                        inst = t.matmul(ps[idx % 2][:, :], wbuf[idx % 2][:, kc * 128:(kc + 1) * 128],
                                        hbuf[tg % 2][:, kc * 512:(kc + 1) * 512],
                                        start=(kc == 0), stop=(kc == NKH - 1))
                    ctx.inc(inst, "PE")

        @block.scalar
        def _(sc):
            ctx.waits(sc, entry)
            for tg in range(8):
                for m in range(16):
                    idx = tg * 16 + m
                    sc.wait_ge(ctx.sems["PE"].h, pe(idx))
                    if idx >= 2:
                        sc.wait_ge(ctx.sems["SS"].h, ss(idx - 2))
                    ctx.inc(sc.activation(stg[idx % 2][:, :], ps[idx % 2][:, :], AF.Silu), "AC")

    return ctx.snap()


def emit_mult_t(ctx, entry, a_name, b_name, out_name, tag=""):
    nc, d, C = ctx.nc, ctx.dram, ctx.consts
    with ExitStack() as st:
        ab = [st.enter_context(nc.sbuf_tensor(f"mt{tag}_a{i}", [128, T], BF16)) for i in range(2)]
        bb = [st.enter_context(nc.sbuf_tensor(f"mt{tag}_b{i}", [128, T], BF16)) for i in range(2)]
        rr = [st.enter_context(nc.sbuf_tensor(f"mt{tag}_r{i}", [128, T], BF16)) for i in range(2)]
        stg = [st.enter_context(nc.sbuf_tensor(f"mt{tag}_s{i}", [128, T], BF16)) for i in range(2)]
        pst = [st.enter_context(nc.psum_tensor(f"mt{tag}_p{i}", [128, D], BF16)) for i in range(2)]
        block = st.enter_context(nc.Block(f"multT{tag}"))
        shb, swb, peb, veb, ssb = (ctx.sems[k].v for k in ("SH", "SW", "PE", "VE", "SS"))
        sh = lambda m: shb + 16 * (m + 1)
        sw = lambda m: swb + 16 * (m + 1)
        pe = lambda m, k: peb + m * 2 + k + 1
        ve = lambda m, j: veb + m * 3 + j + 1
        ss = lambda m: ssb + 16 * (m + 1)

        @block.sync
        def _(s):
            ctx.waits(s, entry)
            for m in range(16):
                if m >= 2:
                    s.wait_ge(ctx.sems["VE"].h, ve(m - 2, 0))
                ctx.inc(s.dma_start(ab[m % 2][:, :], d[a_name][m * 128:(m + 1) * 128, :]), "SH", 16)
                if m >= 1:
                    s.wait_ge(ctx.sems["VE"].h, ve(m - 1, 2))
                    o3 = d[out_name][:, (m - 1) * 128:m * 128].rearrange("(j p) dd -> p j dd", p=128)
                    i3 = stg[(m - 1) % 2][:, :].rearrange("p (j dd) -> p j dd", dd=128)
                    ctx.inc(s.dma_start(o3, i3), "SS", 16)
            s.wait_ge(ctx.sems["VE"].h, ve(15, 2))
            o3 = d[out_name][:, 15 * 128:16 * 128].rearrange("(j p) dd -> p j dd", p=128)
            i3 = stg[15 % 2][:, :].rearrange("p (j dd) -> p j dd", dd=128)
            ctx.inc(s.dma_start(o3, i3), "SS", 16)

        @block.gpsimd
        def _(g):
            ctx.waits(g, entry)
            for m in range(16):
                if m >= 2:
                    g.wait_ge(ctx.sems["VE"].h, ve(m - 2, 0))
                ctx.inc(g.dma_start(bb[m % 2][:, :], d[b_name][m * 128:(m + 1) * 128, :]), "SW", 16)

        @block.vector
        def _(v):
            ctx.waits(v, entry)
            for m in range(16):
                v.wait_ge(ctx.sems["SH"].h, sh(m))
                v.wait_ge(ctx.sems["SW"].h, sw(m))
                if m >= 2:
                    v.wait_ge(ctx.sems["PE"].h, pe(m - 1, 1))
                ctx.inc(v.tensor_mul(rr[m % 2][:, :], ab[m % 2][:, :], bb[m % 2][:, :]), "VE")
                for k in range(2):
                    v.wait_ge(ctx.sems["PE"].h, pe(m, k))
                    if m >= 2 and k == 0:
                        v.wait_ge(ctx.sems["SS"].h, ss(m - 2))
                    ctx.inc(v.tensor_copy(stg[m % 2][:, k * 2048:(k + 1) * 2048], pst[k][:, :]), "VE")

        @block.tensor
        def _(t):
            ctx.waits(t, entry)
            for m in range(16):
                t.wait_ge(ctx.sems["VE"].h, ve(m, 0))
                for k in range(2):
                    if m >= 1:
                        t.wait_ge(ctx.sems["VE"].h, ve(m - 1, 1 + k))
                    for j in range(16):
                        inst = t.transpose(pst[k][:, j * 128:(j + 1) * 128],
                                           rr[m % 2][:, (k * 16 + j) * 128:(k * 16 + j + 1) * 128],
                                           C["identb"][:, :])
                    ctx.inc(inst, "PE")

    return ctx.snap()


def emit_banks(ctx, entry):
    nc, d, C = ctx.nc, ctx.dram, ctx.consts
    with ExitStack() as st:
        sqh = [st.enter_context(nc.sbuf_tensor(f"bk_s{i}", [128, T], BF16)) for i in range(2)]
        membuf = [st.enter_context(nc.sbuf_tensor(f"bk_m{i}", [128, 512], BF16)) for i in range(2)]
        mnb = [st.enter_context(nc.sbuf_tensor(f"bk_n{i}", [128, 4], BF16)) for i in range(2)]
        norms = st.enter_context(nc.sbuf_tensor("bk_no", [4, T], F32))
        rn = st.enter_context(nc.sbuf_tensor("bk_rn", [4, T], F32))
        rnb = st.enter_context(nc.sbuf_tensor("bk_rnb", [4, T], BF16))
        asb = [st.enter_context(nc.sbuf_tensor(f"bk_a{i}", [128, 512], F32)) for i in range(4)]
        accb = st.enter_context(nc.sbuf_tensor("bk_ac", [128, 512], F32))
        t1 = st.enter_context(nc.sbuf_tensor("bk_t1", [128, 512], F32))
        t2 = st.enter_context(nc.sbuf_tensor("bk_t2", [128, 512], F32))
        cstage = [st.enter_context(nc.sbuf_tensor(f"bk_c{i}", [128, T], BF16)) for i in range(2)]
        ps_n = [st.enter_context(nc.psum_tensor(f"bk_pn{i}", [4, 512], F32)) for i in range(2)]
        ps_a = [st.enter_context(nc.psum_tensor(f"bk_pa{i}", [128, 512], F32)) for i in range(4)]
        ps_rep = [st.enter_context(nc.psum_tensor(f"bk_pr{i}", [128, 512], F32)) for i in range(2)]
        block = st.enter_context(nc.Block("banks"))
        shb, swb, peb, acb, veb, ssb = (ctx.sems[k].v for k in ("SH", "SW", "PE", "AC", "VE", "SS"))
        NPE, NAC, NVE = 48, 40, 17
        pe_b = lambda h: peb + h * NPE
        pe_n = lambda h, g: pe_b(h) + g + 1
        pe_ag = lambda h, g: pe_b(h) + 8 + g * 5 + 1
        pe_rp = lambda h, g, n: pe_b(h) + 8 + g * 5 + 2 + n
        ac_b = lambda h: acb + h * NAC
        ac_n = lambda h, g: ac_b(h) + g + 1
        ac_ae = lambda h, g, n: ac_b(h) + 8 + g * 4 + n + 1
        ve_b = lambda h: veb + h * NVE
        ve_rn = lambda h: ve_b(h) + 1
        ve_pair = lambda h, g: ve_b(h) + 1 + g * 2 + 1
        ve_comb = lambda h, g: ve_b(h) + 1 + g * 2 + 2
        sh = lambda h: shb + 16 * (h + 1)
        sw = lambda h: swb + 32 * (h + 1)
        ss = lambda h: ssb + 16 * (h + 1)

        @block.sync
        def _(s):
            ctx.waits(s, entry)
            for h in range(NH):
                if h >= 2:
                    s.wait_ge(ctx.sems["PE"].h, pe_b(h - 1))
                ctx.inc(s.dma_start(sqh[h % 2][:, :], d["sq"][h * 128:(h + 1) * 128, :]), "SH", 16)
                if h >= 1:
                    s.wait_ge(ctx.sems["VE"].h, ve_comb(h - 1, 7))
                    ctx.inc(s.dma_start(d["cc"][(h - 1) * 128:h * 128, :], cstage[(h - 1) % 2][:, :]), "SS", 16)
            s.wait_ge(ctx.sems["VE"].h, ve_comb(NH - 1, 7))
            ctx.inc(s.dma_start(d["cc"][(NH - 1) * 128:NH * 128, :], cstage[(NH - 1) % 2][:, :]), "SS", 16)

        @block.gpsimd
        def _(g):
            ctx.waits(g, entry)
            for h in range(NH):
                if h >= 2:
                    g.wait_ge(ctx.sems["PE"].h, pe_b(h - 1))
                m3 = d["mem2"][h * 512:(h + 1) * 512, :].rearrange("(n p) e -> p n e", p=128)
                o3 = membuf[h % 2][:, :].rearrange("p (n e) -> p n e", e=128)
                ctx.inc(g.dma_start(o3, m3), "SW", 16)
                ctx.inc(g.dma_start(mnb[h % 2][:, :], d["mnl"][h * 128:(h + 1) * 128, :]), "SW", 16)

        @block.tensor
        def _(t):
            ctx.waits(t, entry)
            for h in range(NH):
                t.wait_ge(ctx.sems["SW"].h, sw(h))
                t.wait_ge(ctx.sems["SH"].h, sh(h))
                for g in range(8):
                    if g >= 2:
                        t.wait_ge(ctx.sems["AC"].h, ac_n(h, g - 2))
                    elif h >= 1:
                        t.wait_ge(ctx.sems["AC"].h, ac_n(h - 1, 6 + g))
                    ctx.inc(t.matmul(ps_n[g % 2][:, :], mnb[h % 2][:, :],
                                     sqh[h % 2][:, g * 512:(g + 1) * 512], start=True, stop=True), "PE")
                for g in range(8):
                    if g >= 1:
                        t.wait_ge(ctx.sems["AC"].h, ac_ae(h, g - 1, 3))
                    elif h >= 1:
                        t.wait_ge(ctx.sems["AC"].h, ac_ae(h - 1, 7, 3))
                    for n in range(4):
                        inst = t.matmul(ps_a[n][:, :], membuf[h % 2][:, n * 128:(n + 1) * 128],
                                        sqh[h % 2][:, g * 512:(g + 1) * 512], start=True, stop=True)
                    ctx.inc(inst, "PE")
                    if g == 0:
                        t.wait_ge(ctx.sems["VE"].h, ve_rn(h))
                    for n in range(4):
                        if n < 2:
                            w = ve_comb(h, g - 1) if g >= 1 else (ve_comb(h - 1, 7) if h >= 1 else 0)
                        else:
                            w = ve_pair(h, g)
                        if w:
                            t.wait_ge(ctx.sems["VE"].h, w)
                        ctx.inc(t.matmul(ps_rep[n % 2][:, :], C["e4b"][:, n * 128:(n + 1) * 128],
                                         rnb[:, g * 512:(g + 1) * 512], start=True, stop=True), "PE")

        @block.scalar
        def _(sc):
            ctx.waits(sc, entry)
            for h in range(NH):
                for g in range(8):
                    sc.wait_ge(ctx.sems["PE"].h, pe_n(h, g))
                    if g == 0 and h >= 1:
                        sc.wait_ge(ctx.sems["VE"].h, ve_rn(h - 1))
                    ctx.inc(sc.activation(norms[:, g * 512:(g + 1) * 512], ps_n[g % 2][:, :],
                                          AF.Identity), "AC")
                for g in range(8):
                    sc.wait_ge(ctx.sems["PE"].h, pe_ag(h, g))
                    if g >= 1:
                        sc.wait_ge(ctx.sems["VE"].h, ve_comb(h, g - 1))
                    elif h >= 1:
                        sc.wait_ge(ctx.sems["VE"].h, ve_comb(h - 1, 7))
                    for n in range(4):
                        ctx.inc(sc.activation(asb[n][:, :], ps_a[n][:, :], AF.Identity), "AC")

        @block.vector
        def _(v):
            ctx.waits(v, entry)
            for h in range(NH):
                v.wait_ge(ctx.sems["AC"].h, ac_n(h, 7))
                ctx.guard(v, v.tensor_scalar_max(norms[:, :], norms[:, :], C["eps6"][0:4, 0:1]))
                ctx.guard(v, v.reciprocal(rn[:, :], norms[:, :]))
                ctx.guard(v, v.tensor_scalar_mul(rn[:, :], rn[:, :], C["wpT"][:, h:h + 1]))
                ctx.inc(v.tensor_copy(rnb[:, :], rn[:, :]), "VE")
                for g in range(8):
                    if g == 0 and h >= 2:
                        v.wait_ge(ctx.sems["SS"].h, ss(h - 2))
                    v.wait_ge(ctx.sems["AC"].h, ac_ae(h, g, 3))
                    for n in range(4):
                        v.wait_ge(ctx.sems["PE"].h, pe_rp(h, g, n))
                        dst = [accb, t1, t2, t1][n]
                        inst = v.tensor_mul(dst[:, :], asb[n][:, :], ps_rep[n % 2][:, :])
                        if n == 1:
                            ctx.inc(inst, "VE")
                        if n == 1:
                            v.tensor_add(accb[:, :], accb[:, :], t1[:, :])
                        elif n == 2:
                            v.tensor_add(accb[:, :], accb[:, :], t2[:, :])
                    ctx.inc(v.tensor_add(cstage[h % 2][:, g * 512:(g + 1) * 512],
                                         accb[:, :], t1[:, :]), "VE")

    assert ctx.sems["PE"].v == peb + NH * NPE
    assert ctx.sems["AC"].v == acb + NH * NAC
    assert ctx.sems["VE"].v == veb + NH * NVE
    return ctx.snap()


def _emit_build(ctx, v, dst, idx_bf, width, ls=range(16), tmp_name="_btmp"):
    C = ctx.consts
    ls = list(ls)
    l0 = ls[0]
    v.tensor_scalar(dst, idx_bf, C["lvals"][:, l0:l0 + 1], C["lut_rep"][:, l0:l0 + 1],
                    ALU.is_equal, ALU.mult)
    inst = None
    tmp = ctx.consts[tmp_name]
    for l in ls[1:]:
        v.tensor_scalar(tmp[:, 0:width], idx_bf, C["lvals"][:, l:l + 1],
                        C["lut_rep"][:, l:l + 1], ALU.is_equal, ALU.mult)
        inst = v.tensor_add(dst, dst, tmp[:, 0:width])
    return inst


def emit_ghost_gu(ctx, entry):
    nc, d, C = ctx.nc, ctx.dram, ctx.consts
    NT = T // 512
    with ExitStack() as st:
        act = st.enter_context(nc.sbuf_tensor("gg_act", [128, NKC * T], BF16))
        wgb = [st.enter_context(nc.sbuf_tensor(f"gg_wg{i}", [128, D], BF16)) for i in range(2)]
        wub = [st.enter_context(nc.sbuf_tensor(f"gg_wu{i}", [128, D], BF16)) for i in range(2)]
        gs = [st.enter_context(nc.sbuf_tensor(f"gg_g{i}", [128, 512], F32)) for i in range(2)]
        stg = [st.enter_context(nc.sbuf_tensor(f"gg_s{i}", [128, T], BF16)) for i in range(2)]
        pg = [st.enter_context(nc.psum_tensor(f"gg_pg{i}", [128, 512], F32)) for i in range(4)]
        pl = [st.enter_context(nc.psum_tensor(f"gg_pl{i}", [128, 512], F32)) for i in range(4)]
        block = st.enter_context(nc.Block("ghostgu"))
        shb, swb, peb, acb, veb, ssb = (ctx.sems[k].v for k in ("SH", "SW", "PE", "AC", "VE", "SS"))
        sh_act = shb + 64
        NP = NT // 2
        sw = lambda ic: swb + 32 * (ic + 1)
        pe = lambda pidx, k: peb + pidx * 2 + k + 1
        ac = lambda pidx, j: acb + pidx * 2 + j + 1
        ve_m = lambda pidx, j: veb + pidx * 2 + j + 1
        ss = lambda ic: ssb + 16 * (ic + 1)

        @block.sync
        def _(s):
            ctx.waits(s, entry)
            a3 = d["xn1"][:, :].rearrange("(kc p) t -> p kc t", p=128)
            o3 = act[:, :].rearrange("p (kc t) -> p kc t", t=T)
            for gi in range(4):
                ctx.inc(s.dma_start(o3[:, gi * 4:gi * 4 + 4, :], a3[:, gi * 4:gi * 4 + 4, :]), "SH", 16)
            for ic in range(NIC):
                s.wait_ge(ctx.sems["VE"].h, ve_m(ic * NP + NP - 1, 1))
                ctx.inc(s.dma_start(d["h1"][ic * 128:(ic + 1) * 128, :], stg[ic % 2][:, :]), "SS", 16)

        @block.gpsimd
        def _(g):
            ctx.waits(g, entry)
            g.wait_ge(ctx.consts["ccdone"], 1)
            for ic in range(NIC):
                if ic >= 2:
                    g.wait_ge(ctx.sems["PE"].h, pe((ic - 2) * NP + NP - 1, 1))
                rg, ru = _grow(ic, 0), _grow(ic, 1)
                ctx.inc(g.dma_start(wgb[ic % 2][:, :], d["gwall"][rg:rg + 128, :]), "SW", 16)
                ctx.inc(g.dma_start(wub[ic % 2][:, :], d["gwall"][ru:ru + 128, :]), "SW", 16)

        @block.vector
        def _(v):
            ctx.waits(v, entry)
            for ic in range(NIC):
                for tp in range(NP):
                    pidx = ic * NP + tp
                    bs = (pidx % 2) * 2
                    v.wait_ge(ctx.sems["PE"].h, pe(pidx, 1))
                    for j in range(2):
                        tt = tp * 2 + j
                        lin = pidx * 2 + j
                        v.wait_ge(ctx.sems["AC"].h, ac(pidx, j))
                        if tt == 0 and j == 0 and ic >= 2:
                            v.wait_ge(ctx.sems["SS"].h, ss(ic - 2))
                        ctx.inc(v.tensor_mul(stg[ic % 2][:, tt * 512:(tt + 1) * 512],
                                             pl[bs + j][:, :], gs[lin % 2][:, :]), "VE")

        @block.tensor
        def _(t):
            ctx.waits(t, entry)
            t.wait_ge(ctx.sems["SH"].h, sh_act)
            for ic in range(NIC):
                t.wait_ge(ctx.sems["SW"].h, sw(ic))
                for tp in range(NP):
                    pidx = ic * NP + tp
                    if pidx >= 2:
                        t.wait_ge(ctx.sems["AC"].h, ac(pidx - 2, 1))
                        t.wait_ge(ctx.sems["VE"].h, ve_m(pidx - 2, 1))
                    bs = (pidx % 2) * 2
                    o0 = (tp * 2) * 512
                    for kc in range(NKC):
                        co = kc * T + o0
                        inst = t.matmul(pg[bs][:, :], wgb[ic % 2][:, kc * 128:(kc + 1) * 128],
                                        act[:, co: co + 512], start=(kc == 0), stop=(kc == NKC - 1))
                        inst = t.matmul(pg[bs + 1][:, :], wgb[ic % 2][:, kc * 128:(kc + 1) * 128],
                                        act[:, co + 512: co + 1024], start=(kc == 0), stop=(kc == NKC - 1))
                    ctx.inc(inst, "PE")
                    for kc in range(NKC):
                        co = kc * T + o0
                        inst = t.matmul(pl[bs][:, :], wub[ic % 2][:, kc * 128:(kc + 1) * 128],
                                        act[:, co: co + 512], start=(kc == 0), stop=(kc == NKC - 1))
                        inst = t.matmul(pl[bs + 1][:, :], wub[ic % 2][:, kc * 128:(kc + 1) * 128],
                                        act[:, co + 512: co + 1024], start=(kc == 0), stop=(kc == NKC - 1))
                    ctx.inc(inst, "PE")

        @block.scalar
        def _(sc):
            ctx.waits(sc, entry)
            for ic in range(NIC):
                for tp in range(NP):
                    pidx = ic * NP + tp
                    bs = (pidx % 2) * 2
                    sc.wait_ge(ctx.sems["PE"].h, pe(pidx, 0))
                    for j in range(2):
                        lin = pidx * 2 + j
                        if lin >= 2:
                            sc.wait_ge(ctx.sems["VE"].h, veb + lin - 1)
                        ctx.inc(sc.activation(gs[lin % 2][:, :], pg[bs + j][:, :], AF.Silu), "AC")

    return ctx.snap()


def emit_gbuild(ctx, entry):
    nc, d, C = ctx.nc, ctx.dram, ctx.consts
    NTILE = 18
    idxt = [_sb(ctx, f"gb2_i{i}", [128, D], BF16) for i in range(2)]
    wst = [_sb(ctx, f"gb2_w{i}", [128, D], BF16) for i in range(2)]
    btmp = _sb(ctx, "gb2_bt", [128, D], BF16)
    ctx.consts["_btmp"] = btmp
    ccdone = ctx.st.enter_context(nc.semaphore("ccdone"))
    ctx.consts["ccdone"] = ccdone
    blk = nc.Block("gbuild")
    block = blk.__enter__()
    swb, veb, ssb = (ctx.sems[k].v for k in ("SW", "VE", "SS"))
    sw = lambda k: swb + 16 * (k + 1)
    ve = lambda k: veb + k + 1
    ss = lambda k: ssb + 16 * (k + 1)

    @block.gpsimd
    def _(g):
        ctx.waits(g, entry)
        for k in range(NTILE):
            if k >= 2:
                g.wait_ge(ctx.sems["VE"].h, ve(k - 2))
            ctx.inc(g.dma_start(idxt[k % 2][:, :], d["gsh"][k * 128:(k + 1) * 128, :]), "SW", 16)
            if k >= 1:
                g.wait_ge(ctx.sems["VE"].h, ve(k - 1))
                ctx.inc(g.dma_start(d["cgc"][(k - 1) * 128:k * 128, :], wst[(k - 1) % 2][:, :]), "SS", 16)
        g.wait_ge(ctx.sems["VE"].h, ve(NTILE - 1))
        ctx.inc(g.dma_start(d["cgc"][(NTILE - 1) * 128:NTILE * 128, :], wst[(NTILE - 1) % 2][:, :]), "SS", 16)
        ctx.consts["gbuild_ss"] = ss(NTILE - 1)

    @block.vector
    def _(v):
        ctx.waits(v, entry)
        for k in range(NTILE):
            v.wait_ge(ctx.sems["SW"].h, sw(k))
            if k >= 2:
                v.wait_ge(ctx.sems["SS"].h, ss(k - 2))
            ctx.inc(_emit_build(ctx, v, wst[k % 2][:, :], idxt[k % 2][:, :], D), "VE")

    blk.__exit__(None, None, None)
    del ctx.consts["_btmp"]
    return ctx.snap()


def emit_gcoll(ctx, entry):
    nc, d = ctx.nc, ctx.dram
    blk = nc.Block("gcoll")
    block = blk.__enter__()

    @block.gpsimd
    def _(g):
        g.wait_ge(ctx.sems["SS"].h, ctx.consts["gbuild_ss"])
        g.collective_compute(
            "AllGather", mybir.AluOpType.bypass,
            replica_groups=[list(range(NCORES))],
            ins=[d["cgc"].ap().opt()], outs=[d["gwall"].ap().opt()],
        ).then_inc(ctx.consts["ccdone"], 1)

    blk.__exit__(None, None, None)
    return ctx.snap()


def emit_repack(ctx, entry):
    nc, d = ctx.nc, ctx.dram
    ccdone = ctx.consts["ccdone"]
    blk = nc.Block("repack")
    block = blk.__enter__()

    @block.gpsimd
    def _(g):
        ctx.waits(g, entry)
        g.wait_ge(ccdone, 1)
        for ic in range(NIC):
            r = _grow(ic, 2)
            ctx.inc(g.dma_start(d["wdx"][ic * 128:(ic + 1) * 128, :],
                                d["gwall"][r:r + 128, :]), "SW", 16)

    blk.__exit__(None, None, None)
    return ctx.snap()


def emit_gdbuild(ctx, entry):
    nc, d, C = ctx.nc, ctx.dram, ctx.consts
    with ExitStack() as st:
        idx_t = [st.enter_context(nc.sbuf_tensor(f"gb_i{i}", [128, D], BF16)) for i in range(2)]
        wds = [st.enter_context(nc.sbuf_tensor(f"gb_w{i}", [128, D], BF16)) for i in range(2)]
        btmp = st.enter_context(nc.sbuf_tensor("gb_bt", [128, D], BF16))
        ctx.consts["_btmp"] = btmp
        block = st.enter_context(nc.Block("gdbuild"))
        swb, veb, ssb = (ctx.sems[k].v for k in ("SW", "VE", "SS"))
        sw = lambda kc: swb + 16 * (kc + 1)
        ve_b = lambda kc: veb + kc + 1
        ss = lambda kc: ssb + 16 * (kc + 1)

        @block.gpsimd
        def _(g):
            ctx.waits(g, entry)
            for kc in range(NIC):
                if kc >= 2:
                    g.wait_ge(ctx.sems["VE"].h, ve_b(kc - 2))
                ctx.inc(g.dma_start(idx_t[kc % 2][:, :], d["didx"][kc * 128:(kc + 1) * 128, :]), "SW", 16)

        @block.vector
        def _(v):
            ctx.waits(v, entry)
            for kc in range(NIC):
                v.wait_ge(ctx.sems["SW"].h, sw(kc))
                if kc >= 2:
                    v.wait_ge(ctx.sems["SS"].h, ss(kc - 2))
                ctx.inc(_emit_build(ctx, v, wds[kc % 2][:, :], idx_t[kc % 2][:, :], D), "VE")

        @block.sync
        def _(s):
            ctx.waits(s, entry)
            for kc in range(NIC):
                s.wait_ge(ctx.sems["VE"].h, ve_b(kc))
                ctx.inc(s.dma_start(d["wdx"][kc * 128:(kc + 1) * 128, :], wds[kc % 2][:, :]), "SS", 16)

    del ctx.consts["_btmp"]
    return ctx.snap()


def _u8view(a):
    a = np.ascontiguousarray(a)
    return np.ascontiguousarray(a.view(np.uint8).reshape(*a.shape, 4)[..., 0])


def _hadamard(n):
    h = np.array([[1.0]], dtype=np.float64)
    while h.shape[0] < n:
        h = np.block([[h, h], [h, -h]])
    return h


def _grow(ic, mat):
    return ((ic % 8) * 18 + mat * 6 + ic // 8) * 128


def prep_gsh(W, core):
    gi, ui, di = W["gidx"], W["uidx"], W["didx"]
    gsh = np.zeros((18, 128, D), np.uint8)
    for s in range(6):
        ic = core + 8 * s
        if ic >= NIC:
            continue
        for mat, srcm in ((0, gi), (1, ui)):
            col = srcm[:, ic * 128:(ic + 1) * 128]
            gsh[mat * 6 + s] = col.reshape(16, 128, 128).transpose(1, 0, 2).reshape(128, D)
        gsh[12 + s] = di[ic * 128:(ic + 1) * 128, :]
    return np.ascontiguousarray(gsh.reshape(18 * 128, D))


def prep_inputs(inputs):
    W = {}
    f32 = lambda a: np.ascontiguousarray(np.asarray(a), dtype=np.float32)
    W["wqkT"] = np.ascontiguousarray(np.concatenate(
        [f32(inputs["L0_attn_Wq"]).T, f32(inputs["L0_attn_Wk"]).T], axis=1))
    bqk = np.concatenate([f32(inputs["L0_attn_bq"]), f32(inputs["L0_attn_bk"])])
    W["bqk"] = np.ascontiguousarray(bqk.reshape(32, 128).T)
    W["wvT"] = np.ascontiguousarray(f32(inputs["L0_attn_Wv"]).T)
    W["bv"] = f32(inputs["L0_attn_bv"]).reshape(1, D)
    W["woT"] = np.ascontiguousarray(f32(inputs["L0_attn_Wo"]).T)
    W["bo"] = np.ascontiguousarray(f32(inputs["L0_attn_bo"]).reshape(16, 128).T)
    W["w13T"] = np.ascontiguousarray(np.concatenate(
        [f32(inputs["L0_ffn_W1"]).T, f32(inputs["L0_ffn_W3"]).T], axis=1))
    W["w2T"] = np.ascontiguousarray(f32(inputs["L0_ffn_W2"]).T)
    W["l1qT"] = np.ascontiguousarray(f32(inputs["L1_attn_Wq"]).T)
    W["l1oT"] = np.ascontiguousarray(f32(inputs["L1_attn_Wo"]).T)
    W["mem2"] = np.ascontiguousarray(
        np.transpose(f32(inputs["L1_attn_memories"]), (1, 0, 2, 3)).reshape(NH * 4 * 128, 128))
    W["mnl"] = np.ascontiguousarray(
        np.transpose(f32(inputs["L1_attn_memory_norms"]), (1, 2, 0)).reshape(NH * 128, 4))
    W["mnrows"] = np.ascontiguousarray(f32(inputs["L1_attn_memory_norms"]).reshape(4, D))
    W["bwt"] = f32(inputs["L1_attn_bank_weights"])
    e4 = np.zeros((4, 512), np.float32)
    for n in range(4):
        e4[n, n * 128:(n + 1) * 128] = 1.0
    W["e4"] = e4
    W["e4b"] = e4.astype(BF)
    W["lutrow"] = f32(inputs["L1_ffn_lut"]).reshape(1, 16)
    W["lvrow"] = np.arange(16, dtype=np.float32).reshape(1, 16)
    W["gidx"] = _u8view(np.asarray(inputs["L1_ffn_gate_idx"]).T)
    W["uidx"] = _u8view(np.asarray(inputs["L1_ffn_up_idx"]).T)
    W["didx"] = _u8view(np.asarray(inputs["L1_ffn_down_idx"]).T)
    W["identb"] = np.eye(128, dtype=BF)
    W["ident32"] = np.eye(128, dtype=np.float32)
    W["h128b"] = (_hadamard(128) * (2.0 ** -7)).astype(BF)
    W["h128f"] = (_hadamard(128) * (2.0 ** -7)).astype(np.float32)
    W["h64b"] = _hadamard(64).astype(BF)
    return W


def build_phase_program(phases, io_names):
    nc = bass.Bass("TRN2", target_bir_lowering=False)
    st = nc.ctx
    ctx = Ctx(nc, st)
    ins, outs = io_names
    for n in ins:
        shape, dt, _ = SPECS[n]
        ctx.dram[n] = nc.dram_tensor(n, shape, dt, kind="ExternalInput")
    for n in outs:
        shape, dt, _ = SPECS[n]
        ctx.dram[n] = nc.dram_tensor(n, shape, dt, kind="ExternalOutput")
    entry = ctx.snap()
    entry = emit_init(ctx, entry)
    for ph in phases:
        entry = ph(ctx, entry)
    with nc.Block("fin") as block:
        @block.sync
        def _(s):
            ctx.waits(s, entry)
    return nc


CONST_INS = ["identb", "ident32", "h128b", "h64b", "h128f", "lutrow", "lvrow",
             "bqk", "bo", "bv", "bwt", "mnrows", "e4", "e4b"]


PHASE_LIST = [
    ("ln0", [lambda c, e: emit_combine_ln(c, e, "x", None, None, "xn", tag="l0")], ["x"], ["xn"]),
    ("qkv", [lambda c, e: emit_proj(c, e, "wqkT", "xn", "qk", NKC, 32, "bias", bias="bqk", tag="qk")],
     ["wqkT", "xn"], ["qk"]),
    ("vproj", [lambda c, e: emit_vproj(c, e)], ["wvT", "xn"], ["vv"]),
    ("attn", [lambda c, e: emit_attn(c, e)], ["qk", "vv"], ["oo"]),
    ("oproj", [lambda c, e: emit_proj(c, e, "woT", "oo", "s0", NKC, 16, "bias", bias="bo", tag="op")],
     ["woT", "oo"], ["s0"]),
    ("w2x", [lambda c, e: emit_w2x(c, e)], ["w2T"], ["w2x1", "w2x"]),
    ("ffn13", [lambda c, e: emit_ffn13(c, e)], ["w13T", "xn"], ["hs"]),
    ("ffn2", [lambda c, e: emit_ffn2(c, e)], ["w2x", "hs"], ["g0"]),
    ("mt0", [lambda c, e: emit_mult_t(c, e, "s0", "g0", "r0tm", tag="0")], ["s0", "g0"], ["r0tm"]),
    ("cmb0", [lambda c, e: emit_combine_ln(c, e, "x", "r0tm", "x1", "xn1", tag="c0")],
     ["x", "r0tm"], ["x1", "xn1"]),
    ("q1", [lambda c, e: emit_proj(c, e, "l1qT", "xn1", "sq", NKC, 16, "elu1", tag="q1")],
     ["l1qT", "xn1"], ["sq"]),
    ("banks", [lambda c, e: emit_banks(c, e)], ["sq", "mem2", "mnl"], ["cc"]),
    ("o1", [lambda c, e: emit_proj(c, e, "l1oT", "cc", "s1", NKC, 16, "copy", tag="o1")],
     ["l1oT", "cc"], ["s1"]),
    ("ggu", [lambda c, e: emit_ghost_gu(c, e)], ["gsh", "xn1"], ["h1"]),
    ("rpk", [lambda c, e: emit_repack(c, e)], [], ["wdx"]),
    ("gdn", [lambda c, e: emit_ffn2(c, e, "wdx", "h1", "g1", NIC, tag="gd")], ["wdx", "h1"], ["g1"]),
    ("mt1", [lambda c, e: emit_mult_t(c, e, "s1", "g1", "r1tm", tag="1")], ["s1", "g1"], ["r1tm"]),
    ("cmb1", [lambda c, e: emit_combine_ln(c, e, "x1", "r1tm", "out", None, tag="c1")],
     ["x1", "r1tm"], ["out"]),
]


def build_fused_program():
    nc = bass.Bass("TRN2", target_bir_lowering=False)
    ctx = Ctx(nc, nc.ctx)
    for n, (shape, dt, kind) in SPECS.items():
        if kind == "in":
            ctx.dram[n] = nc.dram_tensor(n, shape, dt, kind="ExternalInput")
        elif kind == "out":
            ctx.dram[n] = nc.dram_tensor(n, shape, dt, kind="ExternalOutput")
        else:
            aspace = "Shared" if n == "gwall" else "Local"
            ctx.dram[n] = nc.dram_tensor(n, shape, dt, kind="Internal", addr_space=aspace)
    entry = ctx.snap()
    entry = emit_init(ctx, entry)
    for name, emitters, ins, outs in PHASE_LIST:
        for ph in emitters:
            entry2 = ph(ctx, entry)
            entry = entry2
        if name == "oproj":
            emit_gbuild(ctx, entry)
        elif name == "w2x":
            emit_gcoll(ctx, entry)
    with nc.Block("fin") as block:
        @block.sync
        def _(s):
            ctx.waits(s, entry)
    return nc


FUSED = True


def kernel(**inputs):
    W = prep_inputs(inputs)
    x = np.asarray(inputs["L0_attn_x"], dtype=np.float32)
    percore = {"x": [np.ascontiguousarray(x[i]) for i in range(NCORES)],
               "gsh": [prep_gsh(W, i) for i in range(NCORES)]}
    if FUSED:
        nc = build_fused_program()
        in_names = [n for n, (s, dt, k) in SPECS.items() if k == "in"]
        in_maps = []
        for core in range(NCORES):
            m = {n: (percore[n][core] if n in percore else W[n]) for n in in_names}
            in_maps.append(m)
        res = run_bass_kernel_spmd(nc, in_maps, core_ids=list(range(NCORES)))
        return np.stack([np.asarray(res.results[c]["out"], dtype=np.float32) for c in range(NCORES)])
    for name, emitters, ins, outs in PHASE_LIST:
        all_ins = sorted(set(ins) | set(CONST_INS))
        nc = build_phase_program(emitters, (all_ins, outs))
        in_maps = []
        for core in range(NCORES):
            m = {}
            for n in all_ins:
                m[n] = percore[n][core] if n in percore else W[n]
            in_maps.append(m)
        res = run_bass_kernel_spmd(nc, in_maps, core_ids=list(range(NCORES)))
        for n in outs:
            percore[n] = [res.results[core][n] for core in range(NCORES)]
    return np.stack([np.asarray(o, dtype=np.float32) for o in percore["out"]])


# version 14
# speedup vs baseline: 1.0211x; 1.0174x over previous
import numpy as np
import ml_dtypes
from contextlib import ExitStack

import concourse.bass as bass
import concourse.mybir as mybir
from concourse.bass_utils import run_bass_kernel_spmd

F32 = mybir.dt.float32
BF16 = mybir.dt.bfloat16
U8 = mybir.dt.uint8
AF = mybir.ActivationFunctionType
ALU = mybir.AluOpType
AX = mybir.AxisListType
BF = ml_dtypes.bfloat16

D = 2048
T = 4096
NKC = 16
NH = 16
HID0 = 8192
INTER = 5504
NIC = 43
NCORES = 8
RS128 = 1.0 / float(np.sqrt(128.0))

SPECS = {
    "x":     ([T, D], F32, "in"),
    "wqkT":  ([D, 2 * D], F32, "in"),
    "bqk":   ([128, 32], F32, "in"),
    "wvT":   ([D, D], F32, "in"),
    "bv":    ([1, D], F32, "in"),
    "woT":   ([D, D], F32, "in"),
    "bo":    ([128, 16], F32, "in"),
    "w13T":  ([D, 2 * HID0], F32, "in"),
    "w2T":   ([HID0, D], F32, "in"),
    "l1qT":  ([D, D], F32, "in"),
    "l1oT":  ([D, D], F32, "in"),
    "mem2":  ([NH * 4 * 128, 128], F32, "in"),
    "mnl":   ([NH * 128, 4], F32, "in"),
    "mnrows": ([4, D], F32, "in"),
    "e4":    ([4, 512], F32, "in"),
    "e4b":   ([4, 512], BF16, "in"),
    "bwt":   ([16, 4], F32, "in"),
    "lutrow": ([1, 16], F32, "in"),
    "lvrow": ([1, 16], F32, "in"),
    "gidx":  ([D, INTER], U8, "in"),
    "uidx":  ([D, INTER], U8, "in"),
    "didx":  ([INTER, D], U8, "in"),
    "identb": ([128, 128], BF16, "in"),
    "ident32": ([128, 128], F32, "in"),
    "h128b": ([128, 128], BF16, "in"),
    "h128f": ([128, 128], F32, "in"),
    "h64b":  ([64, 64], BF16, "in"),
    "xn":    ([D, T], BF16, "scr"),
    "qk":    ([2 * D, T], BF16, "scr"),
    "vv":    ([T, D], BF16, "scr"),
    "oo":    ([D, T], BF16, "scr"),
    "s0":    ([D, T], BF16, "scr"),
    "w2x1":  ([HID0, D], BF16, "scr"),
    "w2x":   ([HID0, D], BF16, "scr"),
    "hs":    ([HID0, T], BF16, "scr"),
    "g0":    ([D, T], BF16, "scr"),
    "r0tm":  ([T, D], BF16, "scr"),
    "x1":    ([T, D], F32, "scr"),
    "xn1":   ([D, T], BF16, "scr"),
    "sq":    ([D, T], BF16, "scr"),
    "cc":    ([D, T], BF16, "scr"),
    "s1":    ([D, T], BF16, "scr"),
    "h1":    ([INTER, T], BF16, "scr"),
    "wdx":   ([INTER, D], BF16, "scr"),
    "gsh":   ([18 * 128, D], U8, "in"),
    "cgc":   ([18 * 128, D], BF16, "scr"),
    "gwall": ([8 * 18 * 128, D], BF16, "scr"),
    "g1":    ([D, T], BF16, "scr"),
    "r1tm":  ([T, D], BF16, "scr"),
    "out":   ([T, D], F32, "out"),
}

SEM_NAMES = ("SH", "SW", "SS", "PE", "VE", "AC", "PO", "GV", "GA", "GP")


class Sem:
    __slots__ = ("h", "v")

    def __init__(self, h):
        self.h = h
        self.v = 0


class Ctx:
    def __init__(self, nc, st):
        self.nc = nc
        self.st = st
        self.sems = {n: Sem(st.enter_context(nc.semaphore(n))) for n in SEM_NAMES}
        self.consts = {}
        self.dram = {}

    def snap(self):
        return {k: s.v for k, s in self.sems.items()}

    def waits(self, eng, snap):
        for k, v in snap.items():
            if v > 0:
                eng.wait_ge(self.sems[k].h, v)

    def inc(self, inst, name, n=1):
        s = self.sems[name]
        inst.then_inc(s.h, n)
        s.v += n
        return s.v

    def guard(self, eng, *insts, sem="GV"):
        for inst in insts:
            v = self.inc(inst, sem)
        eng.wait_ge(self.sems[sem].h, v)


def _sb(ctx, name, shape, dtype):
    return ctx.st.enter_context(ctx.nc.sbuf_tensor(name, shape, dtype))


def emit_init(ctx, entry):
    nc, d = ctx.nc, ctx.dram
    C = ctx.consts
    C["identb"] = _sb(ctx, "c_identb", [128, 128], BF16)
    C["ident32"] = _sb(ctx, "c_ident32", [128, 128], F32)
    C["h128b"] = _sb(ctx, "c_h128b", [128, 128], BF16)
    C["h128f"] = _sb(ctx, "c_h128f", [128, 128], F32)
    C["h64b"] = _sb(ctx, "c_h64b", [64, 64], BF16)
    C["bqk"] = _sb(ctx, "c_bqk", [128, 32], F32)
    C["bo"] = _sb(ctx, "c_bo", [128, 16], F32)
    C["bv_bf"] = _sb(ctx, "c_bv", [1, D], BF16)
    C["lut_rep"] = _sb(ctx, "c_lutrep", [128, 16], F32)
    C["lvals"] = _sb(ctx, "c_lvals", [128, 16], F32)
    C["ones1f"] = _sb(ctx, "c_ones1f", [1, 128], F32)
    C["ones1b"] = _sb(ctx, "c_ones1b", [1, 128], BF16)
    C["onescb"] = _sb(ctx, "c_onescb", [128, 1], BF16)
    C["onescf"] = _sb(ctx, "c_onescf", [128, 1], F32)
    C["wpT"] = _sb(ctx, "c_wpT", [4, 16], F32)
    C["e4"] = _sb(ctx, "c_e4", [4, 512], F32)
    C["e4b"] = _sb(ctx, "c_e4b", [4, 512], BF16)
    for nm, val in (("eps5", 1e-5), ("eps6", 1e-6), ("invD", 1.0 / D),
                    ("ninvD", -1.0 / D), ("negone", -1.0), ("nRS", -1.02 * RS128)):
        C[nm] = _sb(ctx, "c_" + nm, [128, 1], F32)
    ist = ExitStack()
    lr = ist.enter_context(nc.sbuf_tensor("i_lutrow", [1, 16], F32))
    lv = ist.enter_context(nc.sbuf_tensor("i_lvrow", [1, 16], F32))
    bwt = ist.enter_context(nc.sbuf_tensor("i_bw", [16, 4], F32))
    mnr = ist.enter_context(nc.sbuf_tensor("i_mnr", [4, D], F32))
    sm = ist.enter_context(nc.sbuf_tensor("i_sm", [16, 4], F32))
    t16 = ist.enter_context(nc.sbuf_tensor("i_t16", [16, 1], F32))
    t4 = ist.enter_context(nc.sbuf_tensor("i_t4", [4, 1], F32))
    mrow = ist.enter_context(nc.sbuf_tensor("i_mrow", [1, 4], F32))
    mrep = ist.enter_context(nc.sbuf_tensor("i_mrep", [16, 4], F32))
    wp = ist.enter_context(nc.sbuf_tensor("i_wp", [16, 4], F32))
    ps_a = ist.enter_context(nc.psum_tensor("i_psa", [128, 512], F32))
    ps_b = ist.enter_context(nc.psum_tensor("i_psb", [128, 512], F32))

    blk_guard = nc.Block("init")
    block = blk_guard.__enter__()
    shb, veb, acb, peb = ctx.sems["SH"].v, ctx.sems["VE"].v, ctx.sems["AC"].v, ctx.sems["PE"].v

    @block.sync
    def _(s):
        ctx.waits(s, entry)
        for nm, src in (("identb", "identb"), ("ident32", "ident32"), ("h128b", "h128b"),
                        ("h64b", "h64b"), ("bqk", "bqk"), ("bo", "bo"), ("e4", "e4"), ("e4b", "e4b"),
                        ("h128f", "h128f")):
            ctx.inc(s.dma_start(C[nm][:, :], d[src][:, :]), "SH", 16)
        ctx.inc(s.dma_start(lr[:, :], d["lutrow"][:, :]), "SH", 16)
        ctx.inc(s.dma_start(lv[:, :], d["lvrow"][:, :]), "SH", 16)
        ctx.inc(s.dma_start(bwt[:, :], d["bwt"][:, :]), "SH", 16)
        ctx.inc(s.dma_start(mnr[:, :], d["mnrows"][:, :]), "SH", 16)

    @block.gpsimd
    def _(g):
        ctx.waits(g, entry)
        ctx.inc(g.dma_start(C["bv_bf"][:, :], d["bv"][:, :]), "SW", 16)
    sh_all = shb + 208

    @block.vector
    def _(v):
        ctx.waits(v, entry)
        v.memset(C["ones1f"][:, :], 1.0)
        v.memset(C["ones1b"][:, :], 1.0)
        v.memset(C["onescb"][:, :], 1.0)
        v.memset(C["onescf"][:, :], 1.0)
        for nm, val in (("eps5", 1e-5), ("eps6", 1e-6), ("invD", 1.0 / D),
                        ("ninvD", -1.0 / D), ("negone", -1.0), ("nRS", -1.02 * RS128)):
            v.memset(C[nm][:, :], val)
        ctx.inc(v.nop(), "VE")
        v.wait_ge(ctx.sems["SH"].h, sh_all)
        ctx.inc(v.reduce_max(t16[:, 0:1], bwt[:, :], axis=AX.X), "VE")
        v.wait_ge(ctx.sems["VE"].h, veb + 2)
        ctx.inc(v.tensor_scalar_mul(t16[:, 0:1], t16[:, 0:1], C["negone"][0:16, 0:1]), "VE")
        v.wait_ge(ctx.sems["AC"].h, acb + 1)
        ctx.inc(v.reduce_sum(t16[:, 0:1], sm[:, :], axis=AX.X), "VE")
        v.wait_ge(ctx.sems["VE"].h, veb + 4)
        ctx.guard(v, v.reciprocal(t16[:, 0:1], t16[:, 0:1]))
        ctx.inc(v.tensor_scalar_mul(sm[:, :], sm[:, :], t16[:, 0:1]), "VE")
        ctx.inc(v.reduce_sum(t4[:, 0:1], mnr[:, :], axis=AX.X), "VE")
        v.wait_ge(ctx.sems["VE"].h, veb + 6)
        ctx.inc(v.tensor_scalar(t4[:, 0:1], t4[:, 0:1], C["eps6"][0:4, 0:1], None, ALU.is_ge), "VE")
        v.wait_ge(ctx.sems["PE"].h, peb + 1)
        v.tensor_copy(C["lut_rep"][:, :], ps_a[:, 0:16])
        ctx.inc(v.tensor_copy(C["lvals"][:, :], ps_b[:, 0:16]), "VE")
        v.wait_ge(ctx.sems["PE"].h, peb + 2)
        ctx.inc(v.tensor_copy(mrow[:, :], ps_a[0:1, 16:20]), "VE")
        v.wait_ge(ctx.sems["PE"].h, peb + 3)
        ctx.guard(v, v.tensor_copy(mrep[:, :], ps_b[0:16, 16:20]))
        ctx.inc(v.tensor_mul(wp[:, :], sm[:, :], mrep[:, :]), "VE")
        v.wait_ge(ctx.sems["PE"].h, peb + 4)
        ctx.inc(v.tensor_copy(C["wpT"][:, :], ps_a[0:4, 32:48]), "VE")

    @block.scalar
    def _(sc):
        ctx.waits(sc, entry)
        sc.wait_ge(ctx.sems["VE"].h, veb + 3)
        ctx.inc(sc.activation(sm[:, :], bwt[:, :], AF.Exp, bias=t16[0:16, 0:1]), "AC")

    @block.tensor
    def _(t):
        ctx.waits(t, entry)
        t.wait_ge(ctx.sems["VE"].h, veb + 1)
        t.wait_ge(ctx.sems["SH"].h, sh_all)
        t.matmul(ps_a[:, 0:16], C["ones1f"][:, :], lr[:, :], start=True, stop=True)
        ctx.inc(t.matmul(ps_b[:, 0:16], C["ones1f"][:, :], lv[:, :], start=True, stop=True), "PE")
        t.wait_ge(ctx.sems["VE"].h, veb + 7)
        ctx.inc(t.transpose(ps_a[0:1, 16:20], t4[:, 0:1], C["ident32"][0:4, 0:4]), "PE")
        t.wait_ge(ctx.sems["VE"].h, veb + 9)
        ctx.inc(t.matmul(ps_b[0:16, 16:20], C["ones1f"][0:1, 0:16], mrow[:, :], start=True, stop=True), "PE")
        t.wait_ge(ctx.sems["VE"].h, veb + 10)
        ctx.inc(t.transpose(ps_a[0:4, 32:48], wp[:, :], C["ident32"][0:16, 0:16]), "PE")

    blk_guard.__exit__(None, None, None)
    ist.close()
    return ctx.snap()


def emit_proj(ctx, entry, wname, aname, oname, K, M, evac, bias=None, tag=""):
    nc, d, C = ctx.nc, ctx.dram, ctx.consts
    NT = T // 512
    NQ = NT // 4
    with ExitStack() as st:
        act = st.enter_context(nc.sbuf_tensor(f"p{tag}_act", [128, K * T], BF16))
        wbuf = [st.enter_context(nc.sbuf_tensor(f"p{tag}_w{i}", [128, K * 128], BF16)) for i in range(2)]
        stg = [st.enter_context(nc.sbuf_tensor(f"p{tag}_s{i}", [128, T], BF16)) for i in range(2)]
        ps = [st.enter_context(nc.psum_tensor(f"p{tag}_ps{i}", [128, 512], F32)) for i in range(8)]
        if evac == "elu1":
            t1 = [st.enter_context(nc.sbuf_tensor(f"p{tag}_t1{i}", [128, 512], F32)) for i in range(2)]
            t2 = [st.enter_context(nc.sbuf_tensor(f"p{tag}_t2{i}", [128, 512], F32)) for i in range(2)]
        block = st.enter_context(nc.Block(f"proj{tag}"))
        shb, swb, peb, acb, veb, ssb = (ctx.sems[k].v for k in ("SH", "SW", "PE", "AC", "VE", "SS"))
        NGRP = max(1, K // 4)
        sh_act = shb + 16 * NGRP
        sw = lambda m: swb + 16 * (m + 1)
        pe = lambda m, tq: peb + m * NQ + tq + 1
        ac = lambda m, tq, j: acb + (m * NQ + tq) * 4 + j + 1
        ve = lambda m, tq, j: veb + (m * NQ + tq) * 4 + j + 1
        ss = lambda m: ssb + 16 * (m + 1)
        ev = ve if evac == "elu1" else ac
        evs = "VE" if evac == "elu1" else "AC"

        @block.sync
        def _(s):
            ctx.waits(s, entry)
            a3 = d[aname][:, :].rearrange("(kc p) t -> p kc t", p=128)
            o3 = act[:, :].rearrange("p (kc t) -> p kc t", t=T)
            for gi in range(NGRP):
                k0, k1 = gi * 4, min(K, gi * 4 + 4)
                ctx.inc(s.dma_start(o3[:, k0:k1, :], a3[:, k0:k1, :]), "SH", 16)
            for m in range(M):
                s.wait_ge(ctx.sems[evs].h, ev(m, NQ - 1, 3))
                ctx.inc(s.dma_start(d[oname][m * 128:(m + 1) * 128, :], stg[m % 2][:, :]), "SS", 16)

        @block.gpsimd
        def _(g):
            ctx.waits(g, entry)
            for m in range(M):
                if m >= 2:
                    g.wait_ge(ctx.sems["PE"].h, pe(m - 2, NQ - 1))
                win = d[wname][:, m * 128:(m + 1) * 128].rearrange("(kc p) m -> p kc m", p=128)
                wout = wbuf[m % 2][:, :].rearrange("p (kc m) -> p kc m", m=128)
                ctx.inc(g.dma_start(wout, win), "SW", 16)

        @block.tensor
        def _(t):
            ctx.waits(t, entry)
            t.wait_ge(ctx.sems["SH"].h, sh_act)
            for m in range(M):
                t.wait_ge(ctx.sems["SW"].h, sw(m))
                for tq in range(NQ):
                    idx = m * NQ + tq
                    if idx >= 2:
                        pm, pq = divmod(idx - 2, NQ)
                        t.wait_ge(ctx.sems[evs].h, ev(pm, pq, 3))
                    bs = (idx % 2) * 4
                    for kc in range(K):
                        for j in range(4):
                            inst = t.matmul(ps[bs + j][:, :], wbuf[m % 2][:, kc * 128:(kc + 1) * 128],
                                            act[:, kc * T + (tq * 4 + j) * 512: kc * T + (tq * 4 + j) * 512 + 512],
                                            start=(kc == 0), stop=(kc == K - 1))
                    ctx.inc(inst, "PE")

        @block.scalar
        def _(sc):
            ctx.waits(sc, entry)
            for m in range(M):
                for tq in range(NQ):
                    idx = m * NQ + tq
                    sc.wait_ge(ctx.sems["PE"].h, pe(m, tq))
                    if tq == 0 and m >= 2:
                        sc.wait_ge(ctx.sems["SS"].h, ss(m - 2))
                    bs = (idx % 2) * 4
                    for j in range(4):
                        tt = tq * 4 + j
                        dst = stg[m % 2][:, tt * 512: tt * 512 + 512]
                        if evac == "copy":
                            ctx.inc(sc.activation(dst, ps[bs + j][:, :], AF.Copy), "AC")
                        elif evac == "bias":
                            ctx.inc(sc.activation(dst, ps[bs + j][:, :], AF.Identity, bias=C[bias][:, m:m + 1]), "AC")
                        elif evac == "silu":
                            ctx.inc(sc.activation(dst, ps[bs + j][:, :], AF.Silu), "AC")
                        elif evac == "elu1":
                            if idx >= 1 or j >= 2:
                                pj = (idx * 4 + j - 2)
                                sc.wait_ge(ctx.sems["VE"].h, veb + pj + 1)
                            sc.activation(t1[j % 2][:, :], ps[bs + j][:, :], AF.Relu)
                            sc.activation(t2[j % 2][:, :], ps[bs + j][:, :], AF.Relu, scale=-1.0)
                            ctx.inc(sc.activation(t2[j % 2][:, :], t2[j % 2][:, :], AF.Exp, scale=-1.0), "AC")

        if evac == "elu1":
            @block.vector
            def _(v):
                ctx.waits(v, entry)
                for m in range(M):
                    for tq in range(NQ):
                        idx = m * NQ + tq
                        for j in range(4):
                            tt = tq * 4 + j
                            v.wait_ge(ctx.sems["AC"].h, ac(m, tq, j))
                            if tq == 0 and j == 0 and m >= 2:
                                v.wait_ge(ctx.sems["SS"].h, ss(m - 2))
                            dst = stg[m % 2][:, tt * 512: tt * 512 + 512]
                            ctx.inc(v.tensor_add(dst, t1[j % 2][:, :], t2[j % 2][:, :]), "VE")

    return ctx.snap()


def emit_vproj(ctx, entry, tag="v"):
    nc, d, C = ctx.nc, ctx.dram, ctx.consts
    with ExitStack() as st:
        act = st.enter_context(nc.sbuf_tensor(f"{tag}_act", [128, NKC * T], BF16))
        wres = [st.enter_context(nc.sbuf_tensor(f"{tag}_w{i}", [128, NKC * 512], BF16)) for i in range(2)]
        stg = [st.enter_context(nc.sbuf_tensor(f"{tag}_s{i}", [128, 512], BF16)) for i in range(2)]
        ps = [st.enter_context(nc.psum_tensor(f"{tag}_ps{i}", [128, 512], F32)) for i in range(2)]
        block = st.enter_context(nc.Block(f"vproj{tag}"))
        shb, swb, peb, acb, ssb = (ctx.sems[k].v for k in ("SH", "SW", "PE", "AC", "SS"))
        sh_act = shb + 16 * 4
        sw = lambda oc: swb + 16 * (oc + 1)
        pe = lambda idx: peb + idx + 1
        ac = lambda idx: acb + idx + 1
        ss = lambda idx: ssb + 16 * (idx + 1)

        @block.sync
        def _(s):
            ctx.waits(s, entry)
            a3 = d["xn"][:, :].rearrange("(kc p) t -> p kc t", p=128)
            o3 = act[:, :].rearrange("p (kc t) -> p kc t", t=T)
            for gi in range(4):
                ctx.inc(s.dma_start(o3[:, gi * 4:gi * 4 + 4, :], a3[:, gi * 4:gi * 4 + 4, :]), "SH", 16)
            for oc in range(4):
                for m in range(32):
                    idx = oc * 32 + m
                    s.wait_ge(ctx.sems["AC"].h, ac(idx))
                    ctx.inc(s.dma_start(d["vv"][m * 128:(m + 1) * 128, oc * 512:(oc + 1) * 512],
                                        stg[idx % 2][:, :]), "SS", 16)

        @block.gpsimd
        def _(g):
            ctx.waits(g, entry)
            for oc in range(4):
                if oc >= 2:
                    g.wait_ge(ctx.sems["PE"].h, pe((oc - 1) * 32 - 1))
                win = d["wvT"][:, oc * 512:(oc + 1) * 512].rearrange("(kc p) o -> p kc o", p=128)
                wout = wres[oc % 2][:, :].rearrange("p (kc o) -> p kc o", o=512)
                ctx.inc(g.dma_start(wout, win), "SW", 16)

        @block.tensor
        def _(t):
            ctx.waits(t, entry)
            t.wait_ge(ctx.sems["SH"].h, sh_act)
            for oc in range(4):
                t.wait_ge(ctx.sems["SW"].h, sw(oc))
                for m in range(32):
                    idx = oc * 32 + m
                    if idx >= 2:
                        t.wait_ge(ctx.sems["AC"].h, ac(idx - 2))
                    p = ps[idx % 2]
                    for kc in range(NKC):
                        t.matmul(p[:, :], act[:, kc * T + m * 128: kc * T + m * 128 + 128],
                                 wres[oc % 2][:, kc * 512:(kc + 1) * 512],
                                 start=(kc == 0), stop=False)
                    inst = t.matmul(p[:, :], C["ones1b"][:, :],
                                    C["bv_bf"][:, oc * 512:oc * 512 + 512], start=False, stop=True)
                    ctx.inc(inst, "PE")

        @block.scalar
        def _(sc):
            ctx.waits(sc, entry)
            for oc in range(4):
                for m in range(32):
                    idx = oc * 32 + m
                    sc.wait_ge(ctx.sems["PE"].h, pe(idx))
                    if idx >= 2:
                        sc.wait_ge(ctx.sems["SS"].h, ss(idx - 2))
                    ctx.inc(sc.activation(stg[idx % 2][:, :], ps[idx % 2][:, :], AF.Copy), "AC")

    return ctx.snap()


def emit_combine_ln(ctx, entry, base_name, r_name, xout_name, ln_out, tag=""):
    nc, d, C = ctx.nc, ctx.dram, ctx.consts
    with ExitStack() as st:
        xa = [st.enter_context(nc.sbuf_tensor(f"c{tag}_x{i}", [128, D], F32)) for i in range(2)]
        ra = [st.enter_context(nc.sbuf_tensor(f"c{tag}_r{i}", [128, D], F32)) for i in range(2)] if r_name else None
        xsq = st.enter_context(nc.sbuf_tensor(f"c{tag}_sq", [128, D], F32))
        sv = st.enter_context(nc.sbuf_tensor(f"c{tag}_sv", [128, 8], F32))
        xnb = [st.enter_context(nc.sbuf_tensor(f"c{tag}_n{i}", [128, D], BF16)) for i in range(2)]
        stg = [st.enter_context(nc.sbuf_tensor(f"c{tag}_t{i}", [128, D], BF16)) for i in range(2)]
        pst = [st.enter_context(nc.psum_tensor(f"c{tag}_p{i}", [128, D], BF16)) for i in range(2)]
        block = st.enter_context(nc.Block(f"comb{tag}"))
        shb, swb, peb, acb, veb, ssb = (ctx.sems[k].v for k in ("SH", "SW", "PE", "AC", "VE", "SS"))
        sh = lambda tt: shb + 16 * (tt + 1)
        sw = lambda tt: swb + 16 * (tt + 1)
        ac = lambda tt, j: acb + tt * 2 + j + 1
        pe = lambda tt: peb + tt + 1
        nve = 6 if ln_out else (2 if r_name else 1)
        ve = lambda tt, j: veb + tt * nve + j + 1
        ss_per = (1 if xout_name else 0) + (1 if ln_out else 0)
        ss_x = lambda tt: ssb + 16 * (tt * ss_per + 1)
        ss_xn = lambda tt: ssb + 16 * (tt * ss_per + ss_per)

        def emit_stores(s, tt):
            if xout_name:
                s.wait_ge(ctx.sems["VE"].h, ve(tt, 0))
                ctx.inc(s.dma_start(d[xout_name][tt * 128:(tt + 1) * 128, :], xa[tt % 2][:, :]), "SS", 16)
            if ln_out:
                s.wait_ge(ctx.sems["VE"].h, ve(tt, 5))
                o3 = d[ln_out][:, tt * 128:(tt + 1) * 128].rearrange("(c p) t -> p c t", p=128)
                i3 = stg[tt % 2][:, :].rearrange("p (c t) -> p c t", t=128)
                ctx.inc(s.dma_start(o3, i3), "SS", 16)

        @block.sync
        def _(s):
            ctx.waits(s, entry)
            for tt in range(32):
                if tt >= 2:
                    s.wait_ge(ctx.sems["VE"].h, ve(tt - 2, nve - 1))
                    if xout_name:
                        s.wait_ge(ctx.sems["SS"].h, ss_x(tt - 2))
                ctx.inc(s.dma_start(xa[tt % 2][:, :], d[base_name][tt * 128:(tt + 1) * 128, :]), "SH", 16)
                if tt >= 1:
                    emit_stores(s, tt - 1)
            emit_stores(s, 31)

        if r_name:
            @block.gpsimd
            def _(g):
                ctx.waits(g, entry)
                for tt in range(32):
                    if tt >= 2:
                        g.wait_ge(ctx.sems["VE"].h, ve(tt - 2, 0))
                    ctx.inc(g.dma_start(ra[tt % 2][:, :], d[r_name][tt * 128:(tt + 1) * 128, :]), "SW", 16)

        @block.vector
        def _(v):
            ctx.waits(v, entry)
            for tt in range(32):
                v.wait_ge(ctx.sems["SH"].h, sh(tt))
                if r_name:
                    v.wait_ge(ctx.sems["SW"].h, sw(tt))
                    ctx.inc(v.tensor_add(xa[tt % 2][:, :], xa[tt % 2][:, :], ra[tt % 2][:, :]), "VE")
                else:
                    ctx.inc(v.nop(), "VE")
                if not ln_out:
                    if nve == 2:
                        ctx.inc(v.nop(), "VE")
                    continue
                ctx.inc(v.reduce_sum(sv[:, 0:1], xa[tt % 2][:, :], axis=AX.X), "VE")
                v.wait_ge(ctx.sems["AC"].h, ac(tt, 0))
                ctx.inc(v.reduce_sum(sv[:, 1:2], xsq[:, :], axis=AX.X), "VE")
                v.wait_ge(ctx.sems["VE"].h, ve(tt, 2))
                i1 = v.tensor_scalar_mul(sv[:, 2:3], sv[:, 0:1], C["ninvD"][:, 0:1])
                i2 = v.tensor_scalar_mul(sv[:, 4:5], sv[:, 1:2], C["invD"][:, 0:1])
                ctx.guard(v, i1, i2)
                ctx.guard(v, v.tensor_mul(sv[:, 3:4], sv[:, 2:3], sv[:, 2:3]))
                ctx.inc(v.tensor_sub(sv[:, 4:5], sv[:, 4:5], sv[:, 3:4]), "VE")
                v.wait_ge(ctx.sems["AC"].h, ac(tt, 1))
                ctx.inc(v.tensor_scalar(xnb[tt % 2][:, :], xa[tt % 2][:, :], sv[:, 2:3], sv[:, 6:7],
                                        ALU.add, ALU.mult), "VE")
                v.wait_ge(ctx.sems["PE"].h, pe(tt))
                if tt >= 2:
                    v.wait_ge(ctx.sems["SS"].h, ss_xn(tt - 2))
                ctx.inc(v.tensor_copy(stg[tt % 2][:, :], pst[tt % 2][:, :]), "VE")

        @block.scalar
        def _(sc):
            ctx.waits(sc, entry)
            if not ln_out:
                return
            for tt in range(32):
                v0 = ve(tt, 0)
                sc.wait_ge(ctx.sems["VE"].h, v0) if r_name else sc.wait_ge(ctx.sems["SH"].h, sh(tt))
                ctx.inc(sc.activation(xsq[:, :], xa[tt % 2][:, :], AF.Square), "AC")
                sc.wait_ge(ctx.sems["VE"].h, ve(tt, 3))
                sc.activation(sv[:, 5:6], sv[:, 4:5], AF.Ln, bias=C["eps5"][:, 0:1])
                ctx.inc(sc.activation(sv[:, 6:7], sv[:, 5:6], AF.Exp, scale=-0.5), "AC")

        if ln_out:
            @block.tensor
            def _(t):
                ctx.waits(t, entry)
                for tt in range(32):
                    t.wait_ge(ctx.sems["VE"].h, ve(tt, 4))
                    if tt >= 2:
                        t.wait_ge(ctx.sems["VE"].h, ve(tt - 2, 5))
                    for c in range(16):
                        inst = t.transpose(pst[tt % 2][:, c * 128:(c + 1) * 128],
                                           xnb[tt % 2][:, c * 128:(c + 1) * 128], C["identb"][:, :])
                    ctx.inc(inst, "PE")

    return ctx.snap()


def emit_attn(ctx, entry):
    nc, d, C = ctx.nc, ctx.dram, ctx.consts
    with ExitStack() as st:
        qh = [st.enter_context(nc.sbuf_tensor(f"at_q{i}", [128, T], BF16)) for i in range(2)]
        kh = [st.enter_context(nc.sbuf_tensor(f"at_k{i}", [128, T], BF16)) for i in range(2)]
        vt = [st.enter_context(nc.sbuf_tensor(f"at_v{i}", [128, T], BF16)) for i in range(2)]
        pbuf = [st.enter_context(nc.sbuf_tensor(f"at_p{i}", [128, 32 * 512], BF16)) for i in range(2)]
        ostage = [st.enter_context(nc.sbuf_tensor(f"at_o{i}", [128, T], BF16)) for i in range(2)]
        oraw = [st.enter_context(nc.sbuf_tensor(f"at_or{i}", [128, 512], F32)) for i in range(2)]
        sqb = [st.enter_context(nc.sbuf_tensor(f"at_sq{i}", [128, 512], BF16)) for i in range(2)]
        qn = st.enter_context(nc.sbuf_tensor("at_qn", [1, T], F32))
        kn = st.enter_context(nc.sbuf_tensor("at_kn", [1, T], F32))
        cm = st.enter_context(nc.sbuf_tensor("at_cm", [1, 8], F32))
        rs = st.enter_context(nc.sbuf_tensor("at_rs", [1, 512], F32))
        rep_sb = st.enter_context(nc.sbuf_tensor("at_rep", [128, 512], F32))
        negc = st.enter_context(nc.sbuf_tensor("at_negc", [128, 2], F32))
        ps_s = [st.enter_context(nc.psum_tensor(f"at_pss{i}", [128, 1024], F32)) for i in range(2)]
        ps_o = st.enter_context(nc.psum_tensor("at_pso", [128, 512], F32))
        ps_sum = [st.enter_context(nc.psum_tensor(f"at_psum{i}", [1, 512], F32)) for i in range(2)]
        ps_rep = st.enter_context(nc.psum_tensor("at_prep", [128, 512], F32))
        block = st.enter_context(nc.Block("attn"))
        shb, peb, acb, veb, ssb = (ctx.sems[k].v for k in ("SH", "PE", "AC", "VE", "SS"))
        NAC, NVE = 146, 44
        ac_base = lambda h: acb + h * NAC
        ac_sq = lambda h, i: ac_base(h) + i + 1
        ac_ln = lambda h: ac_base(h) + 17
        ac_expc = lambda h: ac_base(h) + 18
        ac_exp = lambda h, qg, j: ac_base(h) + 18 + qg * 16 + j + 1
        ve_base = lambda h: veb + h * NVE
        ve_evc = lambda h, i: ve_base(h) + i + 1
        ve_max = lambda h: ve_base(h) + 17
        ve_sum = lambda h: ve_base(h) + 18
        ve_ncp = lambda h: ve_base(h) + 19
        ve_ncc = lambda h: ve_base(h) + 20
        ve_or = lambda h, qg: ve_base(h) + 20 + qg * 3 + 1
        ve_rcp = lambda h, qg: ve_base(h) + 20 + qg * 3 + 2
        ve_ev = lambda h, qg: ve_base(h) + 20 + qg * 3 + 3
        sh = lambda h: shb + 48 * (h + 1)
        ss = lambda h: ssb + 16 * (h + 1)
        pe_c, pe_st, pe_pv, pe_rep = {}, {}, {}, {}

        @block.tensor
        def _(t):
            ctx.waits(t, entry)
            for h in range(NH):
                for i in range(16):
                    t.wait_ge(ctx.sems["AC"].h, ac_sq(h, i))
                    if i >= 4:
                        t.wait_ge(ctx.sems["VE"].h, ve_evc(h, i - 4))
                    elif h >= 1:
                        t.wait_ge(ctx.sems["AC"].h, ac_base(h - 1) + NAC)
                    pe_c[(h, i)] = ctx.inc(t.matmul(
                        ps_s[(i // 2) % 2][0:1, (i % 2) * 512:(i % 2) * 512 + 512],
                        C["onescb"][:, :], sqb[i % 2][:, :], start=True, stop=True), "PE")
                t.wait_ge(ctx.sems["VE"].h, ve_ncp(h))
                if h >= 1:
                    t.wait_ge(ctx.sems["VE"].h, ve_base(h - 1) + NVE)
                pe_c[(h, 16)] = ctx.inc(t.matmul(ps_o[:, 0:1], C["ones1f"][:, :], cm[0:1, 6:7],
                                                 start=True, stop=True), "PE")

                def emit_st1(qg, kc):
                    idx = qg * 32 + kc
                    if qg == 0 and kc == 0:
                        t.wait_ge(ctx.sems["VE"].h, ve_base(h) + 16)
                    if idx >= 4 and idx % 2 == 0:
                        pqg, pj = divmod((idx - 4) // 2, 16)
                        t.wait_ge(ctx.sems["AC"].h, ac_exp(h, pqg, pj))
                    pe_st[(h, qg, kc)] = ctx.inc(
                        t.matmul(ps_s[(idx // 2) % 2][:, (idx % 2) * 512:(idx % 2) * 512 + 512],
                                 kh[h % 2][:, kc * 128:(kc + 1) * 128],
                                 qh[h % 2][:, qg * 512:(qg + 1) * 512],
                                 start=True, stop=True), "PE")

                def emit_rep(qg):
                    t.wait_ge(ctx.sems["VE"].h, ve_rcp(h, qg))
                    pe_rep[(h, qg)] = ctx.inc(t.matmul(ps_rep[:, :], C["ones1f"][:, :], rs[0:1, :],
                                                       start=True, stop=True), "PE")

                def emit_block(qg):
                    pv = qg - 1
                    if qg >= 2:
                        emit_rep(qg - 2)
                    t.wait_ge(ctx.sems["AC"].h, ac_exp(h, pv, 15))
                    if pv >= 2:
                        t.wait_ge(ctx.sems["VE"].h, ve_or(h, pv - 2))
                    elif h >= 1 and pv < 2:
                        t.wait_ge(ctx.sems["VE"].h, ve_or(h - 1, 6 + pv))
                    for kc in range(32):
                        t.matmul(ps_o[:, :], vt[h % 2][:, kc * 128:(kc + 1) * 128],
                                 pbuf[pv % 2][:, kc * 512:(kc + 1) * 512],
                                 start=(kc == 0), stop=(kc == 31))
                        inst = t.matmul(ps_sum[pv % 2][0:1, :], C["onescb"][:, :],
                                        pbuf[pv % 2][:, kc * 512:(kc + 1) * 512],
                                        start=(kc == 0), stop=(kc == 31))
                        if qg < 8:
                            emit_st1(qg, kc)
                    pe_pv[(h, pv)] = ctx.inc(inst, "PE")

                emit_st1(0, 0)
                for kc in range(1, 32):
                    emit_st1(0, kc)
                for qg in range(1, 9):
                    emit_block(qg)
                emit_rep(7)

        @block.sync
        def _(s):
            ctx.waits(s, entry)
            for h in range(NH):
                if h >= 2:
                    s.wait_ge(ctx.sems["PE"].h, pe_c[(h - 1, 0)] - 1)
                ctx.inc(s.dma_start(qh[h % 2][:, :], d["qk"][h * 128:(h + 1) * 128, :]), "SH", 16)
                ctx.inc(s.dma_start(kh[h % 2][:, :], d["qk"][D + h * 128:D + (h + 1) * 128, :]), "SH", 16)
                v3 = d["vv"][:, h * 128:(h + 1) * 128].rearrange("(kc p) e -> p kc e", p=128)
                o3 = vt[h % 2][:, :].rearrange("p (kc e) -> p kc e", e=128)
                ctx.inc(s.dma_start(o3, v3), "SH", 16)
                if h >= 1:
                    s.wait_ge(ctx.sems["VE"].h, ve_ev(h - 1, 7))
                    ctx.inc(s.dma_start(d["oo"][(h - 1) * 128:h * 128, :], ostage[(h - 1) % 2][:, :]), "SS", 16)
            s.wait_ge(ctx.sems["VE"].h, ve_ev(NH - 1, 7))
            ctx.inc(s.dma_start(d["oo"][(NH - 1) * 128:NH * 128, :], ostage[(NH - 1) % 2][:, :]), "SS", 16)

        @block.scalar
        def _(sc):
            ctx.waits(sc, entry)
            for h in range(NH):
                sc.wait_ge(ctx.sems["SH"].h, sh(h))
                for i in range(16):
                    src2 = qh[h % 2] if i < 8 else kh[h % 2]
                    g = i % 8
                    if i >= 2:
                        sc.wait_ge(ctx.sems["PE"].h, pe_c[(h, i - 2)])
                    elif h >= 1:
                        sc.wait_ge(ctx.sems["PE"].h, pe_c[(h - 1, 14 + i)])
                    ctx.inc(sc.activation(sqb[i % 2][:, :], src2[:, g * 512:(g + 1) * 512], AF.Square), "AC")
                sc.wait_ge(ctx.sems["VE"].h, ve_max(h))
                sc.activation(cm[0:1, 2:3], cm[0:1, 0:1], AF.Ln)
                ctx.inc(sc.activation(cm[0:1, 3:4], cm[0:1, 1:2], AF.Ln), "AC")
                sc.wait_ge(ctx.sems["VE"].h, ve_sum(h))
                ctx.inc(sc.activation(cm[0:1, 5:6], cm[0:1, 4:5], AF.Exp, scale=0.5), "AC")
                sc.wait_ge(ctx.sems["VE"].h, ve_ncc(h))
                for qg in range(8):
                    if qg >= 2:
                        sc.wait_ge(ctx.sems["PE"].h, pe_pv[(h, qg - 2)])
                    elif h >= 1:
                        sc.wait_ge(ctx.sems["PE"].h, pe_pv[(h - 1, 6 + qg)])
                    for j in range(16):
                        idx = qg * 32 + 2 * j
                        sc.wait_ge(ctx.sems["PE"].h, pe_st[(h, qg, 2 * j + 1)])
                        ctx.inc(sc.activation(pbuf[qg % 2][:, j * 1024:(j + 1) * 1024],
                                              ps_s[(idx // 2) % 2][:, :], AF.Exp,
                                              bias=negc[:, h % 2:h % 2 + 1], scale=RS128), "AC")

        @block.vector
        def _(v):
            ctx.waits(v, entry)
            for h in range(NH):
                for i in range(16):
                    dst = qn if i < 8 else kn
                    g = i % 8
                    v.wait_ge(ctx.sems["PE"].h, pe_c[(h, i)])
                    ctx.inc(v.tensor_copy(dst[0:1, g * 512:(g + 1) * 512],
                                          ps_s[(i // 2) % 2][0:1, (i % 2) * 512:(i % 2) * 512 + 512]), "VE")
                ctx.guard(v, v.reduce_max(cm[0:1, 0:1], qn[0:1, :], axis=AX.X))
                ctx.guard(v, v.reduce_max(cm[0:1, 1:2], kn[0:1, :], axis=AX.X))
                ctx.inc(v.nop(), "VE")
                v.wait_ge(ctx.sems["AC"].h, ac_ln(h))
                ctx.guard(v, v.tensor_add(cm[0:1, 4:5], cm[0:1, 2:3], cm[0:1, 3:4]))
                ctx.inc(v.nop(), "VE")
                v.wait_ge(ctx.sems["AC"].h, ac_expc(h))
                ctx.guard(v, v.tensor_scalar_mul(cm[0:1, 6:7], cm[0:1, 5:6], C["nRS"][0:1, 0:1]))
                ctx.inc(v.nop(), "VE")
                v.wait_ge(ctx.sems["PE"].h, pe_c[(h, 16)])
                ctx.guard(v, v.tensor_copy(negc[:, h % 2:h % 2 + 1], ps_o[:, 0:1]))
                ctx.inc(v.nop(), "VE")
                for qg in range(8):
                    v.wait_ge(ctx.sems["PE"].h, pe_pv[(h, qg)])
                    ctx.inc(v.tensor_copy(oraw[qg % 2][:, :], ps_o[:, :]), "VE")
                    ctx.guard(v, v.reciprocal(rs[0:1, :], ps_sum[qg % 2][0:1, :]))
                    ctx.inc(v.nop(), "VE")
                    v.wait_ge(ctx.sems["PE"].h, pe_rep[(h, qg)])
                    ctx.guard(v, v.tensor_copy(rep_sb[:, :], ps_rep[:, :]))
                    if qg == 0 and h >= 2:
                        v.wait_ge(ctx.sems["SS"].h, ss(h - 2))
                    ctx.inc(v.tensor_mul(ostage[h % 2][:, qg * 512:(qg + 1) * 512],
                                         oraw[qg % 2][:, :], rep_sb[:, :]), "VE")

    assert ctx.sems["AC"].v == acb + NH * NAC
    assert ctx.sems["VE"].v == veb + NH * NVE
    return ctx.snap()


def emit_w2x(ctx, entry, hooks=None):
    nc, d, C = ctx.nc, ctx.dram, ctx.consts
    with ExitStack() as st:
        chunk = [st.enter_context(nc.sbuf_tensor(f"wx_c{i}", [128, D], F32)) for i in range(2)]
        stg = [st.enter_context(nc.sbuf_tensor(f"wx_s{i}", [128, D], BF16)) for i in range(2)]
        ps = [st.enter_context(nc.psum_tensor(f"wx_p{i}", [128, 512], F32)) for i in range(2)]
        t2in = [st.enter_context(nc.sbuf_tensor(f"wx_i{i}", [64, 16 * 1024], BF16)) for i in range(2)]
        t2out = [st.enter_context(nc.sbuf_tensor(f"wx_o{i}", [64, 16 * 1024], BF16)) for i in range(2)]
        ps2 = [st.enter_context(nc.psum_tensor(f"wx_q{i}", [64, 512], F32)) for i in range(2)]
        block = st.enter_context(nc.Block("w2x"))
        shb, swb, peb, acb, veb, ssb = (ctx.sems[k].v for k in ("SH", "SW", "PE", "AC", "VE", "SS"))
        sh1 = lambda a: shb + 16 * (a + 1)
        pe1 = lambda a, j: peb + a * 4 + j + 1
        ac1 = lambda a, j: acb + a * 4 + j + 1
        ss1 = lambda a: ssb + 16 * (a + 1)
        NM2 = 32
        NIT2 = 16
        pe2 = lambda bb, j: peb + 256 + bb * NM2 + j + 1
        ac2 = lambda bb, j: acb + 256 + bb * NM2 + j + 1
        sh2 = lambda bb: shb + 16 * (64 + bb + 1)
        ss2 = lambda bb: ssb + 16 * (64 + bb + 1)
        ss1_all = ssb + 16 * 64

        @block.tensor
        def _(t):
            ctx.waits(t, entry)
            for a in range(64):
                t.wait_ge(ctx.sems["SH"].h, sh1(a))
                for j in range(4):
                    idx = a * 4 + j
                    if idx >= 2:
                        t.wait_ge(ctx.sems["AC"].h, ac1(*divmod(idx - 2, 4)))
                    ctx.inc(t.matmul(ps[idx % 2][:, :], C["h128f"][:, :],
                                     chunk[a % 2][:, j * 512:(j + 1) * 512], start=True, stop=True), "PE")
            for bb in range(NIT2):
                t.wait_ge(ctx.sems["SH"].h, sh2(bb))
                for j in range(NM2):
                    idx = bb * NM2 + j
                    if idx >= 2:
                        t.wait_ge(ctx.sems["AC"].h, ac2(*divmod(idx - 2, NM2)))
                    ctx.inc(t.matmul(ps2[idx % 2][:, :], C["h64b"][:, :],
                                     t2in[bb % 2][:, j * 512:(j + 1) * 512], start=True, stop=True), "PE")

        if hooks:
            if "gpsimd" in hooks:
                @block.gpsimd
                def _(g):
                    ctx.waits(g, entry)
                    hooks["gpsimd"](g)
            if "vector" in hooks:
                @block.vector
                def _(v):
                    ctx.waits(v, entry)
                    hooks["vector"](v)

        @block.scalar
        def _(sc):
            ctx.waits(sc, entry)
            SQ2 = float(np.sqrt(2.0))
            for a in range(64):
                for j in range(4):
                    idx = a * 4 + j
                    sc.wait_ge(ctx.sems["PE"].h, pe1(a, j))
                    if j == 0 and a >= 2:
                        sc.wait_ge(ctx.sems["SS"].h, ss1(a - 2))
                    ctx.inc(sc.activation(stg[a % 2][:, j * 512:(j + 1) * 512], ps[idx % 2][:, :],
                                          AF.Identity, scale=SQ2), "AC")
            for bb in range(NIT2):
                for j in range(NM2):
                    idx = bb * NM2 + j
                    sc.wait_ge(ctx.sems["PE"].h, pe2(bb, j))
                    if j == 0 and bb >= 2:
                        sc.wait_ge(ctx.sems["SS"].h, ss2(bb - 2))
                    ctx.inc(sc.activation(t2out[bb % 2][:, j * 512:(j + 1) * 512], ps2[idx % 2][:, :],
                                          AF.Identity), "AC")

        @block.sync
        def _(s):
            ctx.waits(s, entry)
            for a in range(64):
                if a >= 2:
                    s.wait_ge(ctx.sems["PE"].h, pe1(a - 2, 3))
                ctx.inc(s.dma_start(chunk[a % 2][:, :], d["w2T"][a * 128:(a + 1) * 128, :]), "SH", 16)
                if a >= 1:
                    s.wait_ge(ctx.sems["AC"].h, ac1(a - 1, 3))
                    ctx.inc(s.dma_start(d["w2x1"][(a - 1) * 128:a * 128, :], stg[(a - 1) % 2][:, :]), "SS", 16)
            s.wait_ge(ctx.sems["AC"].h, ac1(63, 3))
            ctx.inc(s.dma_start(d["w2x1"][63 * 128:64 * 128, :], stg[63 % 2][:, :]), "SS", 16)
            w1v = d["w2x1"][:, :].rearrange("(a b) dd -> a b dd", a=64)
            wov = d["w2x"][:, :].rearrange("(a b) dd -> a b dd", a=64)

            def reg2(bb):
                blk, dh = divmod(bb, 2)
                return slice(blk * 16, (blk + 1) * 16), slice(dh * 1024, (dh + 1) * 1024)

            def store2(bb):
                s.wait_ge(ctx.sems["AC"].h, ac2(bb, NM2 - 1))
                bs, ds = reg2(bb)
                o3 = t2out[bb % 2][:, :].rearrange("p (b dd) -> p b dd", dd=1024)
                ctx.inc(s.dma_start(wov[:, bs, ds], o3), "SS", 16)

            for bb in range(NIT2):
                if bb == 0:
                    s.wait_ge(ctx.sems["SS"].h, ss1_all)
                if bb >= 2:
                    s.wait_ge(ctx.sems["PE"].h, pe2(bb - 2, NM2 - 1))
                bs, ds = reg2(bb)
                i3 = t2in[bb % 2][:, :].rearrange("p (b dd) -> p b dd", dd=1024)
                ctx.inc(s.dma_start(i3, w1v[:, bs, ds]), "SH", 16)
                if bb >= 1:
                    store2(bb - 1)
            store2(NIT2 - 1)

    return ctx.snap()


def emit_ffn13(ctx, entry):
    nc, d, C = ctx.nc, ctx.dram, ctx.consts
    NT = T // 512
    with ExitStack() as st:
        act = st.enter_context(nc.sbuf_tensor("f13_act", [128, NKC * T], BF16))
        wg = [st.enter_context(nc.sbuf_tensor(f"f13_wg{i}", [128, NKC * 128], BF16)) for i in range(2)]
        wl = [st.enter_context(nc.sbuf_tensor(f"f13_wl{i}", [128, NKC * 128], BF16)) for i in range(2)]
        gs = [st.enter_context(nc.sbuf_tensor(f"f13_g{i}", [128, 512], F32)) for i in range(2)]
        stg = [st.enter_context(nc.sbuf_tensor(f"f13_s{i}", [128, T], BF16)) for i in range(2)]
        pg = [st.enter_context(nc.psum_tensor(f"f13_pg{i}", [128, 512], F32)) for i in range(4)]
        pl = [st.enter_context(nc.psum_tensor(f"f13_pl{i}", [128, 512], F32)) for i in range(4)]
        block = st.enter_context(nc.Block("ffn13"))
        shb, swb, peb, acb, veb, ssb = (ctx.sems[k].v for k in ("SH", "SW", "PE", "AC", "VE", "SS"))
        sh_act = shb + 64
        NP = NT // 2
        sw = lambda hc: swb + 32 * (hc + 1)
        pe = lambda pidx, k: peb + pidx * 2 + k + 1
        ac = lambda pidx, j: acb + pidx * 2 + j + 1
        ve = lambda pidx, j: veb + pidx * 2 + j + 1
        ss = lambda hc: ssb + 16 * (hc + 1)

        @block.sync
        def _(s):
            ctx.waits(s, entry)
            a3 = d["xn"][:, :].rearrange("(kc p) t -> p kc t", p=128)
            o3 = act[:, :].rearrange("p (kc t) -> p kc t", t=T)
            for gi in range(4):
                ctx.inc(s.dma_start(o3[:, gi * 4:gi * 4 + 4, :], a3[:, gi * 4:gi * 4 + 4, :]), "SH", 16)
            for hc in range(64):
                s.wait_ge(ctx.sems["VE"].h, ve(hc * NP + NP - 1, 1))
                ctx.inc(s.dma_start(d["hs"][hc * 128:(hc + 1) * 128, :], stg[hc % 2][:, :]), "SS", 16)

        @block.gpsimd
        def _(g):
            ctx.waits(g, entry)
            for hc in range(64):
                if hc >= 2:
                    g.wait_ge(ctx.sems["PE"].h, pe((hc - 2) * NP + NP - 1, 1))
                for wbuf, coff in ((wg, hc * 128), (wl, HID0 + hc * 128)):
                    win = d["w13T"][:, coff:coff + 128].rearrange("(kc p) m -> p kc m", p=128)
                    wout = wbuf[hc % 2][:, :].rearrange("p (kc m) -> p kc m", m=128)
                    ctx.inc(g.dma_start(wout, win), "SW", 16)

        @block.tensor
        def _(t):
            ctx.waits(t, entry)
            t.wait_ge(ctx.sems["SH"].h, sh_act)
            for hc in range(64):
                t.wait_ge(ctx.sems["SW"].h, sw(hc))
                for tp in range(NP):
                    pidx = hc * NP + tp
                    if pidx >= 2:
                        t.wait_ge(ctx.sems["AC"].h, ac(pidx - 2, 1))
                        t.wait_ge(ctx.sems["VE"].h, ve(pidx - 2, 1))
                    bs = (pidx % 2) * 2
                    o0 = (tp * 2) * 512
                    for kc in range(NKC):
                        co = kc * T + o0
                        inst = t.matmul(pg[bs][:, :], wg[hc % 2][:, kc * 128:(kc + 1) * 128],
                                        act[:, co: co + 512], start=(kc == 0), stop=(kc == NKC - 1))
                        inst = t.matmul(pg[bs + 1][:, :], wg[hc % 2][:, kc * 128:(kc + 1) * 128],
                                        act[:, co + 512: co + 1024], start=(kc == 0), stop=(kc == NKC - 1))
                    ctx.inc(inst, "PE")
                    for kc in range(NKC):
                        co = kc * T + o0
                        inst = t.matmul(pl[bs][:, :], wl[hc % 2][:, kc * 128:(kc + 1) * 128],
                                        act[:, co: co + 512], start=(kc == 0), stop=(kc == NKC - 1))
                        inst = t.matmul(pl[bs + 1][:, :], wl[hc % 2][:, kc * 128:(kc + 1) * 128],
                                        act[:, co + 512: co + 1024], start=(kc == 0), stop=(kc == NKC - 1))
                    ctx.inc(inst, "PE")

        @block.scalar
        def _(sc):
            ctx.waits(sc, entry)
            for hc in range(64):
                for tp in range(NP):
                    pidx = hc * NP + tp
                    bs = (pidx % 2) * 2
                    sc.wait_ge(ctx.sems["PE"].h, pe(pidx, 0))
                    for j in range(2):
                        lin = pidx * 2 + j
                        if lin >= 2:
                            sc.wait_ge(ctx.sems["VE"].h, veb + lin - 1)
                        ctx.inc(sc.activation(gs[lin % 2][:, :], pg[bs + j][:, :], AF.Silu), "AC")

        @block.vector
        def _(v):
            ctx.waits(v, entry)
            for hc in range(64):
                for tp in range(NP):
                    pidx = hc * NP + tp
                    bs = (pidx % 2) * 2
                    v.wait_ge(ctx.sems["PE"].h, pe(pidx, 1))
                    for j in range(2):
                        tt = tp * 2 + j
                        lin = pidx * 2 + j
                        v.wait_ge(ctx.sems["AC"].h, ac(pidx, j))
                        if tt == 0 and j == 0 and hc >= 2:
                            v.wait_ge(ctx.sems["SS"].h, ss(hc - 2))
                        ctx.inc(v.tensor_mul(stg[hc % 2][:, tt * 512:(tt + 1) * 512],
                                             pl[bs + j][:, :], gs[lin % 2][:, :]), "VE")

    return ctx.snap()


def emit_ffn2(ctx, entry, wname="w2x", aname="hs", oname="g0", K=64, tag="f2"):
    nc, d, C = ctx.nc, ctx.dram, ctx.consts
    NKH = K
    NLG = (K + 7) // 8
    with ExitStack() as st:
        hbuf = [st.enter_context(nc.sbuf_tensor(f"{tag}_h{i}", [128, NKH * 512], BF16)) for i in range(2)]
        wbuf = [st.enter_context(nc.sbuf_tensor(f"{tag}_w{i}", [128, NKH * 128], BF16)) for i in range(2)]
        stg = [st.enter_context(nc.sbuf_tensor(f"{tag}_s{i}", [128, 512], BF16)) for i in range(2)]
        ps = [st.enter_context(nc.psum_tensor(f"{tag}_p{i}", [128, 512], F32)) for i in range(2)]
        block = st.enter_context(nc.Block(f"ffn2{tag}"))
        shb, swb, peb, acb, ssb = (ctx.sems[k].v for k in ("SH", "SW", "PE", "AC", "SS"))
        sh = lambda tg: shb + 16 * NLG * (tg + 1)
        sw = lambda idx: swb + 16 * (idx + 1)
        pe = lambda idx: peb + idx + 1
        ac = lambda idx: acb + idx + 1
        ss = lambda idx: ssb + 16 * (idx + 1)

        @block.sync
        def _(s):
            ctx.waits(s, entry)
            h3 = d[aname][:, :].rearrange("(kc p) t -> p kc t", p=128)

            def loads(tg):
                o3 = hbuf[tg % 2][:, :].rearrange("p (kc t) -> p kc t", t=512)
                if tg >= 2:
                    s.wait_ge(ctx.sems["PE"].h, pe((tg - 1) * 16 - 1))
                for gi in range(NLG):
                    k0, k1 = gi * 8, min(K, gi * 8 + 8)
                    ctx.inc(s.dma_start(o3[:, k0:k1, :],
                                        h3[:, k0:k1, tg * 512:(tg + 1) * 512]), "SH", 16)

            loads(0)
            for tg in range(8):
                if tg + 1 < 8:
                    loads(tg + 1)
                for m in range(16):
                    idx = tg * 16 + m
                    s.wait_ge(ctx.sems["AC"].h, ac(idx))
                    ctx.inc(s.dma_start(d[oname][m * 128:(m + 1) * 128, tg * 512:(tg + 1) * 512],
                                        stg[idx % 2][:, :]), "SS", 16)

        @block.gpsimd
        def _(g):
            ctx.waits(g, entry)
            for tg in range(8):
                for m in range(16):
                    idx = tg * 16 + m
                    if idx >= 2:
                        g.wait_ge(ctx.sems["PE"].h, pe(idx - 2))
                    win = d[wname][:, m * 128:(m + 1) * 128].rearrange("(kc p) m -> p kc m", p=128)
                    wout = wbuf[idx % 2][:, :].rearrange("p (kc m) -> p kc m", m=128)
                    ctx.inc(g.dma_start(wout, win), "SW", 16)

        @block.tensor
        def _(t):
            ctx.waits(t, entry)
            for tg in range(8):
                t.wait_ge(ctx.sems["SH"].h, sh(tg))
                for m in range(16):
                    idx = tg * 16 + m
                    t.wait_ge(ctx.sems["SW"].h, sw(idx))
                    if idx >= 2:
                        t.wait_ge(ctx.sems["AC"].h, ac(idx - 2))
                    for kc in range(NKH):
                        inst = t.matmul(ps[idx % 2][:, :], wbuf[idx % 2][:, kc * 128:(kc + 1) * 128],
                                        hbuf[tg % 2][:, kc * 512:(kc + 1) * 512],
                                        start=(kc == 0), stop=(kc == NKH - 1))
                    ctx.inc(inst, "PE")

        @block.scalar
        def _(sc):
            ctx.waits(sc, entry)
            for tg in range(8):
                for m in range(16):
                    idx = tg * 16 + m
                    sc.wait_ge(ctx.sems["PE"].h, pe(idx))
                    if idx >= 2:
                        sc.wait_ge(ctx.sems["SS"].h, ss(idx - 2))
                    ctx.inc(sc.activation(stg[idx % 2][:, :], ps[idx % 2][:, :], AF.Silu), "AC")

    return ctx.snap()


def emit_mult_t(ctx, entry, a_name, b_name, out_name, tag=""):
    nc, d, C = ctx.nc, ctx.dram, ctx.consts
    with ExitStack() as st:
        ab = [st.enter_context(nc.sbuf_tensor(f"mt{tag}_a{i}", [128, T], BF16)) for i in range(2)]
        bb = [st.enter_context(nc.sbuf_tensor(f"mt{tag}_b{i}", [128, T], BF16)) for i in range(2)]
        rr = [st.enter_context(nc.sbuf_tensor(f"mt{tag}_r{i}", [128, T], BF16)) for i in range(2)]
        stg = [st.enter_context(nc.sbuf_tensor(f"mt{tag}_s{i}", [128, T], BF16)) for i in range(2)]
        pst = [st.enter_context(nc.psum_tensor(f"mt{tag}_p{i}", [128, D], BF16)) for i in range(2)]
        block = st.enter_context(nc.Block(f"multT{tag}"))
        shb, swb, peb, veb, ssb = (ctx.sems[k].v for k in ("SH", "SW", "PE", "VE", "SS"))
        sh = lambda m: shb + 16 * (m + 1)
        sw = lambda m: swb + 16 * (m + 1)
        pe = lambda m, k: peb + m * 2 + k + 1
        ve = lambda m, j: veb + m * 3 + j + 1
        ss = lambda m: ssb + 16 * (m + 1)

        @block.sync
        def _(s):
            ctx.waits(s, entry)
            for m in range(16):
                if m >= 2:
                    s.wait_ge(ctx.sems["VE"].h, ve(m - 2, 0))
                ctx.inc(s.dma_start(ab[m % 2][:, :], d[a_name][m * 128:(m + 1) * 128, :]), "SH", 16)
                if m >= 1:
                    s.wait_ge(ctx.sems["VE"].h, ve(m - 1, 2))
                    o3 = d[out_name][:, (m - 1) * 128:m * 128].rearrange("(j p) dd -> p j dd", p=128)
                    i3 = stg[(m - 1) % 2][:, :].rearrange("p (j dd) -> p j dd", dd=128)
                    ctx.inc(s.dma_start(o3, i3), "SS", 16)
            s.wait_ge(ctx.sems["VE"].h, ve(15, 2))
            o3 = d[out_name][:, 15 * 128:16 * 128].rearrange("(j p) dd -> p j dd", p=128)
            i3 = stg[15 % 2][:, :].rearrange("p (j dd) -> p j dd", dd=128)
            ctx.inc(s.dma_start(o3, i3), "SS", 16)

        @block.gpsimd
        def _(g):
            ctx.waits(g, entry)
            for m in range(16):
                if m >= 2:
                    g.wait_ge(ctx.sems["VE"].h, ve(m - 2, 0))
                ctx.inc(g.dma_start(bb[m % 2][:, :], d[b_name][m * 128:(m + 1) * 128, :]), "SW", 16)

        @block.vector
        def _(v):
            ctx.waits(v, entry)
            for m in range(16):
                v.wait_ge(ctx.sems["SH"].h, sh(m))
                v.wait_ge(ctx.sems["SW"].h, sw(m))
                if m >= 2:
                    v.wait_ge(ctx.sems["PE"].h, pe(m - 1, 1))
                ctx.inc(v.tensor_mul(rr[m % 2][:, :], ab[m % 2][:, :], bb[m % 2][:, :]), "VE")
                for k in range(2):
                    v.wait_ge(ctx.sems["PE"].h, pe(m, k))
                    if m >= 2 and k == 0:
                        v.wait_ge(ctx.sems["SS"].h, ss(m - 2))
                    ctx.inc(v.tensor_copy(stg[m % 2][:, k * 2048:(k + 1) * 2048], pst[k][:, :]), "VE")

        @block.tensor
        def _(t):
            ctx.waits(t, entry)
            for m in range(16):
                t.wait_ge(ctx.sems["VE"].h, ve(m, 0))
                for k in range(2):
                    if m >= 1:
                        t.wait_ge(ctx.sems["VE"].h, ve(m - 1, 1 + k))
                    for j in range(16):
                        inst = t.transpose(pst[k][:, j * 128:(j + 1) * 128],
                                           rr[m % 2][:, (k * 16 + j) * 128:(k * 16 + j + 1) * 128],
                                           C["identb"][:, :])
                    ctx.inc(inst, "PE")

    return ctx.snap()


def emit_banks(ctx, entry):
    nc, d, C = ctx.nc, ctx.dram, ctx.consts
    with ExitStack() as st:
        sqh = [st.enter_context(nc.sbuf_tensor(f"bk_s{i}", [128, T], BF16)) for i in range(2)]
        membuf = [st.enter_context(nc.sbuf_tensor(f"bk_m{i}", [128, 512], BF16)) for i in range(2)]
        mnb = [st.enter_context(nc.sbuf_tensor(f"bk_n{i}", [128, 4], BF16)) for i in range(2)]
        norms = st.enter_context(nc.sbuf_tensor("bk_no", [4, T], F32))
        rn = st.enter_context(nc.sbuf_tensor("bk_rn", [4, T], F32))
        rnb = st.enter_context(nc.sbuf_tensor("bk_rnb", [4, T], BF16))
        asb = [st.enter_context(nc.sbuf_tensor(f"bk_a{i}", [128, 512], F32)) for i in range(4)]
        accb = st.enter_context(nc.sbuf_tensor("bk_ac", [128, 512], F32))
        t1 = st.enter_context(nc.sbuf_tensor("bk_t1", [128, 512], F32))
        t2 = st.enter_context(nc.sbuf_tensor("bk_t2", [128, 512], F32))
        cstage = [st.enter_context(nc.sbuf_tensor(f"bk_c{i}", [128, T], BF16)) for i in range(2)]
        ps_n = [st.enter_context(nc.psum_tensor(f"bk_pn{i}", [4, 512], F32)) for i in range(2)]
        ps_a = [st.enter_context(nc.psum_tensor(f"bk_pa{i}", [128, 512], F32)) for i in range(4)]
        ps_rep = [st.enter_context(nc.psum_tensor(f"bk_pr{i}", [128, 512], F32)) for i in range(2)]
        block = st.enter_context(nc.Block("banks"))
        shb, swb, peb, acb, veb, ssb = (ctx.sems[k].v for k in ("SH", "SW", "PE", "AC", "VE", "SS"))
        NPE, NAC, NVE = 48, 40, 17
        pe_b = lambda h: peb + h * NPE
        pe_n = lambda h, g: pe_b(h) + g + 1
        pe_ag = lambda h, g: pe_b(h) + 8 + g * 5 + 1
        pe_rp = lambda h, g, n: pe_b(h) + 8 + g * 5 + 2 + n
        ac_b = lambda h: acb + h * NAC
        ac_n = lambda h, g: ac_b(h) + g + 1
        ac_ae = lambda h, g, n: ac_b(h) + 8 + g * 4 + n + 1
        ve_b = lambda h: veb + h * NVE
        ve_rn = lambda h: ve_b(h) + 1
        ve_pair = lambda h, g: ve_b(h) + 1 + g * 2 + 1
        ve_comb = lambda h, g: ve_b(h) + 1 + g * 2 + 2
        sh = lambda h: shb + 16 * (h + 1)
        sw = lambda h: swb + 32 * (h + 1)
        ss = lambda h: ssb + 16 * (h + 1)

        @block.sync
        def _(s):
            ctx.waits(s, entry)
            for h in range(NH):
                if h >= 2:
                    s.wait_ge(ctx.sems["PE"].h, pe_b(h - 1))
                ctx.inc(s.dma_start(sqh[h % 2][:, :], d["sq"][h * 128:(h + 1) * 128, :]), "SH", 16)
                if h >= 1:
                    s.wait_ge(ctx.sems["VE"].h, ve_comb(h - 1, 7))
                    ctx.inc(s.dma_start(d["cc"][(h - 1) * 128:h * 128, :], cstage[(h - 1) % 2][:, :]), "SS", 16)
            s.wait_ge(ctx.sems["VE"].h, ve_comb(NH - 1, 7))
            ctx.inc(s.dma_start(d["cc"][(NH - 1) * 128:NH * 128, :], cstage[(NH - 1) % 2][:, :]), "SS", 16)

        @block.gpsimd
        def _(g):
            ctx.waits(g, entry)
            for h in range(NH):
                if h >= 2:
                    g.wait_ge(ctx.sems["PE"].h, pe_b(h - 1))
                m3 = d["mem2"][h * 512:(h + 1) * 512, :].rearrange("(n p) e -> p n e", p=128)
                o3 = membuf[h % 2][:, :].rearrange("p (n e) -> p n e", e=128)
                ctx.inc(g.dma_start(o3, m3), "SW", 16)
                ctx.inc(g.dma_start(mnb[h % 2][:, :], d["mnl"][h * 128:(h + 1) * 128, :]), "SW", 16)

        @block.tensor
        def _(t):
            ctx.waits(t, entry)
            for h in range(NH):
                t.wait_ge(ctx.sems["SW"].h, sw(h))
                t.wait_ge(ctx.sems["SH"].h, sh(h))
                for g in range(8):
                    if g >= 2:
                        t.wait_ge(ctx.sems["AC"].h, ac_n(h, g - 2))
                    elif h >= 1:
                        t.wait_ge(ctx.sems["AC"].h, ac_n(h - 1, 6 + g))
                    ctx.inc(t.matmul(ps_n[g % 2][:, :], mnb[h % 2][:, :],
                                     sqh[h % 2][:, g * 512:(g + 1) * 512], start=True, stop=True), "PE")
                for g in range(8):
                    if g >= 1:
                        t.wait_ge(ctx.sems["AC"].h, ac_ae(h, g - 1, 3))
                    elif h >= 1:
                        t.wait_ge(ctx.sems["AC"].h, ac_ae(h - 1, 7, 3))
                    for n in range(4):
                        inst = t.matmul(ps_a[n][:, :], membuf[h % 2][:, n * 128:(n + 1) * 128],
                                        sqh[h % 2][:, g * 512:(g + 1) * 512], start=True, stop=True)
                    ctx.inc(inst, "PE")
                    if g == 0:
                        t.wait_ge(ctx.sems["VE"].h, ve_rn(h))
                    for n in range(4):
                        if n < 2:
                            w = ve_comb(h, g - 1) if g >= 1 else (ve_comb(h - 1, 7) if h >= 1 else 0)
                        else:
                            w = ve_pair(h, g)
                        if w:
                            t.wait_ge(ctx.sems["VE"].h, w)
                        ctx.inc(t.matmul(ps_rep[n % 2][:, :], C["e4b"][:, n * 128:(n + 1) * 128],
                                         rnb[:, g * 512:(g + 1) * 512], start=True, stop=True), "PE")

        @block.scalar
        def _(sc):
            ctx.waits(sc, entry)
            for h in range(NH):
                for g in range(8):
                    sc.wait_ge(ctx.sems["PE"].h, pe_n(h, g))
                    if g == 0 and h >= 1:
                        sc.wait_ge(ctx.sems["VE"].h, ve_rn(h - 1))
                    ctx.inc(sc.activation(norms[:, g * 512:(g + 1) * 512], ps_n[g % 2][:, :],
                                          AF.Identity), "AC")
                for g in range(8):
                    sc.wait_ge(ctx.sems["PE"].h, pe_ag(h, g))
                    if g >= 1:
                        sc.wait_ge(ctx.sems["VE"].h, ve_comb(h, g - 1))
                    elif h >= 1:
                        sc.wait_ge(ctx.sems["VE"].h, ve_comb(h - 1, 7))
                    for n in range(4):
                        ctx.inc(sc.activation(asb[n][:, :], ps_a[n][:, :], AF.Identity), "AC")

        @block.vector
        def _(v):
            ctx.waits(v, entry)
            for h in range(NH):
                v.wait_ge(ctx.sems["AC"].h, ac_n(h, 7))
                ctx.guard(v, v.tensor_scalar_max(norms[:, :], norms[:, :], C["eps6"][0:4, 0:1]))
                ctx.guard(v, v.reciprocal(rn[:, :], norms[:, :]))
                ctx.guard(v, v.tensor_scalar_mul(rn[:, :], rn[:, :], C["wpT"][:, h:h + 1]))
                ctx.inc(v.tensor_copy(rnb[:, :], rn[:, :]), "VE")
                for g in range(8):
                    if g == 0 and h >= 2:
                        v.wait_ge(ctx.sems["SS"].h, ss(h - 2))
                    v.wait_ge(ctx.sems["AC"].h, ac_ae(h, g, 3))
                    for n in range(4):
                        v.wait_ge(ctx.sems["PE"].h, pe_rp(h, g, n))
                        dst = [accb, t1, t2, t1][n]
                        inst = v.tensor_mul(dst[:, :], asb[n][:, :], ps_rep[n % 2][:, :])
                        if n == 1:
                            ctx.inc(inst, "VE")
                        if n == 1:
                            v.tensor_add(accb[:, :], accb[:, :], t1[:, :])
                        elif n == 2:
                            v.tensor_add(accb[:, :], accb[:, :], t2[:, :])
                    ctx.inc(v.tensor_add(cstage[h % 2][:, g * 512:(g + 1) * 512],
                                         accb[:, :], t1[:, :]), "VE")

    assert ctx.sems["PE"].v == peb + NH * NPE
    assert ctx.sems["AC"].v == acb + NH * NAC
    assert ctx.sems["VE"].v == veb + NH * NVE
    return ctx.snap()


def _emit_build(ctx, v, dst, idx_bf, width, ls=range(16), tmp_name="_btmp"):
    C = ctx.consts
    ls = list(ls)
    l0 = ls[0]
    v.tensor_scalar(dst, idx_bf, C["lvals"][:, l0:l0 + 1], C["lut_rep"][:, l0:l0 + 1],
                    ALU.is_equal, ALU.mult)
    inst = None
    tmp = ctx.consts[tmp_name]
    for l in ls[1:]:
        v.tensor_scalar(tmp[:, 0:width], idx_bf, C["lvals"][:, l:l + 1],
                        C["lut_rep"][:, l:l + 1], ALU.is_equal, ALU.mult)
        inst = v.tensor_add(dst, dst, tmp[:, 0:width])
    return inst


def emit_ghost_gu(ctx, entry):
    nc, d, C = ctx.nc, ctx.dram, ctx.consts
    NT = T // 512
    with ExitStack() as st:
        act = st.enter_context(nc.sbuf_tensor("gg_act", [128, NKC * T], BF16))
        wgb = [st.enter_context(nc.sbuf_tensor(f"gg_wg{i}", [128, D], BF16)) for i in range(2)]
        wub = [st.enter_context(nc.sbuf_tensor(f"gg_wu{i}", [128, D], BF16)) for i in range(2)]
        gs = [st.enter_context(nc.sbuf_tensor(f"gg_g{i}", [128, 512], F32)) for i in range(2)]
        stg = [st.enter_context(nc.sbuf_tensor(f"gg_s{i}", [128, T], BF16)) for i in range(2)]
        pg = [st.enter_context(nc.psum_tensor(f"gg_pg{i}", [128, 512], F32)) for i in range(4)]
        pl = [st.enter_context(nc.psum_tensor(f"gg_pl{i}", [128, 512], F32)) for i in range(4)]
        block = st.enter_context(nc.Block("ghostgu"))
        shb, swb, peb, acb, veb, ssb = (ctx.sems[k].v for k in ("SH", "SW", "PE", "AC", "VE", "SS"))
        sh_act = shb + 64
        NP = NT // 2
        sw = lambda ic: swb + 32 * (ic + 1)
        pe = lambda pidx, k: peb + pidx * 2 + k + 1
        ac = lambda pidx, j: acb + pidx * 2 + j + 1
        ve_m = lambda pidx, j: veb + pidx * 2 + j + 1
        ss = lambda ic: ssb + 16 * (ic + 1)

        @block.sync
        def _(s):
            ctx.waits(s, entry)
            a3 = d["xn1"][:, :].rearrange("(kc p) t -> p kc t", p=128)
            o3 = act[:, :].rearrange("p (kc t) -> p kc t", t=T)
            for gi in range(4):
                ctx.inc(s.dma_start(o3[:, gi * 4:gi * 4 + 4, :], a3[:, gi * 4:gi * 4 + 4, :]), "SH", 16)
            for ic in range(NIC):
                s.wait_ge(ctx.sems["VE"].h, ve_m(ic * NP + NP - 1, 1))
                ctx.inc(s.dma_start(d["h1"][ic * 128:(ic + 1) * 128, :], stg[ic % 2][:, :]), "SS", 16)

        @block.gpsimd
        def _(g):
            ctx.waits(g, entry)
            g.wait_ge(ctx.consts["ccdone"], 1)
            for ic in range(NIC):
                if ic >= 2:
                    g.wait_ge(ctx.sems["PE"].h, pe((ic - 2) * NP + NP - 1, 1))
                rg, ru = _grow(ic, 0), _grow(ic, 1)
                ctx.inc(g.dma_start(wgb[ic % 2][:, :], d["gwall"][rg:rg + 128, :]), "SW", 16)
                ctx.inc(g.dma_start(wub[ic % 2][:, :], d["gwall"][ru:ru + 128, :]), "SW", 16)

        @block.vector
        def _(v):
            ctx.waits(v, entry)
            for ic in range(NIC):
                for tp in range(NP):
                    pidx = ic * NP + tp
                    bs = (pidx % 2) * 2
                    v.wait_ge(ctx.sems["PE"].h, pe(pidx, 1))
                    for j in range(2):
                        tt = tp * 2 + j
                        lin = pidx * 2 + j
                        v.wait_ge(ctx.sems["AC"].h, ac(pidx, j))
                        if tt == 0 and j == 0 and ic >= 2:
                            v.wait_ge(ctx.sems["SS"].h, ss(ic - 2))
                        ctx.inc(v.tensor_mul(stg[ic % 2][:, tt * 512:(tt + 1) * 512],
                                             pl[bs + j][:, :], gs[lin % 2][:, :]), "VE")

        @block.tensor
        def _(t):
            ctx.waits(t, entry)
            t.wait_ge(ctx.sems["SH"].h, sh_act)
            for ic in range(NIC):
                t.wait_ge(ctx.sems["SW"].h, sw(ic))
                for tp in range(NP):
                    pidx = ic * NP + tp
                    if pidx >= 2:
                        t.wait_ge(ctx.sems["AC"].h, ac(pidx - 2, 1))
                        t.wait_ge(ctx.sems["VE"].h, ve_m(pidx - 2, 1))
                    bs = (pidx % 2) * 2
                    o0 = (tp * 2) * 512
                    for kc in range(NKC):
                        co = kc * T + o0
                        inst = t.matmul(pg[bs][:, :], wgb[ic % 2][:, kc * 128:(kc + 1) * 128],
                                        act[:, co: co + 512], start=(kc == 0), stop=(kc == NKC - 1))
                        inst = t.matmul(pg[bs + 1][:, :], wgb[ic % 2][:, kc * 128:(kc + 1) * 128],
                                        act[:, co + 512: co + 1024], start=(kc == 0), stop=(kc == NKC - 1))
                    ctx.inc(inst, "PE")
                    for kc in range(NKC):
                        co = kc * T + o0
                        inst = t.matmul(pl[bs][:, :], wub[ic % 2][:, kc * 128:(kc + 1) * 128],
                                        act[:, co: co + 512], start=(kc == 0), stop=(kc == NKC - 1))
                        inst = t.matmul(pl[bs + 1][:, :], wub[ic % 2][:, kc * 128:(kc + 1) * 128],
                                        act[:, co + 512: co + 1024], start=(kc == 0), stop=(kc == NKC - 1))
                    ctx.inc(inst, "PE")

        @block.scalar
        def _(sc):
            ctx.waits(sc, entry)
            for ic in range(NIC):
                for tp in range(NP):
                    pidx = ic * NP + tp
                    bs = (pidx % 2) * 2
                    sc.wait_ge(ctx.sems["PE"].h, pe(pidx, 0))
                    for j in range(2):
                        lin = pidx * 2 + j
                        if lin >= 2:
                            sc.wait_ge(ctx.sems["VE"].h, veb + lin - 1)
                        ctx.inc(sc.activation(gs[lin % 2][:, :], pg[bs + j][:, :], AF.Silu), "AC")

    return ctx.snap()


def make_gbuild_hooks(ctx):
    nc, d = ctx.nc, ctx.dram
    NTILE = 18
    idxt = [_sb(ctx, f"gb2_i{i}", [128, D], BF16) for i in range(2)]
    wst = [_sb(ctx, f"gb2_w{i}", [128, D], BF16) for i in range(2)]
    btmp = _sb(ctx, "gb2_bt", [128, D], BF16)
    ccdone = ctx.st.enter_context(nc.semaphore("ccdone"))
    ctx.consts["ccdone"] = ccdone
    swb, veb, ssb = (ctx.sems[k].v for k in ("SW", "VE", "SS"))
    sw = lambda k: swb + 16 * (k + 1)
    ve = lambda k: veb + k + 1
    ss = lambda k: ssb + 16 * (k + 1)
    ctx.consts["gbuild_ss"] = ss(NTILE - 1)

    def gp(g):
        for k in range(NTILE):
            if k >= 2:
                g.wait_ge(ctx.sems["VE"].h, ve(k - 2))
            ctx.inc(g.dma_start(idxt[k % 2][:, :], d["gsh"][k * 128:(k + 1) * 128, :]), "SW", 16)
            if k >= 1:
                g.wait_ge(ctx.sems["VE"].h, ve(k - 1))
                ctx.inc(g.dma_start(d["cgc"][(k - 1) * 128:k * 128, :], wst[(k - 1) % 2][:, :]), "SS", 16)
        g.wait_ge(ctx.sems["VE"].h, ve(NTILE - 1))
        ctx.inc(g.dma_start(d["cgc"][(NTILE - 1) * 128:NTILE * 128, :], wst[(NTILE - 1) % 2][:, :]), "SS", 16)

    def vec(v):
        sav = ctx.consts.get("_btmp")
        ctx.consts["_btmp"] = btmp
        for k in range(NTILE):
            v.wait_ge(ctx.sems["SW"].h, sw(k))
            if k >= 2:
                v.wait_ge(ctx.sems["SS"].h, ss(k - 2))
            ctx.inc(_emit_build(ctx, v, wst[k % 2][:, :], idxt[k % 2][:, :], D), "VE")
        if sav is None:
            del ctx.consts["_btmp"]
        else:
            ctx.consts["_btmp"] = sav

    return {"gpsimd": gp, "vector": vec}


def emit_gcoll(ctx, entry):
    nc, d = ctx.nc, ctx.dram
    blk = nc.Block("gcoll")
    block = blk.__enter__()

    @block.gpsimd
    def _(g):
        g.wait_ge(ctx.sems["SS"].h, ctx.consts["gbuild_ss"])
        g.collective_compute(
            "AllGather", mybir.AluOpType.bypass,
            replica_groups=[list(range(NCORES))],
            ins=[d["cgc"].ap().opt()], outs=[d["gwall"].ap().opt()],
        ).then_inc(ctx.consts["ccdone"], 1)

    blk.__exit__(None, None, None)
    return ctx.snap()


def emit_repack(ctx, entry):
    nc, d = ctx.nc, ctx.dram
    ccdone = ctx.consts["ccdone"]
    blk = nc.Block("repack")
    block = blk.__enter__()

    @block.gpsimd
    def _(g):
        ctx.waits(g, entry)
        g.wait_ge(ccdone, 1)
        for ic in range(NIC):
            r = _grow(ic, 2)
            ctx.inc(g.dma_start(d["wdx"][ic * 128:(ic + 1) * 128, :],
                                d["gwall"][r:r + 128, :]), "SW", 16)

    blk.__exit__(None, None, None)
    return ctx.snap()


def emit_gdbuild(ctx, entry):
    nc, d, C = ctx.nc, ctx.dram, ctx.consts
    with ExitStack() as st:
        idx_t = [st.enter_context(nc.sbuf_tensor(f"gb_i{i}", [128, D], BF16)) for i in range(2)]
        wds = [st.enter_context(nc.sbuf_tensor(f"gb_w{i}", [128, D], BF16)) for i in range(2)]
        btmp = st.enter_context(nc.sbuf_tensor("gb_bt", [128, D], BF16))
        ctx.consts["_btmp"] = btmp
        block = st.enter_context(nc.Block("gdbuild"))
        swb, veb, ssb = (ctx.sems[k].v for k in ("SW", "VE", "SS"))
        sw = lambda kc: swb + 16 * (kc + 1)
        ve_b = lambda kc: veb + kc + 1
        ss = lambda kc: ssb + 16 * (kc + 1)

        @block.gpsimd
        def _(g):
            ctx.waits(g, entry)
            for kc in range(NIC):
                if kc >= 2:
                    g.wait_ge(ctx.sems["VE"].h, ve_b(kc - 2))
                ctx.inc(g.dma_start(idx_t[kc % 2][:, :], d["didx"][kc * 128:(kc + 1) * 128, :]), "SW", 16)

        @block.vector
        def _(v):
            ctx.waits(v, entry)
            for kc in range(NIC):
                v.wait_ge(ctx.sems["SW"].h, sw(kc))
                if kc >= 2:
                    v.wait_ge(ctx.sems["SS"].h, ss(kc - 2))
                ctx.inc(_emit_build(ctx, v, wds[kc % 2][:, :], idx_t[kc % 2][:, :], D), "VE")

        @block.sync
        def _(s):
            ctx.waits(s, entry)
            for kc in range(NIC):
                s.wait_ge(ctx.sems["VE"].h, ve_b(kc))
                ctx.inc(s.dma_start(d["wdx"][kc * 128:(kc + 1) * 128, :], wds[kc % 2][:, :]), "SS", 16)

    del ctx.consts["_btmp"]
    return ctx.snap()


def _u8view(a):
    a = np.ascontiguousarray(a)
    return np.ascontiguousarray(a.view(np.uint8).reshape(*a.shape, 4)[..., 0])


def _hadamard(n):
    h = np.array([[1.0]], dtype=np.float64)
    while h.shape[0] < n:
        h = np.block([[h, h], [h, -h]])
    return h


def _grow(ic, mat):
    return ((ic % 8) * 18 + mat * 6 + ic // 8) * 128


def prep_gsh(W, core):
    gi, ui, di = W["gidx"], W["uidx"], W["didx"]
    gsh = np.zeros((18, 128, D), np.uint8)
    for s in range(6):
        ic = core + 8 * s
        if ic >= NIC:
            continue
        for mat, srcm in ((0, gi), (1, ui)):
            col = srcm[:, ic * 128:(ic + 1) * 128]
            gsh[mat * 6 + s] = col.reshape(16, 128, 128).transpose(1, 0, 2).reshape(128, D)
        gsh[12 + s] = di[ic * 128:(ic + 1) * 128, :]
    return np.ascontiguousarray(gsh.reshape(18 * 128, D))


def prep_inputs(inputs):
    W = {}
    f32 = lambda a: np.ascontiguousarray(np.asarray(a), dtype=np.float32)
    W["wqkT"] = np.ascontiguousarray(np.concatenate(
        [f32(inputs["L0_attn_Wq"]).T, f32(inputs["L0_attn_Wk"]).T], axis=1))
    bqk = np.concatenate([f32(inputs["L0_attn_bq"]), f32(inputs["L0_attn_bk"])])
    W["bqk"] = np.ascontiguousarray(bqk.reshape(32, 128).T)
    W["wvT"] = np.ascontiguousarray(f32(inputs["L0_attn_Wv"]).T)
    W["bv"] = f32(inputs["L0_attn_bv"]).reshape(1, D)
    W["woT"] = np.ascontiguousarray(f32(inputs["L0_attn_Wo"]).T)
    W["bo"] = np.ascontiguousarray(f32(inputs["L0_attn_bo"]).reshape(16, 128).T)
    W["w13T"] = np.ascontiguousarray(np.concatenate(
        [f32(inputs["L0_ffn_W1"]).T, f32(inputs["L0_ffn_W3"]).T], axis=1))
    W["w2T"] = np.ascontiguousarray(f32(inputs["L0_ffn_W2"]).T)
    W["l1qT"] = np.ascontiguousarray(f32(inputs["L1_attn_Wq"]).T)
    W["l1oT"] = np.ascontiguousarray(f32(inputs["L1_attn_Wo"]).T)
    W["mem2"] = np.ascontiguousarray(
        np.transpose(f32(inputs["L1_attn_memories"]), (1, 0, 2, 3)).reshape(NH * 4 * 128, 128))
    W["mnl"] = np.ascontiguousarray(
        np.transpose(f32(inputs["L1_attn_memory_norms"]), (1, 2, 0)).reshape(NH * 128, 4))
    W["mnrows"] = np.ascontiguousarray(f32(inputs["L1_attn_memory_norms"]).reshape(4, D))
    W["bwt"] = f32(inputs["L1_attn_bank_weights"])
    e4 = np.zeros((4, 512), np.float32)
    for n in range(4):
        e4[n, n * 128:(n + 1) * 128] = 1.0
    W["e4"] = e4
    W["e4b"] = e4.astype(BF)
    W["lutrow"] = f32(inputs["L1_ffn_lut"]).reshape(1, 16)
    W["lvrow"] = np.arange(16, dtype=np.float32).reshape(1, 16)
    W["gidx"] = _u8view(np.asarray(inputs["L1_ffn_gate_idx"]).T)
    W["uidx"] = _u8view(np.asarray(inputs["L1_ffn_up_idx"]).T)
    W["didx"] = _u8view(np.asarray(inputs["L1_ffn_down_idx"]).T)
    W["identb"] = np.eye(128, dtype=BF)
    W["ident32"] = np.eye(128, dtype=np.float32)
    W["h128b"] = (_hadamard(128) * (2.0 ** -7)).astype(BF)
    W["h128f"] = (_hadamard(128) * (2.0 ** -7)).astype(np.float32)
    W["h64b"] = _hadamard(64).astype(BF)
    return W


def build_phase_program(phases, io_names):
    nc = bass.Bass("TRN2", target_bir_lowering=False)
    st = nc.ctx
    ctx = Ctx(nc, st)
    ins, outs = io_names
    for n in ins:
        shape, dt, _ = SPECS[n]
        ctx.dram[n] = nc.dram_tensor(n, shape, dt, kind="ExternalInput")
    for n in outs:
        shape, dt, _ = SPECS[n]
        ctx.dram[n] = nc.dram_tensor(n, shape, dt, kind="ExternalOutput")
    entry = ctx.snap()
    entry = emit_init(ctx, entry)
    for ph in phases:
        entry = ph(ctx, entry)
    with nc.Block("fin") as block:
        @block.sync
        def _(s):
            ctx.waits(s, entry)
    return nc


CONST_INS = ["identb", "ident32", "h128b", "h64b", "h128f", "lutrow", "lvrow",
             "bqk", "bo", "bv", "bwt", "mnrows", "e4", "e4b"]


PHASE_LIST = [
    ("ln0", [lambda c, e: emit_combine_ln(c, e, "x", None, None, "xn", tag="l0")], ["x"], ["xn"]),
    ("qkv", [lambda c, e: emit_proj(c, e, "wqkT", "xn", "qk", NKC, 32, "bias", bias="bqk", tag="qk")],
     ["wqkT", "xn"], ["qk"]),
    ("vproj", [lambda c, e: emit_vproj(c, e)], ["wvT", "xn"], ["vv"]),
    ("attn", [lambda c, e: emit_attn(c, e)], ["qk", "vv"], ["oo"]),
    ("oproj", [lambda c, e: emit_proj(c, e, "woT", "oo", "s0", NKC, 16, "bias", bias="bo", tag="op")],
     ["woT", "oo"], ["s0"]),
    ("w2x", [lambda c, e: emit_w2x(c, e, hooks=make_gbuild_hooks(c))], ["w2T"], ["w2x1", "w2x"]),
    ("ffn13", [lambda c, e: emit_ffn13(c, e)], ["w13T", "xn"], ["hs"]),
    ("ffn2", [lambda c, e: emit_ffn2(c, e)], ["w2x", "hs"], ["g0"]),
    ("mt0", [lambda c, e: emit_mult_t(c, e, "s0", "g0", "r0tm", tag="0")], ["s0", "g0"], ["r0tm"]),
    ("cmb0", [lambda c, e: emit_combine_ln(c, e, "x", "r0tm", "x1", "xn1", tag="c0")],
     ["x", "r0tm"], ["x1", "xn1"]),
    ("q1", [lambda c, e: emit_proj(c, e, "l1qT", "xn1", "sq", NKC, 16, "elu1", tag="q1")],
     ["l1qT", "xn1"], ["sq"]),
    ("banks", [lambda c, e: emit_banks(c, e)], ["sq", "mem2", "mnl"], ["cc"]),
    ("o1", [lambda c, e: emit_proj(c, e, "l1oT", "cc", "s1", NKC, 16, "copy", tag="o1")],
     ["l1oT", "cc"], ["s1"]),
    ("ggu", [lambda c, e: emit_ghost_gu(c, e)], ["gsh", "xn1"], ["h1"]),
    ("rpk", [lambda c, e: emit_repack(c, e)], [], ["wdx"]),
    ("gdn", [lambda c, e: emit_ffn2(c, e, "wdx", "h1", "g1", NIC, tag="gd")], ["wdx", "h1"], ["g1"]),
    ("mt1", [lambda c, e: emit_mult_t(c, e, "s1", "g1", "r1tm", tag="1")], ["s1", "g1"], ["r1tm"]),
    ("cmb1", [lambda c, e: emit_combine_ln(c, e, "x1", "r1tm", "out", None, tag="c1")],
     ["x1", "r1tm"], ["out"]),
]


def build_fused_program():
    nc = bass.Bass("TRN2", target_bir_lowering=False)
    ctx = Ctx(nc, nc.ctx)
    for n, (shape, dt, kind) in SPECS.items():
        if kind == "in":
            ctx.dram[n] = nc.dram_tensor(n, shape, dt, kind="ExternalInput")
        elif kind == "out":
            ctx.dram[n] = nc.dram_tensor(n, shape, dt, kind="ExternalOutput")
        else:
            aspace = "Shared" if n == "gwall" else "Local"
            ctx.dram[n] = nc.dram_tensor(n, shape, dt, kind="Internal", addr_space=aspace)
    entry = ctx.snap()
    entry = emit_init(ctx, entry)
    for name, emitters, ins, outs in PHASE_LIST:
        for ph in emitters:
            entry2 = ph(ctx, entry)
            entry = entry2
        if name == "w2x":
            emit_gcoll(ctx, entry)
    with nc.Block("fin") as block:
        @block.sync
        def _(s):
            ctx.waits(s, entry)
    return nc


FUSED = True


def kernel(**inputs):
    W = prep_inputs(inputs)
    x = np.asarray(inputs["L0_attn_x"], dtype=np.float32)
    percore = {"x": [np.ascontiguousarray(x[i]) for i in range(NCORES)],
               "gsh": [prep_gsh(W, i) for i in range(NCORES)]}
    if FUSED:
        nc = build_fused_program()
        in_names = [n for n, (s, dt, k) in SPECS.items() if k == "in"]
        in_maps = []
        for core in range(NCORES):
            m = {n: (percore[n][core] if n in percore else W[n]) for n in in_names}
            in_maps.append(m)
        res = run_bass_kernel_spmd(nc, in_maps, core_ids=list(range(NCORES)))
        return np.stack([np.asarray(res.results[c]["out"], dtype=np.float32) for c in range(NCORES)])
    for name, emitters, ins, outs in PHASE_LIST:
        all_ins = sorted(set(ins) | set(CONST_INS))
        nc = build_phase_program(emitters, (all_ins, outs))
        in_maps = []
        for core in range(NCORES):
            m = {}
            for n in all_ins:
                m[n] = percore[n][core] if n in percore else W[n]
            in_maps.append(m)
        res = run_bass_kernel_spmd(nc, in_maps, core_ids=list(range(NCORES)))
        for n in outs:
            percore[n] = [res.results[core][n] for core in range(NCORES)]
    return np.stack([np.asarray(o, dtype=np.float32) for o in percore["out"]])
